# version 02
# baseline (speedup 1.0000x reference)
import contextlib
import os
import numpy as np
import ml_dtypes
import concourse.bass as bass
import concourse.mybir as mybir
from concourse.bass_utils import run_bass_kernel_spmd

F32 = mybir.dt.float32
BF16 = mybir.dt.bfloat16
I32 = mybir.dt.int32
AF = mybir.ActivationFunctionType
ALU = mybir.AluOpType
AX = mybir.AxisListType

SCALE_B = float(1.0 / np.sqrt(128.0))
LN05 = float(np.log(0.5))

N_SUB = 4
TAPS = ()

ENGINES = ("pe", "act", "dve", "pool", "sp")


class Op:
    __slots__ = ("eng", "fn", "deps", "dma", "idx", "cum", "sem", "needed",
                 "tag")

    def __init__(self, eng, fn, deps, dma, idx, tag, slot=None):
        self.eng = eng
        self.fn = fn
        self.deps = deps
        self.dma = dma
        self.idx = idx
        self.cum = None
        if dma:
            assert slot is not None, f"dma op {tag} needs slot="
            self.sem = f"d.{slot}"
        else:
            self.sem = eng
        self.needed = False
        self.tag = tag


class Res:
    __slots__ = ("writer", "readers", "name")

    def __init__(self, name=""):
        self.writer = None
        self.readers = []
        self.name = name


class Prog:
    def __init__(self):
        self.ops = {e: [] for e in ENGINES}
        self.n = 0

    def op(self, eng, fn, reads=(), writes=(), deps=(), dma=False, tag="",
           slot=None):
        dd = [d for d in deps if d is not None]
        for r in reads:
            if r.writer is not None:
                dd.append(r.writer)
        for w in writes:
            if w.writer is not None:
                dd.append(w.writer)
            dd.extend(w.readers)
        if dma and slot is None and writes:
            slot = writes[0].name
        o = Op(eng, fn, dd, dma, len(self.ops[eng]), tag, slot=slot)
        self.ops[eng].append(o)
        self.n += 1
        for r in reads:
            r.readers.append(o)
        for w in writes:
            w.writer = o
            w.readers = []
        return o

    def finalize(self, paranoid=False):
        self.paranoid = paranoid
        for e in ENGINES:
            for o in self.ops[e]:
                if o.dma:
                    o.needed = True
                for d in o.deps:
                    if d.eng != o.eng or d.dma or paranoid:
                        d.needed = True
        counts = {}
        for e in ENGINES:
            for o in self.ops[e]:
                if o.needed:
                    amt = 16 if o.dma else 1
                    counts[o.sem] = counts.get(o.sem, 0) + amt
                    o.cum = counts[o.sem]
        self.sem_totals = counts
        self._simulate()

    def _simulate(self):
        pos = {e: 0 for e in ENGINES}
        vals = {s: 0 for s in self.sem_totals}
        progress = True
        while progress:
            progress = False
            for e in ENGINES:
                while pos[e] < len(self.ops[e]):
                    o = self.ops[e][pos[e]]
                    ok = True
                    for d in o.deps:
                        if d.eng == e and not d.dma:
                            continue
                        if vals.get(d.sem, 0) < d.cum:
                            ok = False
                            break
                    if not ok:
                        break
                    if o.needed:
                        vals[o.sem] = vals.get(o.sem, 0) + (16 if o.dma else 1)
                    pos[e] += 1
                    progress = True
        stuck = {e: len(self.ops[e]) - pos[e] for e in ENGINES
                 if pos[e] < len(self.ops[e])}
        if stuck:
            for e in ENGINES:
                if pos[e] < len(self.ops[e]):
                    o = self.ops[e][pos[e]]
                    info = [(d.eng, d.tag, d.cum, vals.get(d.sem, 0))
                            for d in o.deps if not (d.eng == e and not d.dma)]
                    raise RuntimeError(
                        f"DEADLOCK: engine {e} stuck at op {pos[e]} "
                        f"tag={o.tag} deps={info}; remaining={stuck}")

    def emit_engine(self, e, eng, sem_handles):
        hiw = {}
        paranoid = getattr(self, "paranoid", False)
        for o in self.ops[e]:
            for d in o.deps:
                if d.eng == e and not d.dma and not paranoid:
                    continue
                if hiw.get(d.sem, 0) >= d.cum:
                    continue
                eng.wait_ge(sem_handles[d.sem], d.cum)
                hiw[d.sem] = d.cum
            inst = o.fn(eng)
            if o.needed:
                if inst is None:
                    raise RuntimeError(
                        f"op {o.tag} on {e} needs inc but returned None")
                inst.then_inc(sem_handles[o.sem], 16 if o.dma else 1)

    def sem_names(self):
        return sorted(self.sem_totals)



def _host_consts():
    arr = np.arange(1, 257, dtype=np.float32)
    iota_row = arr[None, :]
    kd_col = np.zeros((128, 2), np.float32)
    for sub in range(2):
        kd_col[:, sub] = 255.0 - (sub * 128 + np.arange(128))
    bd_cols = np.zeros((128, 4), np.float32)
    for i in range(4):
        bd_cols[:, i] = 256.0 * i
    k_ = np.arange(256)[:, None]
    q_ = np.arange(256)[None, :]
    idxt_full = np.where(q_ >= k_, (q_ - k_).astype(np.float32), 1.0e6)
    idxt = np.zeros((128, 512), np.float32)
    idxt[:, 0:256] = idxt_full[0:128]
    idxt[:, 256:512] = idxt_full[128:256]
    R = np.zeros((128, 128), np.float32)
    for d in range(64):
        R[d, 64 + d] = -1.0
        R[64 + d, d] = 1.0
    rt = np.ascontiguousarray(R.T)
    return iota_row, kd_col, bd_cols, idxt, rt


def _rope_tables(T_total=2048, D=128, base=10000.0):
    inv = 1.0 / base ** (np.arange(0, D, 2, dtype=np.float32) / D)
    t = np.arange(T_total, dtype=np.float32)
    freqs = t[:, None] * inv[None, :]
    emb = np.concatenate([freqs, freqs], 1)
    return np.cos(emb).T.astype(np.float32), np.sin(emb).T.astype(np.float32)


def _attnb_masks(parity, W=256):
    t0 = parity * 1024
    gpen = np.zeros((128, 32), np.float32)
    gpen[:] = -1.0e30
    ctri = np.zeros((2, 5, 128, 512), np.float32)
    lmask = np.zeros((2, 5, 128, 512), np.float32)
    for s in range(2):
        G0 = t0 + 512 * s
        n_main = 7 if s == 0 else 11
        for kc in range(n_main):
            if (kc + 1) * 128 <= G0:
                gpen[:, s * 16 + kc] = 0.0
        for j in range(5):
            jj = 4 * s + j
            kk = np.arange(128)
            gkey = t0 + (jj - 1) * 128 + kk
            qq = np.arange(512)
            gq = G0 + qq
            valid_k = gkey >= 0
            c = (gkey[:, None] <= gq[None, :]) & valid_k[:, None]
            ctri[s, j] = c.astype(np.float32)
            blk = gq // W
            k_start = np.maximum(0, blk * W - W // 2)
            upper = gq - (blk * W - k_start)
            l = ((gkey[:, None] >= k_start[None, :])
                 & (gkey[:, None] <= upper[None, :]) & valid_k[:, None])
            lmask[s, j] = l.astype(np.float32)
    def pack(m):
        out = np.zeros((128, 2 * 5 * 512), np.float32)
        for s in range(2):
            for j in range(5):
                out[:, (s * 5 + j) * 512:(s * 5 + j + 1) * 512] = m[s, j]
        return out.astype(ml_dtypes.bfloat16)
    return gpen, pack(ctri), pack(lmask)



class KB:
    def __init__(self, nc, n_sub=4, taps=(), private_shared=False):
        self.nc = nc
        self.P = Prog()
        self.n_sub = n_sub
        self.private_shared = private_shared
        self.taps = set(taps)
        self.t = {}
        self.s = {}
        self.ps = []
        self.r = {}
        self.tap_stores = []

    def R(self, name):
        if name not in self.r:
            self.r[name] = Res(name)
        return self.r[name]

    def sb(self, name, shape, dtype):
        ap = self.nc.alloc_sbuf_tensor("sb_" + name, shape, dtype).ap()
        self.s[name] = ap
        return ap

    def load(self, dst, src, slot, tag, eng="pool", reads=(), deps=(),
             writes_extra=()):
        return self.P.op(eng, (lambda d, s: lambda g: g.dma_start(out=d, in_=s))(dst, src),
                         reads=list(reads),
                         writes=[self.R(slot)] + list(writes_extra), deps=deps,
                         dma=True, tag=tag, slot=slot)

    def mmg(self, ps_ap, pairs, reads, wres, tag, accumulate=False):
        def fn(t):
            ret = None
            n = len(pairs)
            for i, (l, rhs) in enumerate(pairs):
                ret = t.matmul(ps_ap, l, rhs,
                               start=(i == 0 and not accumulate),
                               stop=(i == n - 1), skip_group_check=True)
            return ret
        return self.P.op("pe", fn, reads=list(reads), writes=[wres], tag=tag)

    def dve(self, fn, reads, writes, tag):
        return self.P.op("dve", fn, reads=list(reads), writes=list(writes), tag=tag)

    def act(self, fn, reads, writes, tag):
        return self.P.op("act", fn, reads=list(reads), writes=list(writes), tag=tag)

    def tap(self, name, src_ap, shape, dtype, reads):
        if name not in self.taps:
            return
        d = self.nc.dram_tensor("tap_" + name, shape, dtype,
                                kind="ExternalOutput").ap()
        o = self.P.op("sp", (lambda dd, ss: lambda g: g.dma_start(out=dd, in_=ss))(d, src_ap),
                      reads=list(reads), dma=True, tag="tap_" + name,
                      slot="tap_" + name)
        self.tap_stores.append(o)

    def declare(self):
        nc = self.nc
        T = self.t
        def din(name, shape, dtype=F32):
            T[name] = nc.dram_tensor(name, shape, dtype, kind="ExternalInput").ap()
        din("x", [2048, 1024])
        din("wqkvuT", [2048, 8192])
        din("woutT", [2048, 2048])
        din("wq", [2048, 2048])
        din("wk", [2048, 512])
        din("wv", [2048, 512])
        din("wo", [2048, 2048])
        din("w0T", [2048, 8192])
        din("w1T", [8192, 2048])
        din("b0c", [128, 64])
        din("b1c", [128, 16])
        din("slope_row", [1, 16])
        din("iota_row", [1, 256])
        din("kd_col", [128, 2])
        din("bd_cols", [128, 4])
        din("idxt", [128, 512])
        din("rt", [128, 128])
        din("cosf", [128, 1024])
        din("sinf", [128, 1024])
        din("isodd", [128, 1])
        din("pars", [1, 8], I32)
        din("cbias", [1, 4])
        din("gpen", [128, 32])
        din("ctri", [128, 5120], BF16)
        din("lmask", [128, 5120], BF16)
        T["out"] = nc.dram_tensor("out", [2048, 1024], F32,
                                  kind="ExternalOutput").ap()
        sp_ = "Local" if self.private_shared else "Shared"
        T["sha"] = nc.dram_tensor("sha", [2, 2048, 128], F32, kind="Internal",
                                  addr_space=sp_).ap()
        T["shb_k"] = nc.dram_tensor("shb_k", [2, 512, 1024], BF16,
                                    kind="Internal", addr_space=sp_).ap()
        T["shb_v"] = nc.dram_tensor("shb_v", [2, 1024, 512], BF16,
                                    kind="Internal", addr_space=sp_).ap()
        T["shf"] = nc.dram_tensor("shf", [2, 2, 8], I32, kind="Internal",
                                  addr_space=sp_).ap()

        self.sb("x", [128, 16 * 1024], F32)
        self.sb("y", [128, 16 * 1024], BF16)
        self.arena = self.sb("arena", [128, 86 * 1024], mybir.dt.uint8)
        self.sb("rows", [1, 2048], F32)
        for i in range(3):
            self.sb(f"wst{i}", [128, 2048], BF16)
        self.sb("iota_row", [1, 256], F32)
        self.sb("kd_col", [128, 2], F32)
        self.sb("bd_cols", [128, 4], F32)
        self.sb("idxt", [128, 512], F32)
        self.sb("rt", [128, 128], BF16)
        self.sb("isodd", [128, 1], F32)
        self.sb("pars", [1, 8], I32)
        self.sb("cbias", [1, 4], F32)
        self.sb("gpen", [128, 32], F32)
        self.sb("slope_row", [1, 16], F32)
        self.sb("negs_row", [1, 16], F32)
        self.sb("ones_cb", [128, 1], BF16)
        self.sb("ones_rb", [1, 128], BF16)
        self.sb("ones_rf", [1, 128], F32)
        self.sb("b0c", [128, 64], F32)
        self.sb("b1c", [128, 16], F32)
        self.sb("srow_bf", [1, 512], BF16)
        self.sb("scol", [128, 8], F32)
        for i in range(8):
            self.ps.append(nc.alloc_psum_tensor(f"ps{i}", [128, 512], F32).ap())
        self.psr = [self.R(f"ps{i}") for i in range(8)]

    def carve(self, name, off_bytes, shape, dtype):
        nbytes = {F32: 4, BF16: 2, I32: 4}[dtype]
        assert off_bytes % 4 == 0
        assert off_bytes + shape[1] * nbytes <= 86 * 1024, name
        base = self.s["arena"]
        ap = base.tensor.ap().bitcast(dtype)[0:shape[0],
                                             off_bytes // nbytes:
                                             off_bytes // nbytes + shape[1]]
        self.s[name] = ap
        return ap

    def emit_ln(self, idx):
        P, s, ps = self.P, self.s, self.ps
        x, y = s["x"], s["y"]
        rows = s["rows"]
        tmpf, r_tmp = self.carve("ln_tmp", 0, [128, 1024], F32)
        sqt, r_sq = self.carve("ln_sq", 4096, [128, 1024], BF16)
        sq_mean, r_sqm = self.carve("ln_sqm", 6144, [1, 1024], F32)
        rx = [self.R(f"x{c}") for c in range(16)]
        ry = [self.R(f"y{c}") for c in range(16)]
        r_rows = self.R("rows")
        for c in range(16):
            xc = x[:, c * 1024:(c + 1) * 1024]
            yc = y[:, c * 1024:(c + 1) * 1024]
            self.dve((lambda a, b: lambda v: v.tensor_copy(a, b))(yc, xc),
                     [rx[c]], [ry[c]], f"ln{idx}.cp{c}")
            self.act((lambda a, b: lambda sc: sc.activation(a, b, AF.Square))(sqt, yc),
                     [ry[c]], r_sq, f"ln{idx}.sq{c}")
            for h in range(2):
                ysl = yc[:, h * 512:(h + 1) * 512]
                ssl = sqt[:, h * 512:(h + 1) * 512]
                self.mmg(ps[4 + h][0:1, :], [(s["ones_cb"], ysl)],
                         [ry[c], self.R("ones")], self.psr[4 + h],
                         f"ln{idx}.s1.{c}.{h}", accumulate=(c != 0))
                self.mmg(ps[6 + h][0:1, :], [(s["ones_cb"], ssl)],
                         r_sq + [self.R("ones")], self.psr[6 + h],
                         f"ln{idx}.s2.{c}.{h}", accumulate=(c != 0))
        mean = rows[:, 0:1024]
        var = rows[:, 1024:2048]
        for h in range(2):
            self.dve((lambda a, b: lambda v: v.tensor_scalar_mul(a, b, 1.0 / 2048.0))(
                mean[:, h * 512:(h + 1) * 512], ps[4 + h][0:1, :]),
                [self.psr[4 + h]], [r_rows], f"ln{idx}.mean{h}")
            self.dve((lambda a, b: lambda v: v.tensor_scalar_mul(a, b, 1.0 / 2048.0))(
                var[:, h * 512:(h + 1) * 512], ps[6 + h][0:1, :]),
                [self.psr[6 + h]], [r_rows], f"ln{idx}.msq{h}")
        self.act((lambda a, b: lambda sc: sc.activation(a, b, AF.Square))(sq_mean, mean),
                 [r_rows], r_sqm, f"ln{idx}.m2")
        self.dve((lambda a, b, c2: lambda v: v.tensor_sub(a, b, c2))(var, var, sq_mean),
                 [r_rows] + r_sqm, [r_rows], f"ln{idx}.var")
        self.act((lambda a, b, bb_=s["cbias"][0:1, 0:1]: lambda sc: sc.activation(
            a, b, AF.Ln, bias=bb_))(var, var),
                 [r_rows, self.R("cbias")], [r_rows], f"ln{idx}.ln")
        self.act((lambda a, b: lambda sc: sc.activation(a, b, AF.Exp, scale=-0.5))(var, var),
                 [r_rows], [r_rows], f"ln{idx}.exp")
        self.dve((lambda a, b, c2: lambda v: v.tensor_mul(a, b, c2))(mean, mean, var),
                 [r_rows], [r_rows], f"ln{idx}.mr")
        for h in range(2):
            self.mmg(ps[0 + h], [(s["ones_rf"], var[:, h * 512:(h + 1) * 512])],
                     [r_rows, self.R("ones")], self.psr[0 + h], f"ln{idx}.bR{h}")
            self.mmg(ps[2 + h], [(s["ones_rf"], mean[:, h * 512:(h + 1) * 512])],
                     [r_rows, self.R("ones")], self.psr[2 + h], f"ln{idx}.bM{h}")
        for c in range(16):
            for h in range(2):
                xsl = x[:, c * 1024 + h * 512: c * 1024 + h * 512 + 512]
                ysl = y[:, c * 1024 + h * 512: c * 1024 + h * 512 + 512]
                tsl = tmpf[:, h * 512:(h + 1) * 512]
                self.dve((lambda a, b, c2: lambda v: v.tensor_mul(a, b, c2))(
                    tsl, xsl, ps[0 + h]),
                    [rx[c], self.psr[0 + h]], r_tmp, f"ln{idx}.t{c}.{h}")
                self.dve((lambda a, b, c2: lambda v: v.tensor_sub(a, b, c2))(
                    ysl, tsl, ps[2 + h]),
                    r_tmp + [self.psr[2 + h]], [ry[c]], f"ln{idx}.y{c}.{h}")

    def emit_ffn(self, idx):
        P, s, ps, t = self.P, self.s, self.ps, self.t
        y, x = s["y"], s["x"]
        dbf, _dpg = self.carve("ffn_d", 0, [128, 16 * 1024], BF16)
        ebf, _epg = self.carve("ffn_e", 32 * 1024, [128, 16 * 1024], BF16)

        def dpg(n, h):
            return [self.R(f"pg{n}")]

        def epg(hc):
            return [self.R(f"pg{16 + hc}")]
        ry = [self.R(f"y{c}") for c in range(16)]
        rx = [self.R(f"x{c}") for c in range(16)]
        wst = [s[f"wst{i}"] for i in range(3)]
        r_w = [self.R(f"wst{i}") for i in range(3)]
        for p in range(4):
            for ncnk in range(16):
                gn = p * 16 + ncnk
                bi = gn % 3
                src = t["w0T"][:, gn * 128:(gn + 1) * 128].rearrange(
                    "(kc q) n -> q kc n", q=128)
                self.load(wst[bi].rearrange("q (kc n) -> q kc n", kc=16), src,
                          f"wst{bi}", f"f{idx}.w0.{gn}")
                for h in range(2):
                    pi = (ncnk * 2 + h) % 4
                    pairs = [(wst[bi][:, kc * 128:(kc + 1) * 128],
                              y[:, kc * 1024 + h * 512: kc * 1024 + h * 512 + 512])
                             for kc in range(16)]
                    self.mmg(ps[pi], pairs, [r_w[bi]] + ry, self.psr[pi],
                             f"f{idx}.dmm.{gn}.{h}")
                    dst = dbf[:, ncnk * 1024 + h * 512: ncnk * 1024 + h * 512 + 512]
                    self.act((lambda a, b, bc: lambda sc: sc.activation(
                        a, b, AF.Gelu, bias=bc))(dst, ps[pi], s["b0c"][:, gn:gn + 1]),
                        [self.psr[pi], self.R("b0c")], dpg(ncnk, h),
                        f"f{idx}.gelu.{gn}.{h}")
            for hc in range(16):
                bi = hc % 3
                src = t["w1T"][p * 2048:(p + 1) * 2048,
                               hc * 128:(hc + 1) * 128].rearrange(
                    "(kc q) n -> q kc n", q=128)
                self.load(wst[bi].rearrange("q (kc n) -> q kc n", kc=16), src,
                          f"wst{bi}", f"f{idx}.w1.{p}.{hc}")
                for h in range(2):
                    pi = 4 + (hc * 2 + h) % 2
                    pairs = [(wst[bi][:, n * 128:(n + 1) * 128],
                              dbf[:, n * 1024 + h * 512: n * 1024 + h * 512 + 512])
                             for n in range(16)]
                    self.mmg(ps[pi], pairs,
                             [r_w[bi]] + [self.R(f"pg{n}") for n in range(16)],
                             self.psr[pi], f"f{idx}.emm.{p}.{hc}.{h}")
                    dst = ebf[:, hc * 1024 + h * 512: hc * 1024 + h * 512 + 512]
                    if p == 0 and h == 0:
                        pass
                    if p == 0:
                        fn = (lambda a, b: lambda v: v.tensor_copy(a, b))(dst, ps[pi])
                    else:
                        fn = (lambda a, b: lambda v: v.tensor_add(a, a, b))(dst, ps[pi])
                    self.dve(fn, [self.psr[pi]] + (epg(hc) if p else []),
                             epg(hc), f"f{idx}.eacc.{p}.{hc}.{h}")
        for hc in range(16):
            xc = x[:, hc * 1024:(hc + 1) * 1024]
            ec = ebf[:, hc * 1024:(hc + 1) * 1024]
            self.dve((lambda a, b, bc: lambda v: v.scalar_tensor_tensor(
                a, b, bc, a, ALU.add, ALU.add))(xc, ec, s["b1c"][:, hc:hc + 1]),
                epg(hc) + [self.R("b1c"), rx[hc]], [rx[hc]],
                f"f{idx}.res.{hc}")

    PAGE = 2048

    def carve(self, name, off_bytes, shape, dtype):
        nbytes = {F32: 4, BF16: 2, I32: 4}[dtype]
        end = off_bytes + shape[1] * nbytes
        assert off_bytes % 4 == 0 and end <= 86 * 1024, name
        base = self.s["arena"].tensor.ap().bitcast(dtype)
        ap = base[0:shape[0], off_bytes // nbytes: off_bytes // nbytes + shape[1]]
        pages = [self.R(f"pg{i}") for i in
                 range(off_bytes // self.PAGE, (end + self.PAGE - 1) // self.PAGE)]
        return ap, pages

    def emit_setup(self):
        s, t = self.s, self.t
        x3 = t["x"].rearrange("(c p) t -> p c t", p=128)
        self.load(s["x"].rearrange("p (c t) -> p c t", c=16), x3, "x_in",
                  "ld_x", eng="sp")
        xld = self.R("x_in")
        for c in range(16):
            self.R(f"x{c}").writer = xld.writer
        for nm in ("b0c", "b1c", "slope_row", "iota_row", "kd_col", "bd_cols",
                   "idxt", "isodd", "pars", "gpen", "cbias"):
            self.load(s[nm], t[nm], nm, "ld_" + nm, eng="sp")
        self.load(s["rt"], t["rt"], "rt", "ld_rt", eng="pool")
        def ms(g):
            g.memset(s["ones_cb"], 1.0)
            g.memset(s["ones_rb"], 1.0)
            return g.memset(s["ones_rf"], 1.0)
        self.P.op("pool", ms, writes=[self.R("ones")], tag="ones")
        self.dve((lambda a, b: lambda v: v.tensor_scalar_mul(a, b, -1.0))(
            s["negs_row"], s["slope_row"]),
            [self.R("slope_row")], [self.R("negs")], "negs")

    def _negcol(self, h, tag, bank=7):
        s, ps = self.s, self.ps
        self.mmg(ps[bank][:, 0:1], [(s["ones_rf"], s["negs_row"][0:1, h:h + 1])],
                 [self.R("negs"), self.R("ones")], self.psr[bank], tag + ".ncb")
        self.dve((lambda a, b: lambda v: v.tensor_copy(a, b))(
            s["scol"][:, 0:1], ps[bank][:, 0:1]),
            [self.psr[bank]], [self.R("scol")], tag + ".ncc")

    def emit_attn_a(self):
        P, s, ps, t = self.P, self.s, self.ps, self.t
        y, x = s["y"], s["x"]
        ry = [self.R(f"y{c}") for c in range(16)]
        rx = [self.R(f"x{c}") for c in range(16)]
        wst = [s[f"wst{i}"] for i in range(3)]
        r_w = [self.R(f"wst{i}") for i in range(3)]
        kvpre, r_kvpre = self.carve("kvpre", 0, [128, 16 * 384], BF16)
        kvin_b, r_kvinb = self.carve("kvin_b", 12288, [128, 2048], BF16)
        kvf, r_kvf = self.carve("kvf", 16384, [128, 2048], F32)
        kvin_f, r_kvinf = self.carve("kvin_f", 24576, [128, 2048], F32)
        slab0, r_sl0 = self.carve("slab0", 32768, [128, 8192], BF16)
        slab1, r_sl1 = self.carve("slab1", 49152, [128, 8192], BF16)
        slabs = [(slab0, r_sl0), (slab1, r_sl1)]
        ktm_g, r_ktm = self.carve("ktm_g", 65536, [128, 4096], BF16)
        vtm_e, r_vtme = self.carve("vtm_e", 73728, [128, 4096], BF16)
        r_scol = self.R("scol")

        def kdc_prep(h, tag):
            self.act((lambda a, b, sc_: lambda sc: sc.activation(
                a, b, AF.Exp, scale=sc_))(s["scol"][:, 5:7], s["kd_col"],
                                          s["scol"][:, 0:1]),
                [self.R("kd_col"), r_scol], [r_scol], tag + ".kdc")

        def bdp_prep(h, tag):
            self.act((lambda a, b, sc_: lambda sc: sc.activation(
                a, b, AF.Exp, scale=sc_))(s["scol"][:, 1:5], s["bd_cols"],
                                          s["scol"][:, 0:1]),
                [self.R("bd_cols"), r_scol], [r_scol], tag + ".bdp")

        for g in range(4):
            for kind, coff, slot in (("v", 4096, 0), ("k", 2048, 1)):
                sl_ap, sl_r = slabs[slot]
                src = t["wqkvuT"][:, coff + g * 512: coff + (g + 1) * 512] \
                    .rearrange("(fc p) n -> p fc n", p=128)
                self.load(sl_ap.rearrange("p (fc n) -> p fc n", fc=16), src,
                          f"aslab{slot}", f"aa.ex.w{kind}.{g}",
                          writes_extra=sl_r)
            for tc in range(8):
                for kind, slot, dstap, dstres in (
                        ("v", 0, vtm_e, r_vtme), ("k", 1, ktm_g, r_ktm)):
                    sl_ap, sl_r = slabs[slot]
                    pi = (tc * 2 + slot) % 4
                    pairs = [(y[:, fc * 1024 + tc * 128: fc * 1024 + tc * 128 + 128],
                              sl_ap[:, fc * 512:(fc + 1) * 512])
                             for fc in range(16)]
                    self.mmg(ps[pi], pairs, sl_r + ry, self.psr[pi],
                             f"aa.ex.{kind}mm.{g}.{tc}")
                    dsl = dstap[:, tc * 512:(tc + 1) * 512]
                    if kind == "v":
                        self.dve((lambda a, b: lambda v: v.tensor_copy(a, b))(
                            dsl, ps[pi]), [self.psr[pi]], dstres,
                            f"aa.ex.vcp.{g}.{tc}")
                    else:
                        self.act((lambda a, b: lambda sc: sc.activation(
                            a, b, AF.Silu))(dsl, ps[pi]), [self.psr[pi]],
                            dstres, f"aa.ex.ksl.{g}.{tc}")
            for hh in range(4):
                h = g * 4 + hh
                self._negcol(h, f"aa.ex.h{h}")
                kdc_prep(h, f"aa.ex.h{h}")
                bdp_prep(h, f"aa.ex.h{h}")
                for tc in range(8):
                    sl = ktm_g[:, tc * 512 + hh * 128: tc * 512 + hh * 128 + 128]
                    self.dve((lambda a, c2: lambda v: v.tensor_scalar_mul(
                        a, a, c2))(sl, s["scol"][:, 5 + tc % 2: 6 + tc % 2]),
                        [r_scol] + r_ktm, r_ktm, f"aa.ex.kkd.{h}.{tc}")
                kvf_h = kvf[:, h * 128:(h + 1) * 128]
                for B in range(4):
                    pairs = [(ktm_g[:, (2 * B + j) * 512 + hh * 128:
                                    (2 * B + j) * 512 + hh * 128 + 128],
                              vtm_e[:, (2 * B + j) * 512 + hh * 128:
                                    (2 * B + j) * 512 + hh * 128 + 128])
                             for j in range(2)]
                    self.mmg(ps[6][:, 0:128], pairs, r_ktm + r_vtme,
                             self.psr[6], f"aa.ex.upd.{h}.{B}")
                    if B == 0:
                        fn = (lambda a, b: lambda v: v.tensor_copy(a, b))(
                            kvf_h, ps[6][:, 0:128])
                    else:
                        fn = (lambda a, b, sc_: lambda v: v.scalar_tensor_tensor(
                            a, a, sc_, b, ALU.mult, ALU.add))(
                            kvf_h, ps[6][:, 0:128], s["scol"][:, 2:3])
                    self.dve(fn, [self.psr[6], r_scol] + r_kvf, r_kvf,
                             f"aa.ex.kv.{h}.{B}")
                    if B < 3:
                        dsl = kvpre[:, h * 384 + B * 128: h * 384 + B * 128 + 128]
                        self.dve((lambda a, b: lambda v: v.tensor_copy(a, b))(
                            dsl, kvf_h), r_kvf, r_kvpre, f"aa.ex.pre.{h}.{B}")

        self._exchange_a(r_kvf, r_kvinf)
        self.dve((lambda a, b, c2: lambda v: v.tensor_scalar_mul(a, b, c2))(
            kvin_b, kvin_f, s["isodd"][:, 0:1]),
            r_kvinf + [self.R("isodd")], r_kvinb, "aa.kvinbf")

        Oar, r_O = self.carve("aaO", 49152, [128, 8192], BF16)
        ddt_t, r_ddt = self.carve("aaddt", 16384, [128, 512], F32)
        qd_t, r_qd = self.carve("aaqd", 18432, [128, 256], BF16)
        shat = [self.carve("aash0", 18944, [128, 256], BF16),
                self.carve("aash1", 19456, [128, 256], BF16)]
        qqd_t, r_qqd = self.carve("aaqqd", 19968, [128, 256], BF16)
        qgd_t, r_qgd = self.carve("aaqgd", 20480, [128, 256], BF16)
        sq_t, r_sqt = self.carve("aasq", 20992, [128, 256], BF16)
        vtm, r_vtm = self.carve("aavtm", 65536, [128, 8192], BF16)
        qk_t = [self.carve("aaq0", 81920, [128, 512], BF16),
                self.carve("aaq1", 82944, [128, 512], BF16),
                self.carve("aak0", 83968, [128, 512], BF16),
                self.carve("aak1", 84992, [128, 512], BF16)]

        for sub in range(2):
            S = sub * 512
            for vs in range(4):
                sl_ap, sl_r = slabs[vs % 2]
                src = t["wqkvuT"][:, 4096 + vs * 512: 4096 + (vs + 1) * 512] \
                    .rearrange("(fc p) n -> p fc n", p=128)
                self.load(sl_ap.rearrange("p (fc n) -> p fc n", fc=16), src,
                          f"aslab{vs % 2}", f"aa.m{sub}.wv.{vs}",
                          writes_extra=sl_r)
                for tc in range(4):
                    pi = (vs * 4 + tc) % 4
                    pairs = [(y[:, fc * 1024 + S + tc * 128:
                                fc * 1024 + S + tc * 128 + 128],
                              sl_ap[:, fc * 512:(fc + 1) * 512])
                             for fc in range(16)]
                    self.mmg(ps[pi], pairs, sl_r + ry, self.psr[pi],
                             f"aa.m{sub}.vmm.{vs}.{tc}")
                    dsl = vtm[:, tc * 2048 + vs * 512: tc * 2048 + (vs + 1) * 512]
                    self.dve((lambda a, b: lambda v: v.tensor_copy(a, b))(
                        dsl, ps[pi]), [self.psr[pi]], r_vtm,
                        f"aa.m{sub}.vcp.{vs}.{tc}")
            for h in range(16):
                qa, r_qa = qk_t[h % 2]
                ka, r_ka = qk_t[2 + h % 2]
                for kind, coff, dst, dres in (("q", 0, qa, r_qa),
                                              ("k", 2048, ka, r_ka)):
                    bi = h % 3
                    src = t["wqkvuT"][:, coff + h * 128: coff + (h + 1) * 128] \
                        .rearrange("(kc q) n -> q kc n", q=128)
                    self.load(wst[bi].rearrange("q (kc n) -> q kc n", kc=16),
                              src, f"wst{bi}", f"aa.m{sub}.w{kind}.{h}")
                    pi = 4 + (0 if kind == "q" else 1)
                    pairs = [(wst[bi][:, kc * 128:(kc + 1) * 128],
                              y[:, kc * 1024 + S: kc * 1024 + S + 512])
                             for kc in range(16)]
                    self.mmg(ps[pi], pairs, [r_w[bi]] + ry, self.psr[pi],
                             f"aa.m{sub}.{kind}mm.{h}")
                    self.act((lambda a, b: lambda sc: sc.activation(
                        a, b, AF.Silu))(dst, ps[pi]), [self.psr[pi]], dres,
                        f"aa.m{sub}.{kind}sl.{h}")
                self._negcol(h, f"aa.m{sub}.h{h}", bank=6)
                bdp_prep(h, f"aa.m{sub}.h{h}")
                self.act((lambda a, b, sc_: lambda sc: sc.activation(
                    a, b, AF.Exp, scale=sc_))(ddt_t, s["idxt"], s["scol"][:, 0:1]),
                    [self.R("idxt"), r_scol], r_ddt, f"aa.m{sub}.ddt.{h}")
                self.act((lambda a, b, sc_: lambda sc: sc.activation(
                    a, b, AF.Exp, scale=sc_))(s["srow_bf"][0:1, 0:256],
                                              s["iota_row"], s["negs_row"][0:1, h:h + 1]),
                    [self.R("iota_row"), self.R("negs")], [self.R("srow_bf")],
                    f"aa.m{sub}.qdr.{h}")
                self.mmg(ps[6][:, 0:256], [(s["ones_rb"], s["srow_bf"][0:1, 0:256])],
                         [self.R("srow_bf"), self.R("ones")], self.psr[6],
                         f"aa.m{sub}.qdb.{h}")
                self.dve((lambda a, b: lambda v: v.tensor_copy(a, b))(
                    qd_t, ps[6][:, 0:256]), [self.psr[6]], r_qd,
                    f"aa.m{sub}.qdc.{h}")
                for blk in range(2):
                    B = sub * 2 + blk
                    stp = ps[blk % 2]
                    qb = qa[:, blk * 256: blk * 256 + 256]
                    def st_fn(pp=stp, k0=ka[:, blk * 256: blk * 256 + 128],
                              k1=ka[:, blk * 256 + 128: blk * 256 + 256], qq=qb):
                        def f(te):
                            te.matmul(pp[:, 0:256], k0, qq, start=True,
                                      stop=True, skip_group_check=True)
                            return te.matmul(pp[:, 256:512], k1, qq, start=False,
                                             stop=True, skip_group_check=True)
                        return f
                    self.P.op("pe", st_fn(), reads=r_ka + r_qa,
                              writes=[self.psr[blk % 2]], tag=f"aa.st.{sub}.{h}.{blk}")
                    for j in range(2):
                        sh_ap, sh_r = shat[j]
                        self.dve((lambda a, b, c2: lambda v: v.tensor_mul(a, b, c2))(
                            sh_ap, stp[:, j * 256:(j + 1) * 256],
                            ddt_t[:, j * 256:(j + 1) * 256]),
                            [self.psr[blk % 2]] + r_ddt, sh_r,
                            f"aa.shat.{sub}.{h}.{blk}.{j}")
                    self.dve((lambda a, b, c2: lambda v: v.tensor_mul(a, b, c2))(
                        qqd_t, qb, qd_t), r_qa + r_qd, r_qqd,
                        f"aa.qqd.{sub}.{h}.{blk}")
                    self.dve((lambda a, b, c2: lambda v: v.tensor_scalar_mul(
                        a, b, c2))(qgd_t, qqd_t, s["scol"][:, 1 + B: 2 + B]),
                        r_qqd + [r_scol], r_qgd, f"aa.qgd.{sub}.{h}.{blk}")
                    opi = 2 + (h * 2 + blk) % 2
                    pairs = [(vtm[:, (2 * blk + j) * 2048 + h * 128:
                                  (2 * blk + j) * 2048 + h * 128 + 128],
                              shat[j][0]) for j in range(2)]
                    reads = r_vtm + shat[0][1] + shat[1][1] + r_qgd + r_qqd \
                        + r_kvinb + r_kvpre
                    if B > 0:
                        pairs.append((kvpre[:, h * 384 + (B - 1) * 128:
                                           h * 384 + B * 128], qqd_t))
                    pairs.append((kvin_b[:, h * 128:(h + 1) * 128], qgd_t))
                    self.mmg(ps[opi][:, 0:256], pairs, reads, self.psr[opi],
                             f"aa.omm.{sub}.{h}.{blk}")
                    osl = Oar[:, h * 512 + blk * 256: h * 512 + blk * 256 + 256]
                    self.dve((lambda a, b: lambda v: v.tensor_copy(a, b))(
                        osl, ps[opi][:, 0:256]), [self.psr[opi]], r_O,
                        f"aa.ocp.{sub}.{h}.{blk}")
                    self.act((lambda a, b: lambda sc: sc.activation(
                        a, b, AF.Square))(sq_t, osl), r_O, r_sqt,
                        f"aa.osq.{sub}.{h}.{blk}")
                    self.mmg(ps[7][0:1, blk * 256: blk * 256 + 256],
                             [(s["ones_cb"], sq_t)], r_sqt + [self.R("ones")],
                             self.psr[7], f"aa.ost.{sub}.{h}.{blk}",
                             accumulate=not (h == 0 and blk == 0))
            rows = s["rows"]
            r_rows = self.R("rows")
            self.act((lambda a, b, bb_=s["cbias"][0:1, 1:2]: lambda sc: sc.activation(
                a, b, AF.Ln, scale=1.0 / 2048.0, bias=bb_))(rows[:, 0:512],
                                                            ps[7][0:1, :]),
                [self.psr[7], self.R("cbias")], [r_rows], f"aa.rln.{sub}")
            self.act((lambda a, b: lambda sc: sc.activation(
                a, b, AF.Exp, scale=-0.5))(rows[:, 0:512], rows[:, 0:512]),
                [r_rows], [r_rows], f"aa.rex.{sub}")
            self.mmg(ps[6], [(s["ones_rf"], rows[0:1, 0:512])],
                     [r_rows, self.R("ones")], self.psr[6], f"aa.rbc.{sub}")
            for h in range(16):
                bi = h % 3
                src = t["wqkvuT"][:, 6144 + h * 128: 6144 + (h + 1) * 128] \
                    .rearrange("(kc q) n -> q kc n", q=128)
                self.load(wst[bi].rearrange("q (kc n) -> q kc n", kc=16), src,
                          f"wst{bi}", f"aa.m{sub}.wu.{h}")
                pi = 4 + h % 2
                pairs = [(wst[bi][:, kc * 128:(kc + 1) * 128],
                          y[:, kc * 1024 + S: kc * 1024 + S + 512])
                         for kc in range(16)]
                self.mmg(ps[pi], pairs, [r_w[bi]] + ry, self.psr[pi],
                         f"aa.umm.{sub}.{h}")
                osl = Oar[:, h * 512:(h + 1) * 512]
                self.dve((lambda a, b: lambda v: v.tensor_mul(a, a, b))(
                    osl, ps[pi]), [self.psr[pi]] + r_O, r_O,
                    f"aa.ou.{sub}.{h}")
                self.dve((lambda a, b: lambda v: v.tensor_mul(a, a, b))(
                    osl, ps[6]), [self.psr[6]] + r_O, r_O,
                    f"aa.our.{sub}.{h}")
            for oc in range(16):
                bi = oc % 3
                src = t["woutT"][:, oc * 128:(oc + 1) * 128].rearrange(
                    "(kc q) n -> q kc n", q=128)
                self.load(wst[bi].rearrange("q (kc n) -> q kc n", kc=16), src,
                          f"wst{bi}", f"aa.m{sub}.wo.{oc}")
                pi = 4 + oc % 2
                pairs = [(wst[bi][:, hc * 128:(hc + 1) * 128],
                          Oar[:, hc * 512:(hc + 1) * 512]) for hc in range(16)]
                self.mmg(ps[pi], pairs, [r_w[bi]] + r_O, self.psr[pi],
                         f"aa.wout.{sub}.{oc}")
                xsl = x[:, oc * 1024 + S: oc * 1024 + S + 512]
                self.dve((lambda a, b: lambda v: v.tensor_add(a, a, b))(
                    xsl, ps[pi]), [self.psr[pi], rx[oc]], [rx[oc]],
                    f"aa.res.{sub}.{oc}")

    def _exchange_a(self, r_kvf, r_kvinf):
        s, t = self.s, self.t
        kvf, _ = self.carve("kvf2", 16384, [128, 2048], F32)
        kvin_f, _ = self.carve("kvin_f2", 24576, [128, 2048], F32)
        sems = self.xsems["a"]

        def blob(g):
            sd, sp_ = sems
            src3 = kvf.rearrange("d (h e) -> d h e", h=16)
            dst3 = kvin_f.rearrange("d (h e) -> d h e", h=16)
            with g.register("rparA") as rpar, g.register("rflagA") as rflag, \
                    g.register("rnonA") as rnon, g.register("rcntA") as rcnt:
                g.reg_load(rpar, s["pars"][0:1, 0:1])
                g.reg_load(rnon, s["pars"][0:1, 1:2])
                with g.If_eq(rpar, 0):
                    g.dma_start(out=t["sha"][0].rearrange("(h d) e -> d h e", d=128),
                                in_=src3).then_inc(sd, 16)
                    g.wait_ge(sd, 16)
                    g.dma_start(out=t["shf"][0, 0:1, :], in_=s["pars"]).then_inc(sd, 16)
                    g.wait_ge(sd, 32)
                with g.Else():
                    g.dma_start(out=t["sha"][1].rearrange("(h d) e -> d h e", d=128),
                                in_=src3).then_inc(sd, 16)
                    g.wait_ge(sd, 16)
                    g.dma_start(out=t["shf"][0, 1:2, :], in_=s["pars"]).then_inc(sd, 16)
                    g.wait_ge(sd, 32)
                g.reg_mov(rcnt, 0)
                g.reg_mov(rflag, 0)
                with g.If_eq(rpar, 0):
                    with g.While(lambda: g.snap(rflag) != g.snap(rnon)):
                        g.reg_add(rcnt, rcnt, 16)
                        g.dma_start(out=s["pars"][0:1, 4:8],
                                    in_=t["shf"][0, 1:2, 0:4]).then_inc(sp_, 16)
                        g.wait_ge(sp_, rcnt)
                        g.reg_load(rflag, s["pars"][0:1, 5:6])
                    g.dma_start(out=dst3,
                                in_=t["sha"][1].rearrange("(h d) e -> d h e", d=128)
                                ).then_inc(sd, 16)
                    g.wait_ge(sd, 48)
                with g.Else():
                    with g.While(lambda: g.snap(rflag) != g.snap(rnon)):
                        g.reg_add(rcnt, rcnt, 16)
                        g.dma_start(out=s["pars"][0:1, 4:8],
                                    in_=t["shf"][0, 0:1, 0:4]).then_inc(sp_, 16)
                        g.wait_ge(sp_, rcnt)
                        g.reg_load(rflag, s["pars"][0:1, 5:6])
                    g.dma_start(out=dst3,
                                in_=t["sha"][0].rearrange("(h d) e -> d h e", d=128)
                                ).then_inc(sd, 16)
                    g.wait_ge(sd, 48)
            return g.nop()

        self.P.op("pool", blob, reads=r_kvf + [self.R("pars")],
                  writes=r_kvinf + [self.R("pars_poll")], tag="xchgA")

    def emit_attn_b(self):
        P, s, ps, t = self.P, self.s, self.ps, self.t
        y, x = s["y"], s["x"]
        ry = [self.R(f"y{c}") for c in range(16)]
        rx = [self.R(f"x{c}") for c in range(16)]
        wst = [s[f"wst{i}"] for i in range(3)]
        r_w = [self.R(f"wst{i}") for i in range(3)]
        Kf, r_Kf = self.carve("bKf", 0, [128, 8192], BF16)
        Vf, r_Vf = self.carve("bVf", 16384, [128, 8192], BF16)
        Kl, r_Kl = self.carve("bKl", 32768, [128, 4608], BF16)
        Vl, r_Vl = self.carve("bVl", 41984, [128, 4608], BF16)
        Ob, r_Ob = self.carve("bO", 51200, [128, 8192], BF16)
        cosb, r_cos = self.carve("bcos", 67584, [128, 1024], BF16)
        sinb, r_sin = self.carve("bsin", 69632, [128, 1024], BF16)
        ctri_s, r_ctri = self.carve("bctri", 71680, [128, 2560], BF16)
        lmask_s, r_lm = self.carve("blm", 76800, [128, 2560], BF16)
        kg_t, r_kg = self.carve("bkg", 81920, [128, 1024], BF16)
        t1_t, r_t1 = self.carve("bt1", 83968, [128, 512], BF16)
        t2_t, r_t2 = self.carve("bt2", 84992, [128, 512], BF16)
        E0, r_E0 = self.carve("bE0", 86016 - 2048, [128, 512], BF16)
        E1, r_E1 = self.carve("bE1", 86016 - 1024, [128, 512], BF16)
        Kown, r_Kown = self.carve("bKown", 0, [128, 4096], BF16)
        Vown, r_Vown = self.carve("bVown", 16384, [128, 4096], BF16)
        vslab, r_vsl = self.carve("bvslab", 51200, [128, 8192], BF16)
        qh_t, r_qh = self.carve("bqh", 81920, [128, 512], BF16)
        qro_t, r_qro = self.carve("bqro", 82944, [128, 512], BF16)
        sqq_t, r_sqq = self.carve("bsqq", 83968, [128, 512], BF16)
        bcols = self.sb("bcols", [128, 16], F32)
        r_bcols = self.R("bcols")
        rows, r_rows = s["rows"], self.R("rows")
        r_scol = self.R("scol")
        E = [(E0, r_E0), (E1, r_E1)]

        self.load(cosb, t["cosf"], "bcos", "bb.cos", writes_extra=r_cos)
        self.load(sinb, t["sinf"], "bsin", "bb.sin", writes_extra=r_sin)

        for g in range(4):
            bi = g % 3
            src = t["wk"][:, g * 128:(g + 1) * 128].rearrange(
                "(kc q) n -> q kc n", q=128)
            self.load(wst[bi].rearrange("q (kc n) -> q kc n", kc=16), src,
                      f"wst{bi}", f"bb.wk.{g}")
            for hhalf in range(2):
                pi = hhalf
                pairs = [(wst[bi][:, kc * 128:(kc + 1) * 128],
                          y[:, kc * 1024 + hhalf * 512: kc * 1024 + hhalf * 512 + 512])
                         for kc in range(16)]
                self.mmg(ps[pi], pairs, [r_w[bi]] + ry, self.psr[pi],
                         f"bb.kmm.{g}.{hhalf}")
                ksl = kg_t[:, hhalf * 512:(hhalf + 1) * 512]
                self.dve((lambda a, b: lambda v: v.tensor_copy(a, b))(
                    ksl, ps[pi]), [self.psr[pi]], r_kg, f"bb.kcp.{g}.{hhalf}")
                self.mmg(ps[2 + hhalf], [(s["rt"], ksl)],
                         [self.R("rt")] + r_kg, self.psr[2 + hhalf],
                         f"bb.krot.{g}.{hhalf}")
                self.dve((lambda a, b, c2: lambda v: v.tensor_mul(a, b, c2))(
                    t1_t, ksl, cosb[:, hhalf * 512:(hhalf + 1) * 512]),
                    r_kg + r_cos, r_t1, f"bb.kc1.{g}.{hhalf}")
                self.dve((lambda a, b, c2: lambda v: v.tensor_mul(a, b, c2))(
                    t2_t, ps[2 + hhalf], sinb[:, hhalf * 512:(hhalf + 1) * 512]),
                    [self.psr[2 + hhalf]] + r_sin, r_t2, f"bb.ks1.{g}.{hhalf}")
                dsl = Kown[:, g * 1024 + hhalf * 512: g * 1024 + (hhalf + 1) * 512]
                self.dve((lambda a, b, c2: lambda v: v.tensor_add(a, b, c2))(
                    dsl, t1_t, t2_t), r_t1 + r_t2, r_Kown,
                    f"bb.kro.{g}.{hhalf}")
        src = t["wv"].rearrange("(fc p) n -> p fc n", p=128)
        self.load(vslab.rearrange("p (fc n) -> p fc n", fc=16), src,
                  "bvslab", "bb.wv", writes_extra=r_vsl)
        for tc in range(8):
            pi = tc % 4
            pairs = [(y[:, fc * 1024 + tc * 128: fc * 1024 + tc * 128 + 128],
                      vslab[:, fc * 512:(fc + 1) * 512]) for fc in range(16)]
            self.mmg(ps[pi], pairs, r_vsl + ry, self.psr[pi], f"bb.vmm.{tc}")
            dsl = Vown[:, tc * 512:(tc + 1) * 512]
            self.dve((lambda a, b: lambda v: v.tensor_copy(a, b))(
                dsl, ps[pi]), [self.psr[pi]], r_Vown, f"bb.vcp.{tc}")

        self._exchange_b(r_Kown, r_Vown, r_Kf, r_Vf, r_Kl, r_Vl, Kf, Vf, Kl, Vl)

        for g in range(4):
            for q4 in range(4):
                ksl = Kf[:, g * 2048 + q4 * 512: g * 2048 + (q4 + 1) * 512]
                self.act((lambda a, b: lambda sc: sc.activation(
                    a, b, AF.Square))(sqq_t, ksl), r_Kf, r_sqq,
                    f"bb.ksq.{g}.{q4}")
                self.mmg(ps[4][0:1, :], [(s["ones_cb"], sqq_t)],
                         r_sqq + [self.R("ones")], self.psr[4],
                         f"bb.knm.{g}.{q4}")
                self.dve((lambda a, b, i=g * 4 + q4: lambda v: v.reduce_max(
                    rows[:, 512 + i:513 + i], b, axis=AX.X))(None, ps[4][0:1, :]),
                    [self.psr[4]], [r_rows], f"bb.kmx.{g}.{q4}")
        self.dve((lambda a: lambda v: v.reduce_max(
            rows[:, 528:529], a, axis=AX.X))(rows[0:1, 512:528]),
            [r_rows], [r_rows], "bb.kmx2")
        self.act((lambda a, b: lambda sc: sc.activation(a, b, AF.Sqrt))(
            rows[:, 529:530], rows[:, 528:529]), [r_rows], [r_rows], "bb.ksqrt")

        for st_ in range(2):
            S = st_ * 512
            self.load(ctri_s, t["ctri"][:, st_ * 2560:(st_ + 1) * 2560],
                      "bctri", f"bb.ctri.{st_}", eng="sp", writes_extra=r_ctri)
            self.load(lmask_s, t["lmask"][:, st_ * 2560:(st_ + 1) * 2560],
                      "blm", f"bb.lm.{st_}", eng="sp", writes_extra=r_lm)
            n_main = 7 if st_ == 0 else 11
            for h in range(16):
                g = h // 4
                bi = h % 3
                src = t["wq"][:, h * 128:(h + 1) * 128].rearrange(
                    "(kc q) n -> q kc n", q=128)
                self.load(wst[bi].rearrange("q (kc n) -> q kc n", kc=16), src,
                          f"wst{bi}", f"bb.wq.{st_}.{h}")
                pairs = [(wst[bi][:, kc * 128:(kc + 1) * 128],
                          y[:, kc * 1024 + S: kc * 1024 + S + 512])
                         for kc in range(16)]
                self.mmg(ps[6], pairs, [r_w[bi]] + ry, self.psr[6],
                         f"bb.qmm.{st_}.{h}")
                self.dve((lambda a, b: lambda v: v.tensor_copy(a, b))(
                    qh_t, ps[6]), [self.psr[6]], r_qh, f"bb.qcp.{st_}.{h}")
                self.mmg(ps[7], [(s["rt"], qh_t)], [self.R("rt")] + r_qh,
                         self.psr[7], f"bb.qrot.{st_}.{h}")
                self.dve((lambda a, b, c2: lambda v: v.tensor_mul(a, b, c2))(
                    t1_t, qh_t, cosb[:, S:S + 512]), r_qh + r_cos, r_t1,
                    f"bb.qc1.{st_}.{h}")
                self.dve((lambda a, b, c2: lambda v: v.tensor_mul(a, b, c2))(
                    t2_t, ps[7], sinb[:, S:S + 512]),
                    [self.psr[7]] + r_sin, r_t2, f"bb.qs1.{st_}.{h}")
                self.dve((lambda a, b, c2: lambda v: v.tensor_add(a, b, c2))(
                    qro_t, t1_t, t2_t), r_t1 + r_t2, r_qro,
                    f"bb.qro.{st_}.{h}")
                self.act((lambda a, b: lambda sc: sc.activation(
                    a, b, AF.Square))(sqq_t, qro_t), r_qro, r_sqq,
                    f"bb.qsq.{st_}.{h}")
                self.mmg(ps[6][0:1, :], [(s["ones_cb"], sqq_t)],
                         r_sqq + [self.R("ones")], self.psr[6],
                         f"bb.qnm.{st_}.{h}")
                self.dve((lambda b: lambda v: v.reduce_max(
                    rows[:, 530:531], b, axis=AX.X))(ps[6][0:1, :]),
                    [self.psr[6]], [r_rows], f"bb.qmx.{st_}.{h}")
                self.act((lambda a, b: lambda sc: sc.activation(a, b, AF.Sqrt))(
                    rows[:, 531:532], rows[:, 530:531]), [r_rows], [r_rows],
                    f"bb.qsqrt.{st_}.{h}")
                self.dve((lambda a, b, c2: lambda v: v.tensor_mul(a, b, c2))(
                    rows[:, 532:533], rows[:, 531:532], rows[:, 529:530]),
                    [r_rows], [r_rows], f"bb.negm1.{st_}.{h}")
                self.dve((lambda a, b: lambda v: v.tensor_scalar_mul(
                    a, b, -1.05 * SCALE_B))(rows[:, 533:534], rows[:, 532:533]),
                    [r_rows], [r_rows], f"bb.negm2.{st_}.{h}")
                self.mmg(ps[7][:, 0:1], [(s["ones_rf"], rows[0:1, 533:534])],
                         [r_rows, self.R("ones")], self.psr[7],
                         f"bb.nmb.{st_}.{h}")
                self.dve((lambda a, b: lambda v: v.tensor_copy(a, b))(
                    s["scol"][:, 0:1], ps[7][:, 0:1]), [self.psr[7]],
                    [r_scol], f"bb.nmc.{st_}.{h}")
                self.dve((lambda a, b, c2: lambda v: v.tensor_scalar_add(
                    a, b, c2))(bcols, s["gpen"][:, st_ * 16: st_ * 16 + 16],
                               s["scol"][:, 0:1]),
                    [self.R("gpen"), r_scol], [r_bcols], f"bb.bcols.{st_}.{h}")
                for kc in range(n_main):
                    spi = kc % 2
                    self.mmg(ps[spi], [(Kf[:, g * 2048 + kc * 128:
                                           g * 2048 + (kc + 1) * 128], qro_t)],
                             r_Kf + r_qro, self.psr[spi],
                             f"bb.smm.{st_}.{h}.{kc}")
                    Ea, r_Ea = E[kc % 2]
                    self.act((lambda a, b, bc: lambda sc: sc.activation(
                        a, b, AF.Exp, bias=bc, scale=SCALE_B))(
                        Ea, ps[spi], bcols[:, kc:kc + 1]),
                        [self.psr[spi], r_bcols], r_Ea,
                        f"bb.exp.{st_}.{h}.{kc}")
                    self.mmg(ps[4][0:1, :], [(s["ones_cb"], Ea)],
                             r_Ea + [self.R("ones")], self.psr[4],
                             f"bb.zg.{st_}.{h}.{kc}", accumulate=(kc > 0))
                    self.mmg(ps[2], [(Vf[:, kc * 512 + g * 128:
                                         kc * 512 + (g + 1) * 128], Ea)],
                             r_Vf + r_Ea, self.psr[2],
                             f"bb.pv.{st_}.{h}.{kc}", accumulate=(kc > 0))
                for j in range(5):
                    jj = 4 * st_ + j
                    spi = j % 2
                    self.mmg(ps[spi], [(Kl[:, g * 1152 + jj * 128:
                                           g * 1152 + (jj + 1) * 128], qro_t)],
                             r_Kl + r_qro, self.psr[spi],
                             f"bb.lsmm.{st_}.{h}.{j}")
                    Ea, r_Ea = E[j % 2]
                    Eb, r_Eb = E[1 - j % 2]
                    self.act((lambda a, b, bc: lambda sc: sc.activation(
                        a, b, AF.Exp, bias=bc, scale=SCALE_B))(
                        Ea, ps[spi], s["scol"][:, 0:1]),
                        [self.psr[spi], r_scol], r_Ea,
                        f"bb.lexp.{st_}.{h}.{j}")
                    self.dve((lambda a, b, c2: lambda v: v.tensor_mul(a, b, c2))(
                        Eb, Ea, ctri_s[:, j * 512:(j + 1) * 512]),
                        r_Ea + r_ctri, r_Eb, f"bb.ec.{st_}.{h}.{j}")
                    self.dve((lambda a, b, c2: lambda v: v.tensor_mul(a, a, c2))(
                        Ea, Ea, lmask_s[:, j * 512:(j + 1) * 512]),
                        r_Ea + r_lm, r_Ea, f"bb.el.{st_}.{h}.{j}")
                    self.mmg(ps[4][0:1, :], [(s["ones_cb"], Eb)],
                             r_Eb + [self.R("ones")], self.psr[4],
                             f"bb.lzg.{st_}.{h}.{j}", accumulate=True)
                    self.mmg(ps[5][0:1, :], [(s["ones_cb"], Ea)],
                             r_Ea + [self.R("ones")], self.psr[5],
                             f"bb.lzl.{st_}.{h}.{j}", accumulate=(j > 0))
                    vsl = Vl[:, jj * 512 + g * 128: jj * 512 + (g + 1) * 128]
                    self.mmg(ps[2], [(vsl, Eb)], r_Vl + r_Eb, self.psr[2],
                             f"bb.lpv.{st_}.{h}.{j}", accumulate=True)
                    self.mmg(ps[3], [(vsl, Ea)], r_Vl + r_Ea, self.psr[3],
                             f"bb.lpvl.{st_}.{h}.{j}", accumulate=(j > 0))
                self.dve((lambda a, b: lambda v: v.tensor_copy(a, b))(
                    E0, ps[2]), [self.psr[2]], r_E0, f"bb.ouc.{st_}.{h}")
                self.dve((lambda a, b: lambda v: v.tensor_copy(a, b))(
                    E1, ps[3]), [self.psr[3]], r_E1, f"bb.olc.{st_}.{h}")
                self.act((lambda a, b: lambda sc: sc.activation(
                    a, b, AF.Ln))(rows[:, 0:512], ps[4][0:1, :]),
                    [self.psr[4]], [r_rows], f"bb.cgl.{st_}.{h}")
                self.act((lambda a, b, bb_=s["cbias"][0:1, 2:3]: lambda sc: sc.activation(
                    a, b, AF.Exp, scale=-1.0, bias=bb_))(rows[:, 0:512],
                                                         rows[:, 0:512]),
                    [r_rows, self.R("cbias")], [r_rows], f"bb.cge.{st_}.{h}")
                self.mmg(ps[6], [(s["ones_rf"], rows[0:1, 0:512])],
                         [r_rows, self.R("ones")], self.psr[6],
                         f"bb.cgb.{st_}.{h}")
                self.dve((lambda a, b, c2: lambda v: v.tensor_mul(a, b, c2))(
                    qh_t, E0, ps[6]), r_E0 + [self.psr[6]], r_qh,
                    f"bb.on1.{st_}.{h}")
                self.act((lambda a, b: lambda sc: sc.activation(
                    a, b, AF.Ln))(rows[:, 0:512], ps[5][0:1, :]),
                    [self.psr[5]], [r_rows], f"bb.cll.{st_}.{h}")
                self.act((lambda a, b, bb_=s["cbias"][0:1, 2:3]: lambda sc: sc.activation(
                    a, b, AF.Exp, scale=-1.0, bias=bb_))(rows[:, 0:512],
                                                         rows[:, 0:512]),
                    [r_rows, self.R("cbias")], [r_rows], f"bb.cle.{st_}.{h}")
                self.mmg(ps[6], [(s["ones_rf"], rows[0:1, 0:512])],
                         [r_rows, self.R("ones")], self.psr[6],
                         f"bb.clb.{st_}.{h}")
                self.dve((lambda a, b, c2: lambda v: v.tensor_mul(a, b, c2))(
                    qro_t, E1, ps[6]), r_E1 + [self.psr[6]], r_qro,
                    f"bb.on2.{st_}.{h}")
                osl = Ob[:, h * 512:(h + 1) * 512]
                self.dve((lambda a, b, c2: lambda v: v.tensor_add(a, b, c2))(
                    osl, qh_t, qro_t), r_qh + r_qro, r_Ob,
                    f"bb.ow.{st_}.{h}")
            for oc in range(16):
                bi = oc % 3
                src = t["wo"][:, oc * 128:(oc + 1) * 128].rearrange(
                    "(kc q) n -> q kc n", q=128)
                self.load(wst[bi].rearrange("q (kc n) -> q kc n", kc=16), src,
                          f"wst{bi}", f"bb.wo.{st_}.{oc}")
                pi = 4 + oc % 2
                pairs = [(wst[bi][:, hc * 128:(hc + 1) * 128],
                          Ob[:, hc * 512:(hc + 1) * 512]) for hc in range(16)]
                self.mmg(ps[pi], pairs, [r_w[bi]] + r_Ob, self.psr[pi],
                         f"bb.womm.{st_}.{oc}")
                xsl = x[:, oc * 1024 + S: oc * 1024 + S + 512]
                self.dve((lambda a, b: lambda v: v.tensor_add(a, a, b))(
                    xsl, ps[pi]), [self.psr[pi], rx[oc]], [rx[oc]],
                    f"bb.res.{st_}.{oc}")

    def _exchange_b(self, r_Kown, r_Vown, r_Kf, r_Vf, r_Kl, r_Vl,
                    Kf, Vf, Kl, Vl):
        s, t = self.s, self.t
        Kown, _ = self.carve("bKown2", 0, [128, 4096], BF16)
        Vown, _ = self.carve("bVown2", 16384, [128, 4096], BF16)
        sems = self.xsems["b"]

        def blob(g):
            sd, sp_ = sems
            ko3 = Kown.rearrange("d (g t) -> d g t", g=4)
            vo3 = Vown.rearrange("p (tc v) -> p tc v", tc=8)
            kf3 = Kf.rearrange("d (g t) -> d g t", g=4)
            vf3 = Vf.rearrange("p (tc v) -> p tc v", tc=16)
            kl3 = Kl.rearrange("d (g t) -> d g t", g=4)
            vl3 = Vl.rearrange("p (tc v) -> p tc v", tc=9)
            shk = [t["shb_k"][i].rearrange("(g d) t -> d g t", d=128)
                   for i in range(2)]
            shv = [t["shb_v"][i].rearrange("(tc p) v -> p tc v", p=128)
                   for i in range(2)]
            with g.register("rparB") as rpar, g.register("rflagB") as rflag, \
                    g.register("rnonB") as rnon, g.register("rcntB") as rcnt:
                g.reg_load(rpar, s["pars"][0:1, 0:1])
                g.reg_load(rnon, s["pars"][0:1, 2:3])
                for p in range(2):
                    ctx = g.If_eq(rpar, 0) if p == 0 else g.Else()
                    with ctx:
                        g.dma_start(out=shk[p], in_=ko3).then_inc(sd, 16)
                        g.dma_start(out=shv[p], in_=vo3).then_inc(sd, 16)
                        g.wait_ge(sd, 32)
                        g.dma_start(out=t["shf"][1, p:p + 1, :],
                                    in_=s["pars"]).then_inc(sd, 16)
                        g.wait_ge(sd, 48)
                g.reg_mov(rcnt, 0)
                g.reg_mov(rflag, 0)
                for p in range(2):
                    o = 1 - p
                    ctx = g.If_eq(rpar, 0) if p == 0 else g.Else()
                    with ctx:
                        with g.While(lambda: g.snap(rflag) != g.snap(rnon)):
                            g.reg_add(rcnt, rcnt, 16)
                            g.dma_start(out=s["pars"][0:1, 4:8],
                                        in_=t["shf"][1, o:o + 1, 0:4]).then_inc(sp_, 16)
                            g.wait_ge(sp_, rcnt)
                            g.reg_load(rflag, s["pars"][0:1, 6:7])
                        g.dma_start(out=kf3[:, :, p * 1024:(p + 1) * 1024],
                                    in_=shk[p]).then_inc(sd, 16)
                        g.dma_start(out=kf3[:, :, o * 1024:(o + 1) * 1024],
                                    in_=shk[o]).then_inc(sd, 16)
                        g.dma_start(out=vf3[:, p * 8:(p + 1) * 8, :],
                                    in_=shv[p]).then_inc(sd, 16)
                        g.dma_start(out=vf3[:, o * 8:(o + 1) * 8, :],
                                    in_=shv[o]).then_inc(sd, 16)
                        g.dma_start(out=kl3[:, :, 0:128],
                                    in_=shk[o][:, :, 896:1024]).then_inc(sd, 16)
                        g.dma_start(out=kl3[:, :, 128:1152],
                                    in_=shk[p]).then_inc(sd, 16)
                        g.dma_start(out=vl3[:, 0:1, :],
                                    in_=shv[o][:, 7:8, :]).then_inc(sd, 16)
                        g.dma_start(out=vl3[:, 1:9, :],
                                    in_=shv[p]).then_inc(sd, 16)
                        g.wait_ge(sd, 48 + 8 * 16)
            return g.nop()

        self.P.op("pool", blob, reads=r_Kown + r_Vown + [self.R("pars")],
                  writes=r_Kf + r_Vf + r_Kl + r_Vl + [self.R("pars_poll")],
                  tag="xchgB")

    def emit_store(self):
        s, t = self.s, self.t
        rx = [self.R(f"x{c}") for c in range(16)]
        o3 = t["out"].rearrange("(c p) t -> p c t", p=128)
        so = self.P.op("sp", (lambda d, src: lambda g: g.dma_start(out=d, in_=src))(
            o3, s["x"].rearrange("p (c t) -> p c t", c=16)),
            reads=rx, dma=True, tag="store", slot="store")
        self.P.op("sp", lambda g: g.nop(),
                  deps=[so] + self.tap_stores, tag="final")

    def build(self):
        self.declare()
        with contextlib.ExitStack() as stack:
            self.xsems = {
                "a": (stack.enter_context(self.nc.semaphore("xad")),
                      stack.enter_context(self.nc.semaphore("xap"))),
                "b": (stack.enter_context(self.nc.semaphore("xbd")),
                      stack.enter_context(self.nc.semaphore("xbp"))),
            }
            self.emit_setup()
            self.emit_ln(1)
            self.emit_attn_a()
            self.tap("x1", self.s["x"], [128, 16 * 1024], F32,
                     [self.R(f"x{c}") for c in range(16)])
            if self.n_sub >= 2:
                self.emit_ln(2)
                self.emit_ffn(1)
                self.tap("x2", self.s["x"], [128, 16 * 1024], F32,
                         [self.R(f"x{c}") for c in range(16)])
            if self.n_sub >= 3:
                self.emit_ln(3)
                self.emit_attn_b()
                self.tap("x3", self.s["x"], [128, 16 * 1024], F32,
                         [self.R(f"x{c}") for c in range(16)])
            if self.n_sub >= 4:
                self.emit_ln(4)
                self.emit_ffn(2)
            self.emit_store()
            self.P.finalize(paranoid=bool(os.environ.get("KFW_PARANOID")))
            sems = {nm: stack.enter_context(self.nc.semaphore(nm.replace(".", "_")))
                    for nm in self.P.sem_names()}
            block = stack.enter_context(self.nc.Block())

            @block.tensor
            def _(te):
                self.P.emit_engine("pe", te, sems)

            @block.scalar
            def _(sc):
                self.P.emit_engine("act", sc, sems)

            @block.vector
            def _(v):
                self.P.emit_engine("dve", v, sems)

            @block.gpsimd
            def _(g):
                self.P.emit_engine("pool", g, sems)

            @block.sync
            def _(sp):
                self.P.emit_engine("sp", sp, sems)
        return self.nc


def build_nc(n_sub=None, taps=None, detect=True, private_shared=False):
    nc = bass.Bass("TRN2", target_bir_lowering=False, debug=False,
                   detect_race_conditions=detect)
    kb = KB(nc, n_sub=N_SUB if n_sub is None else n_sub,
            taps=TAPS if taps is None else taps, private_shared=private_shared)
    kb.build()
    return nc


_NC_CACHE = {}


def make_inputs(inputs_full, nonce):
    x_full = np.ascontiguousarray(
        np.asarray(inputs_full["L0_attn_a_x"], np.float32).transpose(0, 2, 1))
    wqkvuT = np.ascontiguousarray(np.asarray(inputs_full["L0_attn_a_W_qkvu"], np.float32).T)
    woutT = np.ascontiguousarray(np.asarray(inputs_full["L0_attn_a_W_out"], np.float32).T)
    w0T = np.ascontiguousarray(np.asarray(inputs_full["L0_ffn_W0"], np.float32).T)
    w1T = np.ascontiguousarray(np.asarray(inputs_full["L0_ffn_W1"], np.float32).T)
    b0 = np.asarray(inputs_full["L0_ffn_bias0"], np.float32)
    b1 = np.asarray(inputs_full["L0_ffn_bias1"], np.float32)
    shared = {
        "wqkvuT": wqkvuT, "woutT": woutT,
        "wq": np.asarray(inputs_full["L0_attn_b_wq"], np.float32),
        "wk": np.asarray(inputs_full["L0_attn_b_wk"], np.float32),
        "wv": np.asarray(inputs_full["L0_attn_b_wv"], np.float32),
        "wo": np.asarray(inputs_full["L0_attn_b_wo"], np.float32),
        "w0T": w0T, "w1T": w1T,
        "b0c": np.ascontiguousarray(b0.reshape(64, 128).T),
        "b1c": np.ascontiguousarray(b1.reshape(16, 128).T),
        "slope_row": np.asarray(inputs_full["L0_attn_a_slope_rate"],
                                np.float32).reshape(1, 16),
        "cbias": np.array([[1e-5, 1e-6, LN05, 0.0]], np.float32),
    }
    iota_row, kd_col, bd_cols, idxt, rt = _host_consts()
    cos_f, sin_f = _rope_tables()
    consts = {"iota_row": iota_row, "kd_col": kd_col, "bd_cols": bd_cols,
              "idxt": idxt, "rt": rt}
    in_maps = []
    for c in range(8):
        b, par = c // 2, c % 2
        t0 = par * 1024
        gpen, ctri, lmask = _attnb_masks(par)
        m = dict(shared)
        m.update(consts)
        m["x"] = np.ascontiguousarray(x_full[b][:, t0:t0 + 1024])
        m["cosf"] = np.ascontiguousarray(cos_f[:, t0:t0 + 1024])
        m["sinf"] = np.ascontiguousarray(sin_f[:, t0:t0 + 1024])
        m["isodd"] = np.full((128, 1), float(par), np.float32)
        m["pars"] = np.array([[par, nonce + 1, nonce + 2, 0, 0, 0, 0, 0]],
                             np.int32)
        m["gpen"] = gpen
        m["ctri"] = ctri
        m["lmask"] = lmask
        in_maps.append(m)
    return in_maps


def kernel(**inputs):
    import time
    key = (N_SUB, tuple(sorted(TAPS)))
    if key not in _NC_CACHE:
        _NC_CACHE[key] = build_nc()
    nc = _NC_CACHE[key]
    nonce = (int(time.time() * 1000) & 0x3FFFFFF) * 16 + 3
    in_maps = make_inputs(inputs, nonce)
    res = run_bass_kernel_spmd(nc, in_maps, core_ids=list(range(8)))
    out = np.zeros((4, 2048, 2048), np.float32)
    for c in range(8):
        b, par = c // 2, c % 2
        shard = np.asarray(res.results[c]["out"])
        out[b, par * 1024:(par + 1) * 1024, :] = shard.T
    return out


# version 03
# speedup vs baseline: 1.0539x; 1.0539x over previous
import contextlib
import os
import numpy as np
import ml_dtypes
import concourse.bass as bass
import concourse.mybir as mybir
from concourse.bass_utils import run_bass_kernel_spmd

F32 = mybir.dt.float32
BF16 = mybir.dt.bfloat16
I32 = mybir.dt.int32
AF = mybir.ActivationFunctionType
ALU = mybir.AluOpType
AX = mybir.AxisListType

SCALE_B = float(1.0 / np.sqrt(128.0))
LN05 = float(np.log(0.5))

N_SUB = 4
TAPS = ()

ENGINES = ("pe", "act", "dve", "pool", "sp")


class Op:
    __slots__ = ("eng", "fn", "deps", "dma", "idx", "cum", "sem", "needed",
                 "tag")

    def __init__(self, eng, fn, deps, dma, idx, tag, slot=None):
        self.eng = eng
        self.fn = fn
        self.deps = deps
        self.dma = dma
        self.idx = idx
        self.cum = None
        if dma:
            assert slot is not None, f"dma op {tag} needs slot="
            self.sem = f"d.{slot}"
        else:
            self.sem = eng
        self.needed = False
        self.tag = tag


class Res:
    __slots__ = ("writer", "readers", "name")

    def __init__(self, name=""):
        self.writer = None
        self.readers = []
        self.name = name


class Prog:
    def __init__(self):
        self.ops = {e: [] for e in ENGINES}
        self.n = 0

    def op(self, eng, fn, reads=(), writes=(), deps=(), dma=False, tag="",
           slot=None):
        dd = [d for d in deps if d is not None]
        for r in reads:
            if r.writer is not None:
                dd.append(r.writer)
        for w in writes:
            if w.writer is not None:
                dd.append(w.writer)
            dd.extend(w.readers)
        if dma and slot is None and writes:
            slot = writes[0].name
        o = Op(eng, fn, dd, dma, len(self.ops[eng]), tag, slot=slot)
        self.ops[eng].append(o)
        self.n += 1
        for r in reads:
            r.readers.append(o)
        for w in writes:
            w.writer = o
            w.readers = []
        return o

    def finalize(self, paranoid=False):
        self.paranoid = paranoid
        for e in ENGINES:
            for o in self.ops[e]:
                if o.dma:
                    o.needed = True
                for d in o.deps:
                    if d.eng != o.eng or d.dma or paranoid:
                        d.needed = True
        counts = {}
        for e in ENGINES:
            for o in self.ops[e]:
                if o.needed:
                    amt = 16 if o.dma else 1
                    counts[o.sem] = counts.get(o.sem, 0) + amt
                    o.cum = counts[o.sem]
        self.sem_totals = counts
        self._simulate()

    def _simulate(self):
        pos = {e: 0 for e in ENGINES}
        vals = {s: 0 for s in self.sem_totals}
        progress = True
        while progress:
            progress = False
            for e in ENGINES:
                while pos[e] < len(self.ops[e]):
                    o = self.ops[e][pos[e]]
                    ok = True
                    for d in o.deps:
                        if d.eng == e and not d.dma:
                            continue
                        if vals.get(d.sem, 0) < d.cum:
                            ok = False
                            break
                    if not ok:
                        break
                    if o.needed:
                        vals[o.sem] = vals.get(o.sem, 0) + (16 if o.dma else 1)
                    pos[e] += 1
                    progress = True
        stuck = {e: len(self.ops[e]) - pos[e] for e in ENGINES
                 if pos[e] < len(self.ops[e])}
        if stuck:
            for e in ENGINES:
                if pos[e] < len(self.ops[e]):
                    o = self.ops[e][pos[e]]
                    info = [(d.eng, d.tag, d.cum, vals.get(d.sem, 0))
                            for d in o.deps if not (d.eng == e and not d.dma)]
                    raise RuntimeError(
                        f"DEADLOCK: engine {e} stuck at op {pos[e]} "
                        f"tag={o.tag} deps={info}; remaining={stuck}")

    def emit_engine(self, e, eng, sem_handles):
        hiw = {}
        paranoid = getattr(self, "paranoid", False)
        for o in self.ops[e]:
            for d in o.deps:
                if d.eng == e and not d.dma and not paranoid:
                    continue
                if hiw.get(d.sem, 0) >= d.cum:
                    continue
                eng.wait_ge(sem_handles[d.sem], d.cum)
                hiw[d.sem] = d.cum
            inst = o.fn(eng)
            if o.needed:
                if inst is None:
                    raise RuntimeError(
                        f"op {o.tag} on {e} needs inc but returned None")
                inst.then_inc(sem_handles[o.sem], 16 if o.dma else 1)

    def sem_names(self):
        return sorted(self.sem_totals)



def _host_consts():
    arr = np.arange(1, 257, dtype=np.float32)
    iota_row = arr[None, :]
    kd_col = np.zeros((128, 2), np.float32)
    for sub in range(2):
        kd_col[:, sub] = 255.0 - (sub * 128 + np.arange(128))
    bd_cols = np.zeros((128, 4), np.float32)
    for i in range(4):
        bd_cols[:, i] = 256.0 * i
    k_ = np.arange(256)[:, None]
    q_ = np.arange(256)[None, :]
    idxt_full = np.where(q_ >= k_, (q_ - k_).astype(np.float32), 1.0e6)
    idxt = np.zeros((128, 512), np.float32)
    idxt[:, 0:256] = idxt_full[0:128]
    idxt[:, 256:512] = idxt_full[128:256]
    R = np.zeros((128, 128), np.float32)
    for d in range(64):
        R[d, 64 + d] = -1.0
        R[64 + d, d] = 1.0
    rt = np.ascontiguousarray(R.T)
    return iota_row, kd_col, bd_cols, idxt, rt


def _rope_tables(T_total=2048, D=128, base=10000.0):
    inv = 1.0 / base ** (np.arange(0, D, 2, dtype=np.float32) / D)
    t = np.arange(T_total, dtype=np.float32)
    freqs = t[:, None] * inv[None, :]
    emb = np.concatenate([freqs, freqs], 1)
    return np.cos(emb).T.astype(np.float32), np.sin(emb).T.astype(np.float32)


def _attnb_masks(parity, W=256):
    t0 = parity * 1024
    gpen = np.zeros((128, 32), np.float32)
    gpen[:] = -1.0e30
    ctri = np.zeros((2, 5, 128, 512), np.float32)
    lmask = np.zeros((2, 5, 128, 512), np.float32)
    for s in range(2):
        G0 = t0 + 512 * s
        n_main = 7 if s == 0 else 11
        for kc in range(n_main):
            if (kc + 1) * 128 <= G0:
                gpen[:, s * 16 + kc] = 0.0
        for j in range(5):
            jj = 4 * s + j
            kk = np.arange(128)
            gkey = t0 + (jj - 1) * 128 + kk
            qq = np.arange(512)
            gq = G0 + qq
            valid_k = gkey >= 0
            c = (gkey[:, None] <= gq[None, :]) & valid_k[:, None]
            ctri[s, j] = c.astype(np.float32)
            blk = gq // W
            k_start = np.maximum(0, blk * W - W // 2)
            upper = gq - (blk * W - k_start)
            l = ((gkey[:, None] >= k_start[None, :])
                 & (gkey[:, None] <= upper[None, :]) & valid_k[:, None])
            lmask[s, j] = l.astype(np.float32)
    def pack(m):
        out = np.zeros((128, 2 * 5 * 512), np.float32)
        for s in range(2):
            for j in range(5):
                out[:, (s * 5 + j) * 512:(s * 5 + j + 1) * 512] = m[s, j]
        return out.astype(ml_dtypes.bfloat16)
    return gpen, pack(ctri), pack(lmask)



class KB:
    def __init__(self, nc, n_sub=4, taps=(), private_shared=False):
        self.nc = nc
        self.P = Prog()
        self.n_sub = n_sub
        self.private_shared = private_shared
        self.taps = set(taps)
        self.t = {}
        self.s = {}
        self.ps = []
        self.r = {}
        self.tap_stores = []

    def R(self, name):
        if name not in self.r:
            self.r[name] = Res(name)
        return self.r[name]

    def sb(self, name, shape, dtype):
        ap = self.nc.alloc_sbuf_tensor("sb_" + name, shape, dtype).ap()
        self.s[name] = ap
        return ap

    def load(self, dst, src, slot, tag, eng="pool", reads=(), deps=(),
             writes_extra=()):
        return self.P.op(eng, (lambda d, s: lambda g: g.dma_start(out=d, in_=s))(dst, src),
                         reads=list(reads),
                         writes=[self.R(slot)] + list(writes_extra), deps=deps,
                         dma=True, tag=tag, slot=slot)

    def mmg(self, ps_ap, pairs, reads, wres, tag, accumulate=False):
        def fn(t):
            ret = None
            n = len(pairs)
            for i, (l, rhs) in enumerate(pairs):
                ret = t.matmul(ps_ap, l, rhs,
                               start=(i == 0 and not accumulate),
                               stop=(i == n - 1), skip_group_check=True)
            return ret
        return self.P.op("pe", fn, reads=list(reads), writes=[wres], tag=tag)

    def dve(self, fn, reads, writes, tag):
        return self.P.op("dve", fn, reads=list(reads), writes=list(writes), tag=tag)

    def act(self, fn, reads, writes, tag):
        return self.P.op("act", fn, reads=list(reads), writes=list(writes), tag=tag)

    def tap(self, name, src_ap, shape, dtype, reads):
        if name not in self.taps:
            return
        d = self.nc.dram_tensor("tap_" + name, shape, dtype,
                                kind="ExternalOutput").ap()
        o = self.P.op("sp", (lambda dd, ss: lambda g: g.dma_start(out=dd, in_=ss))(d, src_ap),
                      reads=list(reads), dma=True, tag="tap_" + name,
                      slot="tap_" + name)
        self.tap_stores.append(o)

    def declare(self):
        nc = self.nc
        T = self.t
        def din(name, shape, dtype=F32):
            T[name] = nc.dram_tensor(name, shape, dtype, kind="ExternalInput").ap()
        din("x", [2048, 1024])
        din("wqkvuT", [2048, 8192])
        din("woutT", [2048, 2048])
        din("wq", [2048, 2048])
        din("wk", [2048, 512])
        din("wv", [2048, 512])
        din("wo", [2048, 2048])
        din("w0T", [2048, 8192])
        din("w1T", [8192, 2048])
        din("b0c", [128, 64])
        din("b1c", [128, 16])
        din("slope_row", [1, 16])
        din("iota_row", [1, 256])
        din("kd_col", [128, 2])
        din("bd_cols", [128, 4])
        din("idxt", [128, 512])
        din("rt", [128, 128])
        din("cosf", [128, 1024])
        din("sinf", [128, 1024])
        din("isodd", [128, 1])
        din("pars", [1, 8], I32)
        din("cbias", [1, 4])
        din("gpen", [128, 32])
        din("ctri", [128, 5120], BF16)
        din("lmask", [128, 5120], BF16)
        T["out"] = nc.dram_tensor("out", [2048, 1024], F32,
                                  kind="ExternalOutput").ap()
        sp_ = "Local" if self.private_shared else "Shared"
        T["sha"] = nc.dram_tensor("sha", [2, 2048, 128], F32, kind="Internal",
                                  addr_space=sp_).ap()
        T["shb_k"] = nc.dram_tensor("shb_k", [2, 512, 1024], BF16,
                                    kind="Internal", addr_space=sp_).ap()
        T["shb_v"] = nc.dram_tensor("shb_v", [2, 1024, 512], BF16,
                                    kind="Internal", addr_space=sp_).ap()
        T["shf"] = nc.dram_tensor("shf", [2, 2, 8], I32, kind="Internal",
                                  addr_space=sp_).ap()

        self.sb("x", [128, 16 * 1024], F32)
        self.sb("y", [128, 16 * 1024], BF16)
        self.arena = self.sb("arena", [128, 86 * 1024], mybir.dt.uint8)
        self.sb("rows", [1, 2048], F32)
        for i in range(3):
            self.sb(f"wst{i}", [128, 2048], BF16)
        self.sb("iota_row", [1, 256], F32)
        self.sb("kd_col", [128, 2], F32)
        self.sb("bd_cols", [128, 4], F32)
        self.sb("idxt", [128, 512], F32)
        self.sb("rt", [128, 128], BF16)
        self.sb("isodd", [128, 1], F32)
        self.sb("pars", [1, 8], I32)
        self.sb("cbias", [1, 4], F32)
        self.sb("gpen", [128, 32], F32)
        self.sb("slope_row", [1, 16], F32)
        self.sb("negs_row", [1, 16], F32)
        self.sb("ones_cb", [128, 1], BF16)
        self.sb("ones_rb", [1, 128], BF16)
        self.sb("ones_rf", [1, 128], F32)
        self.sb("b0c", [128, 64], F32)
        self.sb("b1c", [128, 16], F32)
        self.sb("srow_bf", [1, 256], BF16)
        self.sb("scol", [128, 8], F32)
        for i in range(8):
            self.ps.append(nc.alloc_psum_tensor(f"ps{i}", [128, 512], F32).ap())
        self.psr = [self.R(f"ps{i}") for i in range(8)]

    def carve(self, name, off_bytes, shape, dtype):
        nbytes = {F32: 4, BF16: 2, I32: 4}[dtype]
        assert off_bytes % 4 == 0
        assert off_bytes + shape[1] * nbytes <= 86 * 1024, name
        base = self.s["arena"]
        ap = base.tensor.ap().bitcast(dtype)[0:shape[0],
                                             off_bytes // nbytes:
                                             off_bytes // nbytes + shape[1]]
        self.s[name] = ap
        return ap

    def emit_ln(self, idx):
        P, s, ps = self.P, self.s, self.ps
        x, y = s["x"], s["y"]
        rows = s["rows"]
        tmpf, r_tmp = self.carve("ln_tmp", 0, [128, 1024], F32)
        sqt, r_sq = self.carve("ln_sq", 4096, [128, 1024], BF16)
        sq_mean, r_sqm = self.carve("ln_sqm", 6144, [1, 1024], F32)
        rx = [self.R(f"x{c}") for c in range(16)]
        ry = [self.R(f"y{c}") for c in range(16)]
        r_rows = self.R("rows")
        for c in range(16):
            xc = x[:, c * 1024:(c + 1) * 1024]
            yc = y[:, c * 1024:(c + 1) * 1024]
            self.dve((lambda a, b: lambda v: v.tensor_copy(a, b))(yc, xc),
                     [rx[c]], [ry[c]], f"ln{idx}.cp{c}")
            self.act((lambda a, b: lambda sc: sc.activation(a, b, AF.Square))(sqt, yc),
                     [ry[c]], r_sq, f"ln{idx}.sq{c}")
            for h in range(2):
                ysl = yc[:, h * 512:(h + 1) * 512]
                ssl = sqt[:, h * 512:(h + 1) * 512]
                self.mmg(ps[4 + h][0:1, :], [(s["ones_cb"], ysl)],
                         [ry[c], self.R("ones")], self.psr[4 + h],
                         f"ln{idx}.s1.{c}.{h}", accumulate=(c != 0))
                self.mmg(ps[6 + h][0:1, :], [(s["ones_cb"], ssl)],
                         r_sq + [self.R("ones")], self.psr[6 + h],
                         f"ln{idx}.s2.{c}.{h}", accumulate=(c != 0))
        mean = rows[:, 0:1024]
        var = rows[:, 1024:2048]
        for h in range(2):
            self.dve((lambda a, b: lambda v: v.tensor_scalar_mul(a, b, 1.0 / 2048.0))(
                mean[:, h * 512:(h + 1) * 512], ps[4 + h][0:1, :]),
                [self.psr[4 + h]], [r_rows], f"ln{idx}.mean{h}")
            self.dve((lambda a, b: lambda v: v.tensor_scalar_mul(a, b, 1.0 / 2048.0))(
                var[:, h * 512:(h + 1) * 512], ps[6 + h][0:1, :]),
                [self.psr[6 + h]], [r_rows], f"ln{idx}.msq{h}")
        self.act((lambda a, b: lambda sc: sc.activation(a, b, AF.Square))(sq_mean, mean),
                 [r_rows], r_sqm, f"ln{idx}.m2")
        self.dve((lambda a, b, c2: lambda v: v.tensor_sub(a, b, c2))(var, var, sq_mean),
                 [r_rows] + r_sqm, [r_rows], f"ln{idx}.var")
        self.act((lambda a, b, bb_=s["cbias"][0:1, 0:1]: lambda sc: sc.activation(
            a, b, AF.Ln, bias=bb_))(var, var),
                 [r_rows, self.R("cbias")], [r_rows], f"ln{idx}.ln")
        self.act((lambda a, b: lambda sc: sc.activation(a, b, AF.Exp, scale=-0.5))(var, var),
                 [r_rows], [r_rows], f"ln{idx}.exp")
        self.dve((lambda a, b, c2: lambda v: v.tensor_mul(a, b, c2))(mean, mean, var),
                 [r_rows], [r_rows], f"ln{idx}.mr")
        for h in range(2):
            self.mmg(ps[0 + h], [(s["ones_rf"], var[:, h * 512:(h + 1) * 512])],
                     [r_rows, self.R("ones")], self.psr[0 + h], f"ln{idx}.bR{h}")
            self.mmg(ps[2 + h], [(s["ones_rf"], mean[:, h * 512:(h + 1) * 512])],
                     [r_rows, self.R("ones")], self.psr[2 + h], f"ln{idx}.bM{h}")
        for c in range(16):
            for h in range(2):
                xsl = x[:, c * 1024 + h * 512: c * 1024 + h * 512 + 512]
                ysl = y[:, c * 1024 + h * 512: c * 1024 + h * 512 + 512]
                tsl = tmpf[:, h * 512:(h + 1) * 512]
                self.dve((lambda a, b, c2: lambda v: v.tensor_mul(a, b, c2))(
                    tsl, xsl, ps[0 + h]),
                    [rx[c], self.psr[0 + h]], r_tmp, f"ln{idx}.t{c}.{h}")
                self.dve((lambda a, b, c2: lambda v: v.tensor_sub(a, b, c2))(
                    ysl, tsl, ps[2 + h]),
                    r_tmp + [self.psr[2 + h]], [ry[c]], f"ln{idx}.y{c}.{h}")

    def emit_ffn(self, idx):
        P, s, ps, t = self.P, self.s, self.ps, self.t
        y, x = s["y"], s["x"]
        dbf, _dpg = self.carve("ffn_d", 0, [128, 16 * 1024], BF16)
        ebf, _epg = self.carve("ffn_e", 32 * 1024, [128, 16 * 1024], BF16)

        def dpg(n, h):
            return [self.R(f"pg{n}")]

        def epg(hc):
            return [self.R(f"pg{16 + hc}")]
        ry = [self.R(f"y{c}") for c in range(16)]
        rx = [self.R(f"x{c}") for c in range(16)]
        wst = [s[f"wst{i}"] for i in range(3)]
        r_w = [self.R(f"wst{i}") for i in range(3)]
        for p in range(4):
            for ncnk in range(16):
                gn = p * 16 + ncnk
                bi = gn % 3
                src = t["w0T"][:, gn * 128:(gn + 1) * 128].rearrange(
                    "(kc q) n -> q kc n", q=128)
                self.load(wst[bi].rearrange("q (kc n) -> q kc n", kc=16), src,
                          f"wst{bi}", f"f{idx}.w0.{gn}")
                for h in range(2):
                    pi = (ncnk * 2 + h) % 4
                    pairs = [(wst[bi][:, kc * 128:(kc + 1) * 128],
                              y[:, kc * 1024 + h * 512: kc * 1024 + h * 512 + 512])
                             for kc in range(16)]
                    self.mmg(ps[pi], pairs, [r_w[bi]] + ry, self.psr[pi],
                             f"f{idx}.dmm.{gn}.{h}")
                    dst = dbf[:, ncnk * 1024 + h * 512: ncnk * 1024 + h * 512 + 512]
                    self.act((lambda a, b, bc: lambda sc: sc.activation(
                        a, b, AF.Gelu, bias=bc))(dst, ps[pi], s["b0c"][:, gn:gn + 1]),
                        [self.psr[pi], self.R("b0c")], dpg(ncnk, h),
                        f"f{idx}.gelu.{gn}.{h}")
            for hc in range(16):
                bi = hc % 3
                src = t["w1T"][p * 2048:(p + 1) * 2048,
                               hc * 128:(hc + 1) * 128].rearrange(
                    "(kc q) n -> q kc n", q=128)
                self.load(wst[bi].rearrange("q (kc n) -> q kc n", kc=16), src,
                          f"wst{bi}", f"f{idx}.w1.{p}.{hc}")
                for h in range(2):
                    pi = 4 + (hc * 2 + h) % 2
                    pairs = [(wst[bi][:, n * 128:(n + 1) * 128],
                              dbf[:, n * 1024 + h * 512: n * 1024 + h * 512 + 512])
                             for n in range(16)]
                    self.mmg(ps[pi], pairs,
                             [r_w[bi]] + [self.R(f"pg{n}") for n in range(16)],
                             self.psr[pi], f"f{idx}.emm.{p}.{hc}.{h}")
                    dst = ebf[:, hc * 1024 + h * 512: hc * 1024 + h * 512 + 512]
                    if p == 0 and h == 0:
                        pass
                    if p == 0:
                        fn = (lambda a, b: lambda v: v.tensor_copy(a, b))(dst, ps[pi])
                    else:
                        fn = (lambda a, b: lambda v: v.tensor_add(a, a, b))(dst, ps[pi])
                    self.dve(fn, [self.psr[pi]] + (epg(hc) if p else []),
                             epg(hc), f"f{idx}.eacc.{p}.{hc}.{h}")
        for hc in range(16):
            xc = x[:, hc * 1024:(hc + 1) * 1024]
            ec = ebf[:, hc * 1024:(hc + 1) * 1024]
            self.dve((lambda a, b, bc: lambda v: v.scalar_tensor_tensor(
                a, b, bc, a, ALU.add, ALU.add))(xc, ec, s["b1c"][:, hc:hc + 1]),
                epg(hc) + [self.R("b1c"), rx[hc]], [rx[hc]],
                f"f{idx}.res.{hc}")

    PAGE = 2048

    def carve(self, name, off_bytes, shape, dtype):
        nbytes = {F32: 4, BF16: 2, I32: 4}[dtype]
        end = off_bytes + shape[1] * nbytes
        assert off_bytes % 4 == 0 and end <= 86 * 1024, name
        base = self.s["arena"].tensor.ap().bitcast(dtype)
        ap = base[0:shape[0], off_bytes // nbytes: off_bytes // nbytes + shape[1]]
        pages = [self.R(f"pg{i}") for i in
                 range(off_bytes // self.PAGE, (end + self.PAGE - 1) // self.PAGE)]
        self.s[name] = ap
        return ap, pages

    def emit_setup(self):
        s, t = self.s, self.t
        x3 = t["x"].rearrange("(c p) t -> p c t", p=128)
        self.load(s["x"].rearrange("p (c t) -> p c t", c=16), x3, "x_in",
                  "ld_x", eng="sp")
        xld = self.R("x_in")
        for c in range(16):
            self.R(f"x{c}").writer = xld.writer
        for nm in ("b0c", "b1c", "slope_row", "iota_row", "kd_col", "bd_cols",
                   "idxt", "isodd", "pars", "gpen", "cbias"):
            self.load(s[nm], t[nm], nm, "ld_" + nm, eng="sp")
        self.load(s["rt"], t["rt"], "rt", "ld_rt", eng="pool")
        def ms(g):
            g.memset(s["ones_cb"], 1.0)
            g.memset(s["ones_rb"], 1.0)
            return g.memset(s["ones_rf"], 1.0)
        self.P.op("pool", ms, writes=[self.R("ones")], tag="ones")
        self.dve((lambda a, b: lambda v: v.tensor_scalar_mul(a, b, -1.0))(
            s["negs_row"], s["slope_row"]),
            [self.R("slope_row")], [self.R("negs")], "negs")

    def _negcol(self, h, tag, bank=7):
        s, ps = self.s, self.ps
        self.mmg(ps[bank][:, 0:1], [(s["ones_rf"], s["negs_row"][0:1, h:h + 1])],
                 [self.R("negs"), self.R("ones")], self.psr[bank], tag + ".ncb")
        self.dve((lambda a, b: lambda v: v.tensor_copy(a, b))(
            s["scol"][:, 0:1], ps[bank][:, 0:1]),
            [self.psr[bank]], [self.R("scol")], tag + ".ncc")

    def emit_attn_a(self):
        P, s, ps, t = self.P, self.s, self.ps, self.t
        y, x = s["y"], s["x"]
        ry = [self.R(f"y{c}") for c in range(16)]
        rx = [self.R(f"x{c}") for c in range(16)]
        wst = [s[f"wst{i}"] for i in range(3)]
        r_w = [self.R(f"wst{i}") for i in range(3)]
        kvpre, r_kvpre = self.carve("kvpre", 0, [128, 16 * 384], BF16)
        kvin_b, r_kvinb = self.carve("kvin_b", 12288, [128, 2048], BF16)
        kvf, r_kvf = self.carve("kvf", 16384, [128, 2048], F32)
        kvin_f, r_kvinf = self.carve("kvin_f", 24576, [128, 2048], F32)
        slab0, r_sl0 = self.carve("slab0", 32768, [128, 8192], BF16)
        slab1, r_sl1 = self.carve("slab1", 49152, [128, 8192], BF16)
        slabs = [(slab0, r_sl0), (slab1, r_sl1)]
        ktm_g, r_ktm = self.carve("ktm_g", 65536, [128, 4096], BF16)
        vtm_e, r_vtme = self.carve("vtm_e", 73728, [128, 4096], BF16)
        r_scol = self.R("scol")

        def kdc_prep(h, tag):
            self.act((lambda a, b, sc_: lambda sc: sc.activation(
                a, b, AF.Exp, scale=sc_))(s["scol"][:, 5:7], s["kd_col"],
                                          s["scol"][:, 0:1]),
                [self.R("kd_col"), r_scol], [r_scol], tag + ".kdc")

        def bdp_prep(h, tag):
            self.act((lambda a, b, sc_: lambda sc: sc.activation(
                a, b, AF.Exp, scale=sc_))(s["scol"][:, 1:5], s["bd_cols"],
                                          s["scol"][:, 0:1]),
                [self.R("bd_cols"), r_scol], [r_scol], tag + ".bdp")

        for g in range(4):
            for kind, coff, slot in (("v", 4096, 0), ("k", 2048, 1)):
                sl_ap, sl_r = slabs[slot]
                src = t["wqkvuT"][:, coff + g * 512: coff + (g + 1) * 512] \
                    .rearrange("(fc p) n -> p fc n", p=128)
                self.load(sl_ap.rearrange("p (fc n) -> p fc n", fc=16), src,
                          f"aslab{slot}", f"aa.ex.w{kind}.{g}",
                          writes_extra=sl_r)
            for tc in range(8):
                for kind, slot, dstap, dstres in (
                        ("v", 0, vtm_e, r_vtme), ("k", 1, ktm_g, r_ktm)):
                    sl_ap, sl_r = slabs[slot]
                    pi = (tc * 2 + slot) % 4
                    pairs = [(y[:, fc * 1024 + tc * 128: fc * 1024 + tc * 128 + 128],
                              sl_ap[:, fc * 512:(fc + 1) * 512])
                             for fc in range(16)]
                    self.mmg(ps[pi], pairs, sl_r + ry, self.psr[pi],
                             f"aa.ex.{kind}mm.{g}.{tc}")
                    dsl = dstap[:, tc * 512:(tc + 1) * 512]
                    if kind == "v":
                        self.dve((lambda a, b: lambda v: v.tensor_copy(a, b))(
                            dsl, ps[pi]), [self.psr[pi]], dstres,
                            f"aa.ex.vcp.{g}.{tc}")
                    else:
                        self.act((lambda a, b: lambda sc: sc.activation(
                            a, b, AF.Silu))(dsl, ps[pi]), [self.psr[pi]],
                            dstres, f"aa.ex.ksl.{g}.{tc}")
            for hh in range(4):
                h = g * 4 + hh
                self._negcol(h, f"aa.ex.h{h}")
                kdc_prep(h, f"aa.ex.h{h}")
                bdp_prep(h, f"aa.ex.h{h}")
                for tc in range(8):
                    sl = ktm_g[:, tc * 512 + hh * 128: tc * 512 + hh * 128 + 128]
                    self.dve((lambda a, c2: lambda v: v.tensor_scalar_mul(
                        a, a, c2))(sl, s["scol"][:, 5 + tc % 2: 6 + tc % 2]),
                        [r_scol] + r_ktm, r_ktm, f"aa.ex.kkd.{h}.{tc}")
                kvf_h = kvf[:, h * 128:(h + 1) * 128]
                for B in range(4):
                    pairs = [(ktm_g[:, (2 * B + j) * 512 + hh * 128:
                                    (2 * B + j) * 512 + hh * 128 + 128],
                              vtm_e[:, (2 * B + j) * 512 + hh * 128:
                                    (2 * B + j) * 512 + hh * 128 + 128])
                             for j in range(2)]
                    self.mmg(ps[6][:, 0:128], pairs, r_ktm + r_vtme,
                             self.psr[6], f"aa.ex.upd.{h}.{B}")
                    if B == 0:
                        fn = (lambda a, b: lambda v: v.tensor_copy(a, b))(
                            kvf_h, ps[6][:, 0:128])
                    else:
                        fn = (lambda a, b, sc_: lambda v: v.scalar_tensor_tensor(
                            a, a, sc_, b, ALU.mult, ALU.add))(
                            kvf_h, ps[6][:, 0:128], s["scol"][:, 2:3])
                    self.dve(fn, [self.psr[6], r_scol] + r_kvf, r_kvf,
                             f"aa.ex.kv.{h}.{B}")
                    if B < 3:
                        dsl = kvpre[:, h * 384 + B * 128: h * 384 + B * 128 + 128]
                        self.dve((lambda a, b: lambda v: v.tensor_copy(a, b))(
                            dsl, kvf_h), r_kvf, r_kvpre, f"aa.ex.pre.{h}.{B}")

        self._exchange_a(r_kvf, r_kvinf)
        self.dve((lambda a, b, c2: lambda v: v.tensor_scalar_mul(a, b, c2))(
            kvin_b, kvin_f, s["isodd"][:, 0:1]),
            r_kvinf + [self.R("isodd")], r_kvinb, "aa.kvinbf")

        Oar, r_O = self.carve("aaO", 49152, [128, 8192], BF16)
        ddt_t, r_ddt = self.carve("aaddt", 16384, [128, 512], F32)
        qd_t, r_qd = self.carve("aaqd", 18432, [128, 256], BF16)
        shat = [self.carve("aash0", 18944, [128, 256], BF16),
                self.carve("aash1", 19456, [128, 256], BF16)]
        shatw, r_shatw = self.carve("aashw", 18944, [128, 512], BF16)
        qqd_t, r_qqd = self.carve("aaqqd", 19968, [128, 256], BF16)
        qgd_t, r_qgd = self.carve("aaqgd", 20480, [128, 256], BF16)
        sq_t, r_sqt = self.carve("aasq", 20992, [128, 256], BF16)
        vtm, r_vtm = self.carve("aavtm", 65536, [128, 8192], BF16)
        qk_t = [self.carve("aaq0", 81920, [128, 512], BF16),
                self.carve("aaq1", 82944, [128, 512], BF16),
                self.carve("aak0", 83968, [128, 512], BF16),
                self.carve("aak1", 84992, [128, 512], BF16)]

        for sub in range(2):
            S = sub * 512
            for vs in range(4):
                sl_ap, sl_r = slabs[vs % 2]
                src = t["wqkvuT"][:, 4096 + vs * 512: 4096 + (vs + 1) * 512] \
                    .rearrange("(fc p) n -> p fc n", p=128)
                self.load(sl_ap.rearrange("p (fc n) -> p fc n", fc=16), src,
                          f"aslab{vs % 2}", f"aa.m{sub}.wv.{vs}",
                          writes_extra=sl_r)
                for tc in range(4):
                    pi = (vs * 4 + tc) % 4
                    pairs = [(y[:, fc * 1024 + S + tc * 128:
                                fc * 1024 + S + tc * 128 + 128],
                              sl_ap[:, fc * 512:(fc + 1) * 512])
                             for fc in range(16)]
                    self.mmg(ps[pi], pairs, sl_r + ry, self.psr[pi],
                             f"aa.m{sub}.vmm.{vs}.{tc}")
                    dsl = vtm[:, tc * 2048 + vs * 512: tc * 2048 + (vs + 1) * 512]
                    self.dve((lambda a, b: lambda v: v.tensor_copy(a, b))(
                        dsl, ps[pi]), [self.psr[pi]], r_vtm,
                        f"aa.m{sub}.vcp.{vs}.{tc}")
            for h in range(16):
                qa, r_qa = qk_t[h % 2]
                ka, r_ka = qk_t[2 + h % 2]
                for kind, coff, dst, dres in (("q", 0, qa, r_qa),
                                              ("k", 2048, ka, r_ka)):
                    bi = h % 3
                    src = t["wqkvuT"][:, coff + h * 128: coff + (h + 1) * 128] \
                        .rearrange("(kc q) n -> q kc n", q=128)
                    self.load(wst[bi].rearrange("q (kc n) -> q kc n", kc=16),
                              src, f"wst{bi}", f"aa.m{sub}.w{kind}.{h}")
                    pi = 4 + (0 if kind == "q" else 1)
                    pairs = [(wst[bi][:, kc * 128:(kc + 1) * 128],
                              y[:, kc * 1024 + S: kc * 1024 + S + 512])
                             for kc in range(16)]
                    self.mmg(ps[pi], pairs, [r_w[bi]] + ry, self.psr[pi],
                             f"aa.m{sub}.{kind}mm.{h}")
                    self.act((lambda a, b: lambda sc: sc.activation(
                        a, b, AF.Silu))(dst, ps[pi]), [self.psr[pi]], dres,
                        f"aa.m{sub}.{kind}sl.{h}")
                self._negcol(h, f"aa.m{sub}.h{h}", bank=6)
                bdp_prep(h, f"aa.m{sub}.h{h}")
                self.act((lambda a, b, sc_: lambda sc: sc.activation(
                    a, b, AF.Exp, scale=sc_))(ddt_t, s["idxt"], s["scol"][:, 0:1]),
                    [self.R("idxt"), r_scol], r_ddt, f"aa.m{sub}.ddt.{h}")
                self.act((lambda a, b, sc_: lambda sc: sc.activation(
                    a, b, AF.Exp, scale=sc_))(s["srow_bf"][0:1, 0:256],
                                              s["iota_row"], s["negs_row"][0:1, h:h + 1]),
                    [self.R("iota_row"), self.R("negs")], [self.R("srow_bf")],
                    f"aa.m{sub}.qdr.{h}")
                self.mmg(ps[6][:, 0:256], [(s["ones_rb"], s["srow_bf"][0:1, 0:256])],
                         [self.R("srow_bf"), self.R("ones")], self.psr[6],
                         f"aa.m{sub}.qdb.{h}")
                self.dve((lambda a, b: lambda v: v.tensor_copy(a, b))(
                    qd_t, ps[6][:, 0:256]), [self.psr[6]], r_qd,
                    f"aa.m{sub}.qdc.{h}")
                for blk in range(2):
                    B = sub * 2 + blk
                    stp = ps[blk % 2]
                    qb = qa[:, blk * 256: blk * 256 + 256]
                    def st_fn(pp=stp, k0=ka[:, blk * 256: blk * 256 + 128],
                              k1=ka[:, blk * 256 + 128: blk * 256 + 256], qq=qb):
                        def f(te):
                            te.matmul(pp[:, 0:256], k0, qq, start=True,
                                      stop=True, skip_group_check=True)
                            return te.matmul(pp[:, 256:512], k1, qq, start=False,
                                             stop=True, skip_group_check=True)
                        return f
                    self.P.op("pe", st_fn(), reads=r_ka + r_qa,
                              writes=[self.psr[blk % 2]], tag=f"aa.st.{sub}.{h}.{blk}")
                    self.dve((lambda a, b, c2: lambda v: v.tensor_mul(a, b, c2))(
                        shatw, stp[:, 0:512], ddt_t),
                        [self.psr[blk % 2]] + r_ddt, r_shatw,
                        f"aa.shat.{sub}.{h}.{blk}")
                    self.dve((lambda a, b, c2: lambda v: v.tensor_mul(a, b, c2))(
                        qqd_t, qb, qd_t), r_qa + r_qd, r_qqd,
                        f"aa.qqd.{sub}.{h}.{blk}")
                    self.dve((lambda a, b, c2: lambda v: v.tensor_scalar_mul(
                        a, b, c2))(qgd_t, qqd_t, s["scol"][:, 1 + B: 2 + B]),
                        r_qqd + [r_scol], r_qgd, f"aa.qgd.{sub}.{h}.{blk}")
                    opi = 2 + (h * 2 + blk) % 2
                    pairs = [(vtm[:, (2 * blk + j) * 2048 + h * 128:
                                  (2 * blk + j) * 2048 + h * 128 + 128],
                              shat[j][0]) for j in range(2)]
                    reads = r_vtm + shat[0][1] + shat[1][1] + r_qgd + r_qqd \
                        + r_kvinb + r_kvpre
                    if B > 0:
                        pairs.append((kvpre[:, h * 384 + (B - 1) * 128:
                                           h * 384 + B * 128], qqd_t))
                    pairs.append((kvin_b[:, h * 128:(h + 1) * 128], qgd_t))
                    self.mmg(ps[opi][:, 0:256], pairs, reads, self.psr[opi],
                             f"aa.omm.{sub}.{h}.{blk}")
                    osl = Oar[:, h * 512 + blk * 256: h * 512 + blk * 256 + 256]
                    self.dve((lambda a, b: lambda v: v.tensor_copy(a, b))(
                        osl, ps[opi][:, 0:256]), [self.psr[opi]], r_O,
                        f"aa.ocp.{sub}.{h}.{blk}")
                    self.act((lambda a, b: lambda sc: sc.activation(
                        a, b, AF.Square))(sq_t, osl), r_O, r_sqt,
                        f"aa.osq.{sub}.{h}.{blk}")
                    self.mmg(ps[7][0:1, blk * 256: blk * 256 + 256],
                             [(s["ones_cb"], sq_t)], r_sqt + [self.R("ones")],
                             self.psr[7], f"aa.ost.{sub}.{h}.{blk}",
                             accumulate=not (h == 0 and blk == 0))
            rows = s["rows"]
            r_rows = self.R("rows")
            self.act((lambda a, b, bb_=s["cbias"][0:1, 1:2]: lambda sc: sc.activation(
                a, b, AF.Ln, scale=1.0 / 2048.0, bias=bb_))(rows[:, 0:512],
                                                            ps[7][0:1, :]),
                [self.psr[7], self.R("cbias")], [r_rows], f"aa.rln.{sub}")
            self.act((lambda a, b: lambda sc: sc.activation(
                a, b, AF.Exp, scale=-0.5))(rows[:, 0:512], rows[:, 0:512]),
                [r_rows], [r_rows], f"aa.rex.{sub}")
            self.mmg(ps[6], [(s["ones_rf"], rows[0:1, 0:512])],
                     [r_rows, self.R("ones")], self.psr[6], f"aa.rbc.{sub}")
            for h in range(16):
                bi = h % 3
                src = t["wqkvuT"][:, 6144 + h * 128: 6144 + (h + 1) * 128] \
                    .rearrange("(kc q) n -> q kc n", q=128)
                self.load(wst[bi].rearrange("q (kc n) -> q kc n", kc=16), src,
                          f"wst{bi}", f"aa.m{sub}.wu.{h}")
                pi = 4 + h % 2
                pairs = [(wst[bi][:, kc * 128:(kc + 1) * 128],
                          y[:, kc * 1024 + S: kc * 1024 + S + 512])
                         for kc in range(16)]
                self.mmg(ps[pi], pairs, [r_w[bi]] + ry, self.psr[pi],
                         f"aa.umm.{sub}.{h}")
                osl = Oar[:, h * 512:(h + 1) * 512]
                self.dve((lambda a, b: lambda v: v.tensor_mul(a, a, b))(
                    osl, ps[pi]), [self.psr[pi]] + r_O, r_O,
                    f"aa.ou.{sub}.{h}")
                self.dve((lambda a, b: lambda v: v.tensor_mul(a, a, b))(
                    osl, ps[6]), [self.psr[6]] + r_O, r_O,
                    f"aa.our.{sub}.{h}")
            for oc in range(16):
                bi = oc % 3
                src = t["woutT"][:, oc * 128:(oc + 1) * 128].rearrange(
                    "(kc q) n -> q kc n", q=128)
                self.load(wst[bi].rearrange("q (kc n) -> q kc n", kc=16), src,
                          f"wst{bi}", f"aa.m{sub}.wo.{oc}")
                pi = 4 + oc % 2
                pairs = [(wst[bi][:, hc * 128:(hc + 1) * 128],
                          Oar[:, hc * 512:(hc + 1) * 512]) for hc in range(16)]
                self.mmg(ps[pi], pairs, [r_w[bi]] + r_O, self.psr[pi],
                         f"aa.wout.{sub}.{oc}")
                xsl = x[:, oc * 1024 + S: oc * 1024 + S + 512]
                self.dve((lambda a, b: lambda v: v.tensor_add(a, a, b))(
                    xsl, ps[pi]), [self.psr[pi], rx[oc]], [rx[oc]],
                    f"aa.res.{sub}.{oc}")

    def _exchange_a(self, r_kvf, r_kvinf):
        s, t = self.s, self.t
        kvf, _ = self.carve("kvf2", 16384, [128, 2048], F32)
        kvin_f, _ = self.carve("kvin_f2", 24576, [128, 2048], F32)
        sems = self.xsems["a"]

        def blob(g):
            sd, sp_ = sems
            src3 = kvf.rearrange("d (h e) -> d h e", h=16)
            dst3 = kvin_f.rearrange("d (h e) -> d h e", h=16)
            with g.register("rparA") as rpar, g.register("rflagA") as rflag, \
                    g.register("rnonA") as rnon, g.register("rcntA") as rcnt:
                g.reg_load(rpar, s["pars"][0:1, 0:1])
                g.reg_load(rnon, s["pars"][0:1, 1:2])
                with g.If_eq(rpar, 0):
                    g.dma_start(out=t["sha"][0].rearrange("(h d) e -> d h e", d=128),
                                in_=src3).then_inc(sd, 16)
                    g.wait_ge(sd, 16)
                    g.dma_start(out=t["shf"][0, 0:1, :], in_=s["pars"]).then_inc(sd, 16)
                    g.wait_ge(sd, 32)
                with g.Else():
                    g.dma_start(out=t["sha"][1].rearrange("(h d) e -> d h e", d=128),
                                in_=src3).then_inc(sd, 16)
                    g.wait_ge(sd, 16)
                    g.dma_start(out=t["shf"][0, 1:2, :], in_=s["pars"]).then_inc(sd, 16)
                    g.wait_ge(sd, 32)
                g.reg_mov(rcnt, 0)
                g.reg_mov(rflag, 0)
                with g.If_eq(rpar, 0):
                    with g.While(lambda: g.snap(rflag) != g.snap(rnon)):
                        g.reg_add(rcnt, rcnt, 16)
                        g.dma_start(out=s["pars"][0:1, 4:8],
                                    in_=t["shf"][0, 1:2, 0:4]).then_inc(sp_, 16)
                        g.wait_ge(sp_, rcnt)
                        g.reg_load(rflag, s["pars"][0:1, 5:6])
                    g.dma_start(out=dst3,
                                in_=t["sha"][1].rearrange("(h d) e -> d h e", d=128)
                                ).then_inc(sd, 16)
                    g.wait_ge(sd, 48)
                with g.Else():
                    with g.While(lambda: g.snap(rflag) != g.snap(rnon)):
                        g.reg_add(rcnt, rcnt, 16)
                        g.dma_start(out=s["pars"][0:1, 4:8],
                                    in_=t["shf"][0, 0:1, 0:4]).then_inc(sp_, 16)
                        g.wait_ge(sp_, rcnt)
                        g.reg_load(rflag, s["pars"][0:1, 5:6])
                    g.dma_start(out=dst3,
                                in_=t["sha"][0].rearrange("(h d) e -> d h e", d=128)
                                ).then_inc(sd, 16)
                    g.wait_ge(sd, 48)
            return g.nop()

        self.P.op("sp", blob, reads=r_kvf + [self.R("pars")],
                  writes=r_kvinf + [self.R("pars_poll")], tag="xchgA")

    def emit_attn_b(self):
        P, s, ps, t = self.P, self.s, self.ps, self.t
        y, x = s["y"], s["x"]
        ry = [self.R(f"y{c}") for c in range(16)]
        rx = [self.R(f"x{c}") for c in range(16)]
        wst = [s[f"wst{i}"] for i in range(3)]
        r_w = [self.R(f"wst{i}") for i in range(3)]
        Kf, r_Kf = self.carve("bKf", 0, [128, 8192], BF16)
        Vf, r_Vf = self.carve("bVf", 16384, [128, 8192], BF16)
        Kl, r_Kl = self.carve("bKl", 32768, [128, 4608], BF16)
        Vl, r_Vl = self.carve("bVl", 41984, [128, 4608], BF16)
        Ob, r_Ob = self.carve("bO", 51200, [128, 8192], BF16)
        cosb, r_cos = self.carve("bcos", 67584, [128, 1024], BF16)
        sinb, r_sin = self.carve("bsin", 69632, [128, 1024], BF16)
        ctri_s, r_ctri = self.carve("bctri", 71680, [128, 2560], BF16)
        lmask_s, r_lm = self.carve("blm", 76800, [128, 2560], BF16)
        E = [self.carve("bE0", 81920, [128, 512], BF16),
             self.carve("bE1", 82944, [128, 512], BF16)]
        qh_t, r_qh = self.carve("bqh", 83968, [128, 512], BF16)
        qro = [self.carve("bqro0", 84992, [128, 512], BF16),
               self.carve("bqro1", 86016, [128, 512], BF16)]
        t1_t, r_t1 = self.carve("bt1", 87040, [128, 512], BF16)
        sqq_t, r_sqq = self.carve("bsqq", 87040, [128, 512], BF16)
        Kown, r_Kown = self.carve("bKown", 0, [128, 4096], BF16)
        Vown, r_Vown = self.carve("bVown", 16384, [128, 4096], BF16)
        vslab, r_vsl = self.carve("bvslab", 51200, [128, 8192], BF16)
        kg_t, r_kg = self.carve("bkg", 71680, [128, 1024], BF16)
        bcols = self.sb("bcols", [128, 32], F32)
        r_bcols = [self.R("bcols0"), self.R("bcols1")]
        brow = self.sb("brow", [1, 16], F32)
        r_brow = self.R("brow")
        rows, r_rows = s["rows"], self.R("rows")
        r_scol = [self.R("scolB0"), self.R("scolB1")]

        self.load(cosb, t["cosf"], "bcos", "bb.cos", writes_extra=r_cos)
        self.load(sinb, t["sinf"], "bsin", "bb.sin", writes_extra=r_sin)

        for g in range(4):
            bi = g % 3
            src = t["wk"][:, g * 128:(g + 1) * 128].rearrange(
                "(kc q) n -> q kc n", q=128)
            self.load(wst[bi].rearrange("q (kc n) -> q kc n", kc=16), src,
                      f"wst{bi}", f"bb.wk.{g}")
            for hhalf in range(2):
                pi = hhalf
                pairs = [(wst[bi][:, kc * 128:(kc + 1) * 128],
                          y[:, kc * 1024 + hhalf * 512: kc * 1024 + hhalf * 512 + 512])
                         for kc in range(16)]
                self.mmg(ps[pi], pairs, [r_w[bi]] + ry, self.psr[pi],
                         f"bb.kmm.{g}.{hhalf}")
                ksl = kg_t[:, hhalf * 512:(hhalf + 1) * 512]
                self.dve((lambda a, b: lambda v: v.tensor_copy(a, b))(
                    ksl, ps[pi]), [self.psr[pi]], r_kg, f"bb.kcp.{g}.{hhalf}")
                self.mmg(ps[2 + hhalf], [(s["rt"], ksl)],
                         [self.R("rt")] + r_kg, self.psr[2 + hhalf],
                         f"bb.krot.{g}.{hhalf}")
                self.dve((lambda a, b, c2: lambda v: v.tensor_mul(a, b, c2))(
                    t1_t, ksl, cosb[:, hhalf * 512:(hhalf + 1) * 512]),
                    r_kg + r_cos, r_t1, f"bb.kc1.{g}.{hhalf}")
                dsl = Kown[:, g * 1024 + hhalf * 512: g * 1024 + (hhalf + 1) * 512]
                self.dve((lambda a, b, c2: lambda v: v.tensor_mul(a, b, c2))(
                    dsl, ps[2 + hhalf], sinb[:, hhalf * 512:(hhalf + 1) * 512]),
                    [self.psr[2 + hhalf]] + r_sin, r_Kown, f"bb.ks1.{g}.{hhalf}")
                self.dve((lambda a, b, c2: lambda v: v.tensor_add(a, a, c2))(
                    dsl, dsl, t1_t), r_Kown + r_t1, r_Kown,
                    f"bb.kro.{g}.{hhalf}")
        src = t["wv"].rearrange("(fc p) n -> p fc n", p=128)
        self.load(vslab.rearrange("p (fc n) -> p fc n", fc=16), src,
                  "bvslab", "bb.wv", writes_extra=r_vsl)
        for tc in range(8):
            pi = 4 + tc % 2
            pairs = [(y[:, fc * 1024 + tc * 128: fc * 1024 + tc * 128 + 128],
                      vslab[:, fc * 512:(fc + 1) * 512]) for fc in range(16)]
            self.mmg(ps[pi], pairs, r_vsl + ry, self.psr[pi], f"bb.vmm.{tc}")
            dsl = Vown[:, tc * 512:(tc + 1) * 512]
            self.dve((lambda a, b: lambda v: v.tensor_copy(a, b))(
                dsl, ps[pi]), [self.psr[pi]], r_Vown, f"bb.vcp.{tc}")

        self._exchange_b(r_Kown, r_Vown, r_Kf, r_Vf, r_Kl, r_Vl, Kf, Vf, Kl, Vl)

        for g in range(4):
            for q4 in range(4):
                ksl = Kf[:, g * 2048 + q4 * 512: g * 2048 + (q4 + 1) * 512]
                self.act((lambda a, b: lambda sc: sc.activation(
                    a, b, AF.Square))(sqq_t, ksl), r_Kf, r_sqq,
                    f"bb.ksq.{g}.{q4}")
                self.mmg(ps[6][0:1, :], [(s["ones_cb"], sqq_t)],
                         r_sqq + [self.R("ones")], self.psr[6],
                         f"bb.knm.{g}.{q4}")
                self.dve((lambda b, i=g * 4 + q4: lambda v: v.reduce_max(
                    rows[:, 1024 + i:1025 + i], b, axis=AX.X))(ps[6][0:1, :]),
                    [self.psr[6]], [r_rows], f"bb.kmx.{g}.{q4}")
        self.dve((lambda a: lambda v: v.reduce_max(
            brow[:, 0:1], a, axis=AX.X))(rows[0:1, 1024:1040]),
            [r_rows], [r_brow], "bb.kmx2")
        self.act((lambda a, b: lambda sc: sc.activation(a, b, AF.Sqrt))(
            brow[:, 1:2], brow[:, 0:1]), [r_brow], [r_brow], "bb.ksqrt")

        for st_ in range(2):
            S = st_ * 512
            self.load(ctri_s, t["ctri"][:, st_ * 2560:(st_ + 1) * 2560],
                      "bctri", f"bb.ctri.{st_}", eng="sp", writes_extra=r_ctri)
            self.load(lmask_s, t["lmask"][:, st_ * 2560:(st_ + 1) * 2560],
                      "blm", f"bb.lm.{st_}", eng="sp", writes_extra=r_lm)
            n_main = 7 if st_ == 0 else 11

            def prep(h):
                par = h % 2
                A, B = ps[2 + 2 * par], ps[3 + 2 * par]
                rA, rB = self.psr[2 + 2 * par], self.psr[3 + 2 * par]
                qro_t, r_qro = qro[par]
                bi = h % 3
                src = t["wq"][:, h * 128:(h + 1) * 128].rearrange(
                    "(kc q) n -> q kc n", q=128)
                self.load(wst[bi].rearrange("q (kc n) -> q kc n", kc=16), src,
                          f"wst{bi}", f"bb.wq.{st_}.{h}")
                pairs = [(wst[bi][:, kc * 128:(kc + 1) * 128],
                          y[:, kc * 1024 + S: kc * 1024 + S + 512])
                         for kc in range(16)]
                self.mmg(A, pairs, [r_w[bi]] + ry, rA, f"bb.qmm.{st_}.{h}")
                self.dve((lambda a, b: lambda v: v.tensor_copy(a, b))(
                    qh_t, A), [rA], r_qh, f"bb.qcp.{st_}.{h}")
                self.mmg(B, [(s["rt"], qh_t)], [self.R("rt")] + r_qh, rB,
                         f"bb.qrot.{st_}.{h}")
                self.dve((lambda a, b, c2: lambda v: v.tensor_mul(a, b, c2))(
                    t1_t, qh_t, cosb[:, S:S + 512]), r_qh + r_cos, r_t1,
                    f"bb.qc1.{st_}.{h}")
                self.dve((lambda a, b, c2: lambda v: v.tensor_mul(a, b, c2))(
                    qro_t, B, sinb[:, S:S + 512]),
                    [rB] + r_sin, r_qro, f"bb.qs1.{st_}.{h}")
                self.dve((lambda a, b, c2: lambda v: v.tensor_add(a, a, c2))(
                    qro_t, qro_t, t1_t), r_qro + r_t1, r_qro,
                    f"bb.qro.{st_}.{h}")
                self.act((lambda a, b: lambda sc: sc.activation(
                    a, b, AF.Square))(sqq_t, qro_t), r_qro, r_sqq,
                    f"bb.qsq.{st_}.{h}")
                self.mmg(A[0:1, :], [(s["ones_cb"], sqq_t)],
                         r_sqq + [self.R("ones")], rA, f"bb.qnm.{st_}.{h}")
                c0 = 2 + par * 4
                self.dve((lambda b, c1=c0: lambda v: v.reduce_max(
                    brow[:, c1:c1 + 1], b, axis=AX.X))(A[0:1, :]),
                    [rA], [r_brow], f"bb.qmx.{st_}.{h}")
                self.act((lambda a, b: lambda sc: sc.activation(a, b, AF.Sqrt))(
                    brow[:, c0 + 1:c0 + 2], brow[:, c0:c0 + 1]),
                    [r_brow], [r_brow], f"bb.qsqrt.{st_}.{h}")
                self.dve((lambda a, b, c2: lambda v: v.tensor_mul(a, b, c2))(
                    brow[:, c0 + 2:c0 + 3], brow[:, c0 + 1:c0 + 2], brow[:, 1:2]),
                    [r_brow], [r_brow], f"bb.negm1.{st_}.{h}")
                self.dve((lambda a, b: lambda v: v.tensor_scalar_mul(
                    a, b, -1.05 * SCALE_B))(brow[:, c0 + 3:c0 + 4],
                                            brow[:, c0 + 2:c0 + 3]),
                    [r_brow], [r_brow], f"bb.negm2.{st_}.{h}")
                self.mmg(B[:, 0:1], [(s["ones_rf"], brow[0:1, c0 + 3:c0 + 4])],
                         [r_brow, self.R("ones")], rB, f"bb.nmb.{st_}.{h}")
                self.dve((lambda a, b: lambda v: v.tensor_copy(a, b))(
                    s["scol"][:, 0 + 3 * par:1 + 3 * par], B[:, 0:1]), [rB],
                    [r_scol[par]], f"bb.nmc.{st_}.{h}")
                self.dve((lambda a, b, c2: lambda v: v.tensor_scalar_add(
                    a, b, c2))(bcols[:, par * 16: par * 16 + 16],
                               s["gpen"][:, st_ * 16: st_ * 16 + 16],
                               s["scol"][:, 0 + 3 * par:1 + 3 * par]),
                    [self.R("gpen"), r_scol[par]], [r_bcols[par]],
                    f"bb.bcols.{st_}.{h}")

            def kcloop(h):
                par = h % 2
                g = h // 4
                A, B = ps[2 + 2 * par], ps[3 + 2 * par]
                rA, rB = self.psr[2 + 2 * par], self.psr[3 + 2 * par]
                qro_t, r_qro = qro[par]
                nkc = 0
                for kc in range(n_main):
                    spi = kc % 2
                    self.mmg(ps[spi], [(Kf[:, g * 2048 + kc * 128:
                                           g * 2048 + (kc + 1) * 128], qro_t)],
                             r_Kf + r_qro, self.psr[spi],
                             f"bb.smm.{st_}.{h}.{kc}")
                    Ea, r_Ea = E[kc % 2]
                    self.act((lambda a, b, bc: lambda sc: sc.activation(
                        a, b, AF.Exp, bias=bc, scale=SCALE_B))(
                        Ea, ps[spi], bcols[:, par * 16 + kc: par * 16 + kc + 1]),
                        [self.psr[spi], r_bcols[par]], r_Ea,
                        f"bb.exp.{st_}.{h}.{kc}")
                    self.mmg(ps[6][0:1, :], [(s["ones_cb"], Ea)],
                             r_Ea + [self.R("ones")], self.psr[6],
                             f"bb.zg.{st_}.{h}.{kc}", accumulate=(nkc > 0))
                    self.mmg(A, [(Vf[:, kc * 512 + g * 128:
                                     kc * 512 + (g + 1) * 128], Ea)],
                             r_Vf + r_Ea, rA,
                             f"bb.pv.{st_}.{h}.{kc}", accumulate=(nkc > 0))
                    nkc += 1
                for j in range(5):
                    jj = 4 * st_ + j
                    spi = j % 2
                    self.mmg(ps[spi], [(Kl[:, g * 1152 + jj * 128:
                                           g * 1152 + (jj + 1) * 128], qro_t)],
                             r_Kl + r_qro, self.psr[spi],
                             f"bb.lsmm.{st_}.{h}.{j}")
                    Ea, r_Ea = E[j % 2]
                    Eb, r_Eb = E[1 - j % 2]
                    self.act((lambda a, b, bc: lambda sc: sc.activation(
                        a, b, AF.Exp, bias=bc, scale=SCALE_B))(
                        Ea, ps[spi], s["scol"][:, 0 + 3 * par:1 + 3 * par]),
                        [self.psr[spi], r_scol[par]], r_Ea,
                        f"bb.lexp.{st_}.{h}.{j}")
                    self.dve((lambda a, b, c2: lambda v: v.tensor_mul(a, b, c2))(
                        Eb, Ea, ctri_s[:, j * 512:(j + 1) * 512]),
                        r_Ea + r_ctri, r_Eb, f"bb.ec.{st_}.{h}.{j}")
                    self.dve((lambda a, b, c2: lambda v: v.tensor_mul(a, a, c2))(
                        Ea, Ea, lmask_s[:, j * 512:(j + 1) * 512]),
                        r_Ea + r_lm, r_Ea, f"bb.el.{st_}.{h}.{j}")
                    self.mmg(ps[6][0:1, :], [(s["ones_cb"], Eb)],
                             r_Eb + [self.R("ones")], self.psr[6],
                             f"bb.lzg.{st_}.{h}.{j}", accumulate=True)
                    self.mmg(ps[7][0:1, :], [(s["ones_cb"], Ea)],
                             r_Ea + [self.R("ones")], self.psr[7],
                             f"bb.lzl.{st_}.{h}.{j}", accumulate=(j > 0))
                    vsl = Vl[:, jj * 512 + g * 128: jj * 512 + (g + 1) * 128]
                    self.mmg(A, [(vsl, Eb)], r_Vl + r_Eb, rA,
                             f"bb.lpv.{st_}.{h}.{j}", accumulate=True)
                    self.mmg(B, [(vsl, Ea)], r_Vl + r_Ea, rB,
                             f"bb.lpvl.{st_}.{h}.{j}", accumulate=(j > 0))
                par4 = par * 512
                self.dve((lambda a, b: lambda v: v.tensor_copy(a, b))(
                    rows[:, par4:par4 + 512], ps[6][0:1, :]),
                    [self.psr[6]], [r_rows], f"bb.zge.{st_}.{h}")
                self.dve((lambda a, b: lambda v: v.tensor_copy(a, b))(
                    rows[:, 1024 + par4:1024 + par4 + 512], ps[7][0:1, :]),
                    [self.psr[7]], [r_rows], f"bb.zle.{st_}.{h}")

            def tail(h):
                par = h % 2
                A, B = ps[2 + 2 * par], ps[3 + 2 * par]
                rA, rB = self.psr[2 + 2 * par], self.psr[3 + 2 * par]
                par4 = par * 512
                zg = rows[:, par4:par4 + 512]
                zl = rows[:, 1024 + par4:1024 + par4 + 512]
                self.dve((lambda a, b: lambda v: v.tensor_copy(a, b))(
                    t1_t, A), [rA], r_t1, f"bb.ouc.{st_}.{h}")
                self.act((lambda a, b: lambda sc: sc.activation(
                    a, b, AF.Ln))(zg, zg), [r_rows], [r_rows],
                    f"bb.cgl.{st_}.{h}")
                self.act((lambda a, b, bb_=s["cbias"][0:1, 2:3]: lambda sc:
                          sc.activation(a, b, AF.Exp, scale=-1.0, bias=bb_))(
                    zg, zg), [r_rows, self.R("cbias")], [r_rows],
                    f"bb.cge.{st_}.{h}")
                self.act((lambda a, b: lambda sc: sc.activation(
                    a, b, AF.Ln))(zl, zl), [r_rows], [r_rows],
                    f"bb.cll.{st_}.{h}")
                self.act((lambda a, b, bb_=s["cbias"][0:1, 2:3]: lambda sc:
                          sc.activation(a, b, AF.Exp, scale=-1.0, bias=bb_))(
                    zl, zl), [r_rows, self.R("cbias")], [r_rows],
                    f"bb.cle.{st_}.{h}")
                self.mmg(A, [(s["ones_rf"], zg[0:1, :])],
                         [r_rows, self.R("ones")] + r_t1, rA,
                         f"bb.cgb.{st_}.{h}")
                self.dve((lambda a, b, c2: lambda v: v.tensor_mul(a, b, c2))(
                    qh_t, t1_t, A), r_t1 + [rA], r_qh,
                    f"bb.on1.{st_}.{h}")
                self.dve((lambda a, b: lambda v: v.tensor_copy(a, b))(
                    t1_t, B), [rB], r_t1, f"bb.olc.{st_}.{h}")
                self.mmg(B, [(s["ones_rf"], zl[0:1, :])],
                         [r_rows, self.R("ones")] + r_t1, rB,
                         f"bb.clb.{st_}.{h}")
                self.dve((lambda a, b, c2: lambda v: v.tensor_mul(a, a, c2))(
                    t1_t, t1_t, B), r_t1 + [rB], r_t1,
                    f"bb.on2.{st_}.{h}")
                osl = Ob[:, h * 512:(h + 1) * 512]
                self.dve((lambda a, b, c2: lambda v: v.tensor_add(a, b, c2))(
                    osl, qh_t, t1_t), r_qh + r_t1, r_Ob,
                    f"bb.ow.{st_}.{h}")

            for h in range(16):
                prep(h)
                kcloop(h)
                if h > 0:
                    tail(h - 1)
            tail(15)
            for oc in range(16):
                bi = oc % 3
                src = t["wo"][:, oc * 128:(oc + 1) * 128].rearrange(
                    "(kc q) n -> q kc n", q=128)
                self.load(wst[bi].rearrange("q (kc n) -> q kc n", kc=16), src,
                          f"wst{bi}", f"bb.wo.{st_}.{oc}")
                pi = 2 + oc % 2
                pairs = [(wst[bi][:, hc * 128:(hc + 1) * 128],
                          Ob[:, hc * 512:(hc + 1) * 512]) for hc in range(16)]
                self.mmg(ps[pi], pairs, [r_w[bi]] + r_Ob, self.psr[pi],
                         f"bb.womm.{st_}.{oc}")
                xsl = x[:, oc * 1024 + S: oc * 1024 + S + 512]
                self.dve((lambda a, b: lambda v: v.tensor_add(a, a, b))(
                    xsl, ps[pi]), [self.psr[pi], rx[oc]], [rx[oc]],
                    f"bb.res.{st_}.{oc}")

    def _exchange_b(self, r_Kown, r_Vown, r_Kf, r_Vf, r_Kl, r_Vl,
                    Kf, Vf, Kl, Vl):
        s, t = self.s, self.t
        Kown, _ = self.carve("bKown2", 0, [128, 4096], BF16)
        Vown, _ = self.carve("bVown2", 16384, [128, 4096], BF16)
        sems = self.xsems["b"]

        def blob(g):
            sd, sp_ = sems
            ko3 = Kown.rearrange("d (g t) -> d g t", g=4)
            vo3 = Vown.rearrange("p (tc v) -> p tc v", tc=8)
            kf3 = Kf.rearrange("d (g t) -> d g t", g=4)
            vf3 = Vf.rearrange("p (tc v) -> p tc v", tc=16)
            kl3 = Kl.rearrange("d (g t) -> d g t", g=4)
            vl3 = Vl.rearrange("p (tc v) -> p tc v", tc=9)
            shk = [t["shb_k"][i].rearrange("(g d) t -> d g t", d=128)
                   for i in range(2)]
            shv = [t["shb_v"][i].rearrange("(tc p) v -> p tc v", p=128)
                   for i in range(2)]
            with g.register("rparB") as rpar, g.register("rflagB") as rflag, \
                    g.register("rnonB") as rnon, g.register("rcntB") as rcnt:
                g.reg_load(rpar, s["pars"][0:1, 0:1])
                g.reg_load(rnon, s["pars"][0:1, 2:3])
                for p in range(2):
                    ctx = g.If_eq(rpar, 0) if p == 0 else g.Else()
                    with ctx:
                        g.dma_start(out=shk[p], in_=ko3).then_inc(sd, 16)
                        g.dma_start(out=shv[p], in_=vo3).then_inc(sd, 16)
                        g.wait_ge(sd, 32)
                        g.dma_start(out=t["shf"][1, p:p + 1, :],
                                    in_=s["pars"]).then_inc(sd, 16)
                        g.wait_ge(sd, 48)
                g.reg_mov(rcnt, 0)
                g.reg_mov(rflag, 0)
                for p in range(2):
                    o = 1 - p
                    ctx = g.If_eq(rpar, 0) if p == 0 else g.Else()
                    with ctx:
                        with g.While(lambda: g.snap(rflag) != g.snap(rnon)):
                            g.reg_add(rcnt, rcnt, 16)
                            g.dma_start(out=s["pars"][0:1, 4:8],
                                        in_=t["shf"][1, o:o + 1, 0:4]).then_inc(sp_, 16)
                            g.wait_ge(sp_, rcnt)
                            g.reg_load(rflag, s["pars"][0:1, 6:7])
                        g.dma_start(out=kf3[:, :, p * 1024:(p + 1) * 1024],
                                    in_=shk[p]).then_inc(sd, 16)
                        g.dma_start(out=kf3[:, :, o * 1024:(o + 1) * 1024],
                                    in_=shk[o]).then_inc(sd, 16)
                        g.dma_start(out=vf3[:, p * 8:(p + 1) * 8, :],
                                    in_=shv[p]).then_inc(sd, 16)
                        g.dma_start(out=vf3[:, o * 8:(o + 1) * 8, :],
                                    in_=shv[o]).then_inc(sd, 16)
                        g.dma_start(out=kl3[:, :, 0:128],
                                    in_=shk[o][:, :, 896:1024]).then_inc(sd, 16)
                        g.dma_start(out=kl3[:, :, 128:1152],
                                    in_=shk[p]).then_inc(sd, 16)
                        g.dma_start(out=vl3[:, 0:1, :],
                                    in_=shv[o][:, 7:8, :]).then_inc(sd, 16)
                        g.dma_start(out=vl3[:, 1:9, :],
                                    in_=shv[p]).then_inc(sd, 16)
                        g.wait_ge(sd, 48 + 8 * 16)
            return g.nop()

        self.P.op("sp", blob, reads=r_Kown + r_Vown + [self.R("pars")],
                  writes=r_Kf + r_Vf + r_Kl + r_Vl + [self.R("pars_poll")],
                  tag="xchgB")

    def emit_store(self):
        s, t = self.s, self.t
        rx = [self.R(f"x{c}") for c in range(16)]
        o3 = t["out"].rearrange("(c p) t -> p c t", p=128)
        so = self.P.op("sp", (lambda d, src: lambda g: g.dma_start(out=d, in_=src))(
            o3, s["x"].rearrange("p (c t) -> p c t", c=16)),
            reads=rx, dma=True, tag="store", slot="store")
        self.P.op("sp", lambda g: g.nop(),
                  deps=[so] + self.tap_stores, tag="final")

    def build(self):
        self.declare()
        with contextlib.ExitStack() as stack:
            self.xsems = {
                "a": (stack.enter_context(self.nc.semaphore("xad")),
                      stack.enter_context(self.nc.semaphore("xap"))),
                "b": (stack.enter_context(self.nc.semaphore("xbd")),
                      stack.enter_context(self.nc.semaphore("xbp"))),
            }
            self.emit_setup()
            self.emit_ln(1)
            self.emit_attn_a()
            self.tap("x1", self.s["x"], [128, 16 * 1024], F32,
                     [self.R(f"x{c}") for c in range(16)])
            if self.n_sub >= 2:
                self.emit_ln(2)
                self.emit_ffn(1)
                self.tap("x2", self.s["x"], [128, 16 * 1024], F32,
                         [self.R(f"x{c}") for c in range(16)])
            if self.n_sub >= 3:
                self.emit_ln(3)
                self.emit_attn_b()
                self.tap("x3", self.s["x"], [128, 16 * 1024], F32,
                         [self.R(f"x{c}") for c in range(16)])
                self.tap("bO", self.s["bO"], [128, 8192], BF16,
                         [self.R(f"pg{i}") for i in range(25, 33)])
            if self.n_sub >= 4:
                self.emit_ln(4)
                self.emit_ffn(2)
            self.emit_store()
            self.P.finalize(paranoid=bool(os.environ.get("KFW_PARANOID")))
            sems = {nm: stack.enter_context(self.nc.semaphore(nm.replace(".", "_")))
                    for nm in self.P.sem_names()}
            block = stack.enter_context(self.nc.Block())

            @block.tensor
            def _(te):
                self.P.emit_engine("pe", te, sems)

            @block.scalar
            def _(sc):
                self.P.emit_engine("act", sc, sems)

            @block.vector
            def _(v):
                self.P.emit_engine("dve", v, sems)

            @block.gpsimd
            def _(g):
                self.P.emit_engine("pool", g, sems)

            @block.sync
            def _(sp):
                self.P.emit_engine("sp", sp, sems)
        return self.nc


def build_nc(n_sub=None, taps=None, detect=True, private_shared=False):
    nc = bass.Bass("TRN2", target_bir_lowering=False, debug=False,
                   detect_race_conditions=detect)
    kb = KB(nc, n_sub=N_SUB if n_sub is None else n_sub,
            taps=TAPS if taps is None else taps, private_shared=private_shared)
    kb.build()
    return nc


_NC_CACHE = {}


def make_inputs(inputs_full, nonce):
    x_full = np.ascontiguousarray(
        np.asarray(inputs_full["L0_attn_a_x"], np.float32).transpose(0, 2, 1))
    wqkvuT = np.ascontiguousarray(np.asarray(inputs_full["L0_attn_a_W_qkvu"], np.float32).T)
    woutT = np.ascontiguousarray(np.asarray(inputs_full["L0_attn_a_W_out"], np.float32).T)
    w0T = np.ascontiguousarray(np.asarray(inputs_full["L0_ffn_W0"], np.float32).T)
    w1T = np.ascontiguousarray(np.asarray(inputs_full["L0_ffn_W1"], np.float32).T)
    b0 = np.asarray(inputs_full["L0_ffn_bias0"], np.float32)
    b1 = np.asarray(inputs_full["L0_ffn_bias1"], np.float32)
    shared = {
        "wqkvuT": wqkvuT, "woutT": woutT,
        "wq": np.asarray(inputs_full["L0_attn_b_wq"], np.float32),
        "wk": np.asarray(inputs_full["L0_attn_b_wk"], np.float32),
        "wv": np.asarray(inputs_full["L0_attn_b_wv"], np.float32),
        "wo": np.asarray(inputs_full["L0_attn_b_wo"], np.float32),
        "w0T": w0T, "w1T": w1T,
        "b0c": np.ascontiguousarray(b0.reshape(64, 128).T),
        "b1c": np.ascontiguousarray(b1.reshape(16, 128).T),
        "slope_row": np.asarray(inputs_full["L0_attn_a_slope_rate"],
                                np.float32).reshape(1, 16),
        "cbias": np.array([[1e-5, 1e-6, LN05, 0.0]], np.float32),
    }
    iota_row, kd_col, bd_cols, idxt, rt = _host_consts()
    cos_f, sin_f = _rope_tables()
    consts = {"iota_row": iota_row, "kd_col": kd_col, "bd_cols": bd_cols,
              "idxt": idxt, "rt": rt}
    in_maps = []
    for c in range(8):
        b, par = c // 2, c % 2
        t0 = par * 1024
        gpen, ctri, lmask = _attnb_masks(par)
        m = dict(shared)
        m.update(consts)
        m["x"] = np.ascontiguousarray(x_full[b][:, t0:t0 + 1024])
        m["cosf"] = np.ascontiguousarray(cos_f[:, t0:t0 + 1024])
        m["sinf"] = np.ascontiguousarray(sin_f[:, t0:t0 + 1024])
        m["isodd"] = np.full((128, 1), float(par), np.float32)
        m["pars"] = np.array([[par, nonce + 1, nonce + 2, 0, 0, 0, 0, 0]],
                             np.int32)
        m["gpen"] = gpen
        m["ctri"] = ctri
        m["lmask"] = lmask
        in_maps.append(m)
    return in_maps


def kernel(**inputs):
    import time
    key = (N_SUB, tuple(sorted(TAPS)))
    if key not in _NC_CACHE:
        _NC_CACHE[key] = build_nc()
    nc = _NC_CACHE[key]
    nonce = (int(time.time() * 1000) & 0x3FFFFFF) * 16 + 3
    in_maps = make_inputs(inputs, nonce)
    res = run_bass_kernel_spmd(nc, in_maps, core_ids=list(range(8)))
    out = np.zeros((4, 2048, 2048), np.float32)
    for c in range(8):
        b, par = c // 2, c % 2
        shard = np.asarray(res.results[c]["out"])
        out[b, par * 1024:(par + 1) * 1024, :] = shard.T
    return out


# version 06
# speedup vs baseline: 1.2817x; 1.2162x over previous
import contextlib
import os
import numpy as np
import ml_dtypes
import concourse.bass as bass
import concourse.mybir as mybir
from concourse.bass_utils import run_bass_kernel_spmd

F32 = mybir.dt.float32
BF16 = mybir.dt.bfloat16
I32 = mybir.dt.int32
AF = mybir.ActivationFunctionType
ALU = mybir.AluOpType
AX = mybir.AxisListType

SCALE_B = float(1.0 / np.sqrt(128.0))
LN05 = float(np.log(0.5))

N_SUB = 4
TAPS = ()

ENGINES = ("pe", "act", "dve", "pool", "sp")


class Op:
    __slots__ = ("eng", "fn", "deps", "dma", "idx", "cum", "sem", "needed",
                 "tag")

    def __init__(self, eng, fn, deps, dma, idx, tag, slot=None):
        self.eng = eng
        self.fn = fn
        self.deps = deps
        self.dma = dma
        self.idx = idx
        self.cum = None
        if dma:
            assert slot is not None, f"dma op {tag} needs slot="
            self.sem = f"d.{slot}"
        else:
            self.sem = eng
        self.needed = False
        self.tag = tag


class Res:
    __slots__ = ("writer", "readers", "name")

    def __init__(self, name=""):
        self.writer = None
        self.readers = []
        self.name = name


class Prog:
    def __init__(self):
        self.ops = {e: [] for e in ENGINES}
        self.n = 0

    def op(self, eng, fn, reads=(), writes=(), deps=(), dma=False, tag="",
           slot=None):
        dd = [d for d in deps if d is not None]
        for r in reads:
            if r.writer is not None:
                dd.append(r.writer)
        for w in writes:
            if w.writer is not None:
                dd.append(w.writer)
            dd.extend(w.readers)
        if dma and slot is None and writes:
            slot = writes[0].name
        o = Op(eng, fn, dd, dma, len(self.ops[eng]), tag, slot=slot)
        self.ops[eng].append(o)
        self.n += 1
        for r in reads:
            r.readers.append(o)
        for w in writes:
            w.writer = o
            w.readers = []
        return o

    def finalize(self, paranoid=False):
        self.paranoid = paranoid
        for e in ENGINES:
            for o in self.ops[e]:
                if o.dma:
                    o.needed = True
                for d in o.deps:
                    if d.eng != o.eng or d.dma or paranoid:
                        d.needed = True
        counts = {}
        for e in ENGINES:
            for o in self.ops[e]:
                if o.needed:
                    amt = 16 if o.dma else 1
                    counts[o.sem] = counts.get(o.sem, 0) + amt
                    o.cum = counts[o.sem]
        self.sem_totals = counts
        self._simulate()

    def _simulate(self):
        pos = {e: 0 for e in ENGINES}
        vals = {s: 0 for s in self.sem_totals}
        progress = True
        while progress:
            progress = False
            for e in ENGINES:
                while pos[e] < len(self.ops[e]):
                    o = self.ops[e][pos[e]]
                    ok = True
                    for d in o.deps:
                        if d.eng == e and not d.dma:
                            continue
                        if vals.get(d.sem, 0) < d.cum:
                            ok = False
                            break
                    if not ok:
                        break
                    if o.needed:
                        vals[o.sem] = vals.get(o.sem, 0) + (16 if o.dma else 1)
                    pos[e] += 1
                    progress = True
        stuck = {e: len(self.ops[e]) - pos[e] for e in ENGINES
                 if pos[e] < len(self.ops[e])}
        if stuck:
            for e in ENGINES:
                if pos[e] < len(self.ops[e]):
                    o = self.ops[e][pos[e]]
                    info = [(d.eng, d.tag, d.cum, vals.get(d.sem, 0))
                            for d in o.deps if not (d.eng == e and not d.dma)]
                    raise RuntimeError(
                        f"DEADLOCK: engine {e} stuck at op {pos[e]} "
                        f"tag={o.tag} deps={info}; remaining={stuck}")

    def emit_engine(self, e, eng, sem_handles):
        hiw = {}
        paranoid = getattr(self, "paranoid", False)
        for o in self.ops[e]:
            for d in o.deps:
                if d.eng == e and not d.dma and not paranoid:
                    continue
                if hiw.get(d.sem, 0) >= d.cum:
                    continue
                eng.wait_ge(sem_handles[d.sem], d.cum)
                hiw[d.sem] = d.cum
            inst = o.fn(eng)
            if o.needed:
                if inst is None:
                    raise RuntimeError(
                        f"op {o.tag} on {e} needs inc but returned None")
                inst.then_inc(sem_handles[o.sem], 16 if o.dma else 1)

    def sem_names(self):
        return sorted(self.sem_totals)



def _host_consts():
    arr = np.arange(1, 257, dtype=np.float32)
    iota_row = arr[None, :]
    kd_col = np.zeros((128, 2), np.float32)
    for sub in range(2):
        kd_col[:, sub] = 255.0 - (sub * 128 + np.arange(128))
    bd_cols = np.zeros((128, 4), np.float32)
    for i in range(4):
        bd_cols[:, i] = 256.0 * i
    k_ = np.arange(256)[:, None]
    q_ = np.arange(256)[None, :]
    idxt_full = np.where(q_ >= k_, (q_ - k_).astype(np.float32), 1.0e6)
    idxt = np.zeros((128, 512), np.float32)
    idxt[:, 0:256] = idxt_full[0:128]
    idxt[:, 256:512] = idxt_full[128:256]
    R = np.zeros((128, 128), np.float32)
    for d in range(64):
        R[d, 64 + d] = -1.0
        R[64 + d, d] = 1.0
    rt = np.ascontiguousarray(R.T)
    return iota_row, kd_col, bd_cols, idxt, rt


def _rope_tables(T_total=2048, D=128, base=10000.0):
    inv = 1.0 / base ** (np.arange(0, D, 2, dtype=np.float32) / D)
    t = np.arange(T_total, dtype=np.float32)
    freqs = t[:, None] * inv[None, :]
    emb = np.concatenate([freqs, freqs], 1)
    return np.cos(emb).T.astype(np.float32), np.sin(emb).T.astype(np.float32)


def _attnb_masks(parity, W=256):
    t0 = parity * 1024
    gpen = np.zeros((128, 32), np.float32)
    gpen[:] = -1.0e30
    ctri = np.zeros((2, 5, 128, 512), np.float32)
    lmask = np.zeros((2, 5, 128, 512), np.float32)
    for s in range(2):
        G0 = t0 + 512 * s
        n_main = 7 if s == 0 else 11
        for r in range(n_main):
            absk = (1 - parity) * 8 + r if r < 8 else parity * 8 + (r - 8)
            if (absk + 1) * 128 <= G0:
                gpen[:, s * 16 + r] = 0.0
        for j in range(5):
            jj = 4 * s + j
            kk = np.arange(128)
            gkey = t0 + (jj - 1) * 128 + kk
            qq = np.arange(512)
            gq = G0 + qq
            valid_k = gkey >= 0
            c = (gkey[:, None] <= gq[None, :]) & valid_k[:, None]
            ctri[s, j] = c.astype(np.float32)
            blk = gq // W
            k_start = np.maximum(0, blk * W - W // 2)
            upper = gq - (blk * W - k_start)
            l = ((gkey[:, None] >= k_start[None, :])
                 & (gkey[:, None] <= upper[None, :]) & valid_k[:, None])
            lmask[s, j] = l.astype(np.float32)
    def pack(m):
        out = np.zeros((128, 2 * 5 * 512), np.float32)
        for s in range(2):
            for j in range(5):
                out[:, (s * 5 + j) * 512:(s * 5 + j + 1) * 512] = m[s, j]
        return out.astype(ml_dtypes.bfloat16)
    return gpen, pack(ctri), pack(lmask)



class KB:
    def __init__(self, nc, n_sub=4, taps=(), private_shared=False):
        self.nc = nc
        self.P = Prog()
        self.n_sub = n_sub
        self.private_shared = private_shared
        self.taps = set(taps)
        self.t = {}
        self.s = {}
        self.ps = []
        self.r = {}
        self.tap_stores = []

    def R(self, name):
        if name not in self.r:
            self.r[name] = Res(name)
        return self.r[name]

    def sb(self, name, shape, dtype):
        ap = self.nc.alloc_sbuf_tensor("sb_" + name, shape, dtype).ap()
        self.s[name] = ap
        return ap

    def load(self, dst, src, slot, tag, eng="pool", reads=(), deps=(),
             writes_extra=()):
        return self.P.op(eng, (lambda d, s: lambda g: g.dma_start(out=d, in_=s))(dst, src),
                         reads=list(reads),
                         writes=[self.R(slot)] + list(writes_extra), deps=deps,
                         dma=True, tag=tag, slot=slot)

    def mmg(self, ps_ap, pairs, reads, wres, tag, accumulate=False):
        def fn(t):
            ret = None
            n = len(pairs)
            for i, (l, rhs) in enumerate(pairs):
                ret = t.matmul(ps_ap, l, rhs,
                               start=(i == 0 and not accumulate),
                               stop=(i == n - 1), skip_group_check=True)
            return ret
        return self.P.op("pe", fn, reads=list(reads), writes=[wres], tag=tag)

    def dve(self, fn, reads, writes, tag):
        return self.P.op("dve", fn, reads=list(reads), writes=list(writes), tag=tag)

    def act(self, fn, reads, writes, tag):
        return self.P.op("act", fn, reads=list(reads), writes=list(writes), tag=tag)

    def tap(self, name, src_ap, shape, dtype, reads):
        if name not in self.taps:
            return
        d = self.nc.dram_tensor("tap_" + name, shape, dtype,
                                kind="ExternalOutput").ap()
        o = self.P.op("sp", (lambda dd, ss: lambda g: g.dma_start(out=dd, in_=ss))(d, src_ap),
                      reads=list(reads), dma=True, tag="tap_" + name,
                      slot="tap_" + name)
        self.tap_stores.append(o)

    def declare(self):
        nc = self.nc
        T = self.t
        def din(name, shape, dtype=F32):
            T[name] = nc.dram_tensor(name, shape, dtype, kind="ExternalInput").ap()
        din("x", [2048, 1024])
        din("wqkvuT", [2048, 8192])
        din("woutT", [2048, 2048])
        din("wq", [2048, 2048])
        din("wk", [2048, 512])
        din("wv", [2048, 512])
        din("wo", [2048, 2048])
        din("w0T", [2048, 8192])
        din("w1T", [8192, 2048])
        din("b0c", [128, 64])
        din("b1c", [128, 16])
        din("slope_row", [1, 16])
        din("iota_row", [1, 256])
        din("kd_col", [128, 2])
        din("bd_cols", [128, 4])
        din("idxt", [128, 512])
        din("rt", [128, 128])
        din("cosf", [128, 1024])
        din("sinf", [128, 1024])
        din("isodd", [128, 1])
        din("pars", [1, 8], I32)
        din("cbias", [1, 4])
        din("gpen", [128, 32])
        din("ctri", [128, 5120], BF16)
        din("lmask", [128, 5120], BF16)
        T["out"] = nc.dram_tensor("out", [2048, 1024], F32,
                                  kind="ExternalOutput").ap()
        sp_ = "Local" if self.private_shared else "Shared"
        T["sha"] = nc.dram_tensor("sha", [2, 2048, 128], F32, kind="Internal",
                                  addr_space=sp_).ap()
        T["shb_k"] = nc.dram_tensor("shb_k", [2, 512, 1024], BF16,
                                    kind="Internal", addr_space=sp_).ap()
        T["shb_v"] = nc.dram_tensor("shb_v", [2, 1024, 512], BF16,
                                    kind="Internal", addr_space=sp_).ap()
        T["shf"] = nc.dram_tensor("shf", [2, 2, 8], I32, kind="Internal",
                                  addr_space=sp_).ap()

        self.sb("x", [128, 16 * 1024], F32)
        self.sb("y", [128, 16 * 1024], BF16)
        self.arena = self.sb("arena", [128, 86 * 1024], mybir.dt.uint8)
        self.sb("rows", [1, 2048], F32)
        for i in range(3):
            self.sb(f"wst{i}", [128, 2048], BF16)
        self.sb("iota_row", [1, 256], F32)
        self.sb("kd_col", [128, 2], F32)
        self.sb("bd_cols", [128, 4], F32)
        self.sb("idxt", [128, 512], F32)
        self.sb("rt", [128, 128], BF16)
        self.sb("isodd", [128, 1], F32)
        self.sb("pars", [1, 8], I32)
        self.sb("cbias", [1, 4], F32)
        self.sb("gpen", [128, 32], F32)
        self.sb("slope_row", [1, 16], F32)
        self.sb("negs_row", [1, 16], F32)
        self.sb("ones_cb", [128, 1], BF16)
        self.sb("ones_rb", [1, 128], BF16)
        self.sb("ones_rf", [1, 128], F32)
        self.sb("b0c", [128, 64], F32)
        self.sb("b1c", [128, 16], F32)
        self.sb("srow_bf", [1, 256], BF16)
        self.sb("scol", [128, 28], F32)
        ps_pair = nc.alloc_psum_tensor("psP", [128, 1024], F32).ap()
        self.ps_pair = ps_pair
        self.ps.append(ps_pair[:, 0:512])
        self.ps.append(ps_pair[:, 512:1024])
        for i in range(2, 8):
            self.ps.append(nc.alloc_psum_tensor(f"ps{i}", [128, 512], F32).ap())
        self.psr = [self.R(f"ps{i}") for i in range(8)]

    def carve(self, name, off_bytes, shape, dtype):
        nbytes = {F32: 4, BF16: 2, I32: 4}[dtype]
        assert off_bytes % 4 == 0
        assert off_bytes + shape[1] * nbytes <= 86 * 1024, name
        base = self.s["arena"]
        ap = base.tensor.ap().bitcast(dtype)[0:shape[0],
                                             off_bytes // nbytes:
                                             off_bytes // nbytes + shape[1]]
        self.s[name] = ap
        return ap

    def emit_ln(self, idx):
        P, s, ps = self.P, self.s, self.ps
        x, y = s["x"], s["y"]
        rows = s["rows"]
        tmpf, r_tmp = self.carve("ln_tmp", 0, [128, 1024], F32)
        sqt, r_sq = self.carve("ln_sq", 4096, [128, 1024], BF16)
        sq_mean, r_sqm = self.carve("ln_sqm", 6144, [1, 1024], F32)
        ln_rb, r_lnrb = self.carve("ln_rb", 10240, [1, 2048], BF16)
        rx = [self.R(f"x{c}") for c in range(16)]
        ry = [self.R(f"y{c}") for c in range(16)]
        r_rows = self.R("rows")
        for c in range(16):
            xc = x[:, c * 1024:(c + 1) * 1024]
            yc = y[:, c * 1024:(c + 1) * 1024]
            self.dve((lambda a, b: lambda v: v.tensor_copy(a, b))(yc, xc),
                     [rx[c]], [ry[c]], f"ln{idx}.cp{c}")
            self.dve((lambda a, b: lambda v: v.tensor_mul(a, b, b))(sqt, yc),
                     [ry[c]], r_sq, f"ln{idx}.sq{c}")
            for h in range(2):
                ysl = yc[:, h * 512:(h + 1) * 512]
                ssl = sqt[:, h * 512:(h + 1) * 512]
                self.mmg(ps[4 + h][0:1, :], [(s["ones_cb"], ysl)],
                         [ry[c], self.R("ones")], self.psr[4 + h],
                         f"ln{idx}.s1.{c}.{h}", accumulate=(c != 0))
                self.mmg(ps[6 + h][0:1, :], [(s["ones_cb"], ssl)],
                         r_sq + [self.R("ones")], self.psr[6 + h],
                         f"ln{idx}.s2.{c}.{h}", accumulate=(c != 0))
        mean = rows[:, 0:1024]
        var = rows[:, 1024:2048]
        for h in range(2):
            self.dve((lambda a, b: lambda v: v.tensor_scalar_mul(a, b, 1.0 / 2048.0))(
                mean[:, h * 512:(h + 1) * 512], ps[4 + h][0:1, :]),
                [self.psr[4 + h]], [r_rows], f"ln{idx}.mean{h}")
            self.dve((lambda a, b: lambda v: v.tensor_scalar_mul(a, b, 1.0 / 2048.0))(
                var[:, h * 512:(h + 1) * 512], ps[6 + h][0:1, :]),
                [self.psr[6 + h]], [r_rows], f"ln{idx}.msq{h}")
        self.dve((lambda a, b: lambda v: v.tensor_mul(a, b, b))(sq_mean, mean),
                 [r_rows], r_sqm, f"ln{idx}.m2")
        self.dve((lambda a, b, c2: lambda v: v.tensor_sub(a, b, c2))(var, var, sq_mean),
                 [r_rows] + r_sqm, [r_rows], f"ln{idx}.var")
        self.act((lambda a, b, bb_=s["cbias"][0:1, 0:1]: lambda sc: sc.activation(
            a, b, AF.Ln, bias=bb_))(var, var),
                 [r_rows, self.R("cbias")], [r_rows], f"ln{idx}.ln")
        self.act((lambda a, b: lambda sc: sc.activation(a, b, AF.Exp, scale=-0.5))(var, var),
                 [r_rows], [r_rows], f"ln{idx}.exp")
        self.dve((lambda a, b, c2: lambda v: v.tensor_mul(a, b, c2))(mean, mean, var),
                 [r_rows], [r_rows], f"ln{idx}.mr")
        self.dve((lambda a, b: lambda v: v.tensor_copy(a, b))(
            ln_rb[0:1, 0:1024], var), [r_rows], r_lnrb, f"ln{idx}.rcv")
        self.dve((lambda a, b: lambda v: v.tensor_copy(a, b))(
            ln_rb[0:1, 1024:2048], mean), [r_rows], r_lnrb, f"ln{idx}.rcm")
        for h in range(2):
            self.mmg(ps[0 + h], [(s["ones_rb"], ln_rb[0:1, h * 512:(h + 1) * 512])],
                     r_lnrb + [self.R("ones")], self.psr[0 + h], f"ln{idx}.bR{h}")
            self.mmg(ps[2 + h], [(s["ones_rb"], ln_rb[0:1, 1024 + h * 512:
                                                      1024 + (h + 1) * 512])],
                     r_lnrb + [self.R("ones")], self.psr[2 + h], f"ln{idx}.bM{h}")
        for c in range(16):
            for h in range(2):
                xsl = x[:, c * 1024 + h * 512: c * 1024 + h * 512 + 512]
                ysl = y[:, c * 1024 + h * 512: c * 1024 + h * 512 + 512]
                tsl = tmpf[:, h * 512:(h + 1) * 512]
                self.dve((lambda a, b, c2: lambda v: v.tensor_mul(a, b, c2))(
                    tsl, xsl, ps[0 + h]),
                    [rx[c], self.psr[0 + h]], r_tmp, f"ln{idx}.t{c}.{h}")
                self.dve((lambda a, b, c2: lambda v: v.tensor_sub(a, b, c2))(
                    ysl, tsl, ps[2 + h]),
                    r_tmp + [self.psr[2 + h]], [ry[c]], f"ln{idx}.y{c}.{h}")

    def emit_ffn(self, idx):
        P, s, ps, t = self.P, self.s, self.ps, self.t
        y, x = s["y"], s["x"]
        dbf, _dpg = self.carve("ffn_d", 0, [128, 16 * 1024], BF16)
        ebf, _epg = self.carve("ffn_e", 32 * 1024, [128, 16 * 1024], BF16)

        def dpg(n, h):
            return [self.R(f"pg{n}")]

        def epg(hc):
            return [self.R(f"pg{16 + hc}")]
        ry = [self.R(f"y{c}") for c in range(16)]
        rx = [self.R(f"x{c}") for c in range(16)]
        wst = [s[f"wst{i}"] for i in range(3)]
        r_w = [self.R(f"wst{i}") for i in range(3)]
        for p in range(4):
            for ncnk in range(16):
                gn = p * 16 + ncnk
                bi = gn % 3
                src = t["w0T"][:, gn * 128:(gn + 1) * 128].rearrange(
                    "(kc q) n -> q kc n", q=128)
                self.load(wst[bi].rearrange("q (kc n) -> q kc n", kc=16), src,
                          f"wst{bi}", f"f{idx}.w0.{gn}")
                for h in range(2):
                    pi = (ncnk * 2 + h) % 4
                    pairs = [(wst[bi][:, kc * 128:(kc + 1) * 128],
                              y[:, kc * 1024 + h * 512: kc * 1024 + h * 512 + 512])
                             for kc in range(16)]
                    self.mmg(ps[pi], pairs, [r_w[bi]] + ry, self.psr[pi],
                             f"f{idx}.dmm.{gn}.{h}")
                    dst = dbf[:, ncnk * 1024 + h * 512: ncnk * 1024 + h * 512 + 512]
                    self.act((lambda a, b, bc: lambda sc: sc.activation(
                        a, b, AF.Gelu, bias=bc))(dst, ps[pi], s["b0c"][:, gn:gn + 1]),
                        [self.psr[pi], self.R("b0c")], dpg(ncnk, h),
                        f"f{idx}.gelu.{gn}.{h}")
            for hc in range(16):
                bi = hc % 3
                src = t["w1T"][p * 2048:(p + 1) * 2048,
                               hc * 128:(hc + 1) * 128].rearrange(
                    "(kc q) n -> q kc n", q=128)
                self.load(wst[bi].rearrange("q (kc n) -> q kc n", kc=16), src,
                          f"wst{bi}", f"f{idx}.w1.{p}.{hc}")
                for h in range(2):
                    pi = 4 + (hc * 2 + h) % 2
                    pairs = [(wst[bi][:, n * 128:(n + 1) * 128],
                              dbf[:, n * 1024 + h * 512: n * 1024 + h * 512 + 512])
                             for n in range(16)]
                    self.mmg(ps[pi], pairs,
                             [r_w[bi]] + [self.R(f"pg{n}") for n in range(16)],
                             self.psr[pi], f"f{idx}.emm.{p}.{hc}.{h}")
                    dst = ebf[:, hc * 1024 + h * 512: hc * 1024 + h * 512 + 512]
                    if p == 0 and h == 0:
                        pass
                    if p == 0:
                        fn = (lambda a, b: lambda v: v.tensor_copy(a, b))(dst, ps[pi])
                    else:
                        fn = (lambda a, b: lambda v: v.tensor_add(a, a, b))(dst, ps[pi])
                    self.dve(fn, [self.psr[pi]] + (epg(hc) if p else []),
                             epg(hc), f"f{idx}.eacc.{p}.{hc}.{h}")
        for hc in range(16):
            xc = x[:, hc * 1024:(hc + 1) * 1024]
            ec = ebf[:, hc * 1024:(hc + 1) * 1024]
            self.dve((lambda a, b, bc: lambda v: v.scalar_tensor_tensor(
                a, b, bc, a, ALU.add, ALU.add))(xc, ec, s["b1c"][:, hc:hc + 1]),
                epg(hc) + [self.R("b1c"), rx[hc]], [rx[hc]],
                f"f{idx}.res.{hc}")

    PAGE = 2048

    def carve(self, name, off_bytes, shape, dtype):
        nbytes = {F32: 4, BF16: 2, I32: 4}[dtype]
        end = off_bytes + shape[1] * nbytes
        assert off_bytes % 4 == 0 and end <= 86 * 1024, name
        base = self.s["arena"].tensor.ap().bitcast(dtype)
        ap = base[0:shape[0], off_bytes // nbytes: off_bytes // nbytes + shape[1]]
        pages = [self.R(f"pg{i}") for i in
                 range(off_bytes // self.PAGE, (end + self.PAGE - 1) // self.PAGE)]
        self.s[name] = ap
        return ap, pages

    def emit_setup(self):
        s, t = self.s, self.t
        x3 = t["x"].rearrange("(c p) t -> p c t", p=128)
        self.load(s["x"].rearrange("p (c t) -> p c t", c=16), x3, "x_in",
                  "ld_x", eng="sp")
        xld = self.R("x_in")
        for c in range(16):
            self.R(f"x{c}").writer = xld.writer
        for nm in ("b0c", "b1c", "slope_row", "iota_row", "kd_col", "bd_cols",
                   "idxt", "isodd", "pars", "gpen", "cbias"):
            self.load(s[nm], t[nm], nm, "ld_" + nm, eng="sp")
        self.load(s["rt"], t["rt"], "rt", "ld_rt", eng="pool")
        def ms(g):
            g.memset(s["ones_cb"], 1.0)
            g.memset(s["ones_rb"], 1.0)
            return g.memset(s["ones_rf"], 1.0)
        self.P.op("pool", ms, writes=[self.R("ones")], tag="ones")
        self.dve((lambda a, b: lambda v: v.tensor_scalar_mul(a, b, -1.0))(
            s["negs_row"], s["slope_row"]),
            [self.R("slope_row")], [self.R("negs")], "negs")

    def _negcol(self, h, tag, bank=7):
        s, ps = self.s, self.ps
        self.mmg(ps[bank][:, 0:1], [(s["ones_rf"], s["negs_row"][0:1, h:h + 1])],
                 [self.R("negs"), self.R("ones")], self.psr[bank], tag + ".ncb")
        self.dve((lambda a, b: lambda v: v.tensor_copy(a, b))(
            s["scol"][:, 0:1], ps[bank][:, 0:1]),
            [self.psr[bank]], [self.R("scol")], tag + ".ncc")

    def _aa_evac(self, sub, h, Oar, r_O, sq_t, r_sqt):
        ps, s = self.ps, self.s
        for blk in range(2):
            opi = 2 + blk % 2
            osl = Oar[:, h * 512 + blk * 256: h * 512 + blk * 256 + 256]
            self.dve((lambda a, b: lambda v: v.tensor_copy(a, b))(
                osl, ps[opi][:, 0:256]), [self.psr[opi]], r_O,
                f"aa.ocp.{sub}.{h}.{blk}")
            self.dve((lambda a, b: lambda v: v.tensor_mul(a, b, b))(
                sq_t, osl), r_O, r_sqt, f"aa.osq.{sub}.{h}.{blk}")
            self.mmg(ps[7][0:1, blk * 256: blk * 256 + 256],
                     [(s["ones_cb"], sq_t)], r_sqt + [self.R("ones")],
                     self.psr[7], f"aa.ost.{sub}.{h}.{blk}",
                     accumulate=not (h == 0 and blk == 0))

    def emit_attn_a(self):
        P, s, ps, t = self.P, self.s, self.ps, self.t
        y, x = s["y"], s["x"]
        ry = [self.R(f"y{c}") for c in range(16)]
        rx = [self.R(f"x{c}") for c in range(16)]
        wst = [s[f"wst{i}"] for i in range(3)]
        r_w = [self.R(f"wst{i}") for i in range(3)]
        negsb = self.sb("aanegsb", [1, 16], BF16)
        r_negsb = [self.R("aanegsb")]
        self.r_negsb_pages = r_negsb
        aarb, r_aarb = self.carve("aarb", 31232, [1, 512], BF16)
        self.r_aarb_pages = r_aarb
        self.dve((lambda a, b: lambda v: v.tensor_copy(a, b))(
            negsb, s["negs_row"]), [self.R("negs")], r_negsb, "aa.negsb")
        kvpre, r_kvpre = self.carve("kvpre", 0, [128, 16 * 384], BF16)
        kvin_b, r_kvinb = self.carve("kvin_b", 12288, [128, 2048], BF16)
        kvf, r_kvf = self.carve("kvf", 16384, [128, 2048], F32)
        kvin_f, r_kvinf = self.carve("kvin_f", 24576, [128, 2048], F32)
        slab0, r_sl0 = self.carve("slab0", 32768, [128, 8192], BF16)
        slab1, r_sl1 = self.carve("slab1", 49152, [128, 8192], BF16)
        slabs = [(slab0, r_sl0), (slab1, r_sl1)]
        ktm_g, r_ktm = self.carve("ktm_g", 65536, [128, 4096], BF16)
        vtm_e, r_vtme = self.carve("vtm_e", 73728, [128, 4096], BF16)
        r_scol = self.R("scol")

        def group_prep(g, tag, with_kdc=False, with_main=False, bank=7):
            self.mmg(ps[bank][:, 0:4], [(s["ones_rf"], s["negs_row"][0:1, 4 * g:4 * g + 4])],
                     [self.R("negs"), self.R("ones")],
                     self.psr[bank], tag + ".nc4")
            self.dve((lambda a, b: lambda v: v.tensor_copy(a, b))(
                s["scol"][:, 0:4], ps[bank][:, 0:4]),
                [self.psr[bank]], [r_scol], tag + ".nc4c")
            for hh in range(4):
                sc_ap = s["scol"][:, hh:hh + 1]
                self.act((lambda a, b, sc_: lambda sc: sc.activation(
                    a, b, AF.Exp, scale=sc_))(
                    s["scol"][:, 4 + 4 * hh:8 + 4 * hh], s["bd_cols"], sc_ap),
                    [self.R("bd_cols"), r_scol], [r_scol], tag + f".bdp{hh}")
                if with_kdc:
                    self.act((lambda a, b, sc_: lambda sc: sc.activation(
                        a, b, AF.Exp, scale=sc_))(
                        s["scol"][:, 20 + 2 * hh:22 + 2 * hh], s["kd_col"], sc_ap),
                        [self.R("kd_col"), r_scol], [r_scol], tag + f".kdc{hh}")
                if with_main:
                    h = 4 * g + hh
                    self.act((lambda a, b, sc_: lambda sc: sc.activation(
                        a, b, AF.Exp, scale=sc_))(
                        self.s["aaddt"][:, hh * 512:(hh + 1) * 512],
                        s["idxt"], sc_ap),
                        [self.R("idxt"), r_scol], self.r_ddt4, tag + f".ddt{hh}")
                    self.act((lambda a, b, sc_: lambda sc: sc.activation(
                        a, b, AF.Exp, scale=sc_))(
                        s["srow_bf"][0:1, 0:256], s["iota_row"],
                        s["negs_row"][0:1, h:h + 1]),
                        [self.R("iota_row"), self.R("negs")], [self.R("srow_bf")],
                        tag + f".qdr{hh}")
                    self.mmg(ps[bank][:, 0:256],
                             [(s["ones_rb"], s["srow_bf"][0:1, 0:256])],
                             [self.R("srow_bf"), self.R("ones")], self.psr[bank],
                             tag + f".qdb{hh}")
                    self.dve((lambda a, b: lambda v: v.tensor_copy(a, b))(
                        self.s["aaqd4"][:, hh * 256:(hh + 1) * 256],
                        ps[bank][:, 0:256]), [self.psr[bank]], self.r_qd4,
                        tag + f".qdc{hh}")

        for g in range(4):
            group_prep(g, f"aa.ex.g{g}", with_kdc=True)
            for kind, coff, slot in (("v", 4096, 0), ("k", 2048, 1)):
                sl_ap, sl_r = slabs[slot]
                src = t["wqkvuT"][:, coff + g * 512: coff + (g + 1) * 512] \
                    .rearrange("(fc p) n -> p fc n", p=128)
                self.load(sl_ap.rearrange("p (fc n) -> p fc n", fc=16), src,
                          f"aslab{slot}", f"aa.ex.w{kind}.{g}",
                          writes_extra=sl_r)
            for tc in range(8):
                for kind, slot, dstap, dstres in (
                        ("v", 0, vtm_e, r_vtme), ("k", 1, ktm_g, r_ktm)):
                    sl_ap, sl_r = slabs[slot]
                    pi = (tc * 2 + slot) % 4
                    pairs = [(y[:, fc * 1024 + tc * 128: fc * 1024 + tc * 128 + 128],
                              sl_ap[:, fc * 512:(fc + 1) * 512])
                             for fc in range(16)]
                    self.mmg(ps[pi], pairs, sl_r + ry, self.psr[pi],
                             f"aa.ex.{kind}mm.{g}.{tc}")
                    dsl = dstap[:, tc * 512:(tc + 1) * 512]
                    if kind == "v":
                        self.dve((lambda a, b: lambda v: v.tensor_copy(a, b))(
                            dsl, ps[pi]), [self.psr[pi]], dstres,
                            f"aa.ex.vcp.{g}.{tc}")
                    else:
                        self.act((lambda a, b: lambda sc: sc.activation(
                            a, b, AF.Silu))(dsl, ps[pi]), [self.psr[pi]],
                            dstres, f"aa.ex.ksl.{g}.{tc}")
            for hh in range(4):
                h = g * 4 + hh
                for tc in range(8):
                    sl = ktm_g[:, tc * 512 + hh * 128: tc * 512 + hh * 128 + 128]
                    self.dve((lambda a, c2: lambda v: v.tensor_scalar_mul(
                        a, a, c2))(sl, s["scol"][:, 20 + 2 * hh + tc % 2:
                                                 21 + 2 * hh + tc % 2]),
                        [r_scol] + r_ktm, r_ktm, f"aa.ex.kkd.{h}.{tc}")
                kvf_h = kvf[:, h * 128:(h + 1) * 128]
                for B in range(4):
                    pairs = [(ktm_g[:, (2 * B + j) * 512 + hh * 128:
                                    (2 * B + j) * 512 + hh * 128 + 128],
                              vtm_e[:, (2 * B + j) * 512 + hh * 128:
                                    (2 * B + j) * 512 + hh * 128 + 128])
                             for j in range(2)]
                    self.mmg(ps[6][:, 0:128], pairs, r_ktm + r_vtme,
                             self.psr[6], f"aa.ex.upd.{h}.{B}")
                    if B == 0:
                        fn = (lambda a, b: lambda v: v.tensor_copy(a, b))(
                            kvf_h, ps[6][:, 0:128])
                    else:
                        fn = (lambda a, b, sc_: lambda v: v.scalar_tensor_tensor(
                            a, a, sc_, b, ALU.mult, ALU.add))(
                            kvf_h, ps[6][:, 0:128],
                            s["scol"][:, 5 + 4 * hh:6 + 4 * hh])
                    self.dve(fn, [self.psr[6], r_scol] + r_kvf, r_kvf,
                             f"aa.ex.kv.{h}.{B}")
                    if B < 3:
                        dsl = kvpre[:, h * 384 + B * 128: h * 384 + B * 128 + 128]
                        self.dve((lambda a, b: lambda v: v.tensor_copy(a, b))(
                            dsl, kvf_h), r_kvf, r_kvpre, f"aa.ex.pre.{h}.{B}")

        self._exchange_a(r_kvf, r_kvinf)
        self.dve((lambda a, b, c2: lambda v: v.tensor_scalar_mul(a, b, c2))(
            kvin_b, kvin_f, s["isodd"][:, 0:1]),
            r_kvinf + [self.R("isodd")], r_kvinb, "aa.kvinbf")

        Oar, r_O = self.carve("aaO", 49152, [128, 8192], BF16)
        ddt4, self.r_ddt4 = self.carve("aaddt", 16384, [128, 2048], BF16)
        qd4, self.r_qd4 = self.carve("aaqd4", 20480, [128, 1024], BF16)
        shat2, r_shat2 = self.carve("aash2", 22528, [128, 1024], BF16)
        qqd2, r_qqd2 = self.carve("aaqqd", 24576, [128, 512], BF16)
        qgd2, r_qgd2 = self.carve("aaqgd", 25600, [128, 512], BF16)
        sq_t, r_sqt = self.carve("aasq", 26624, [128, 256], BF16)
        vtm, r_vtm = self.carve("aavtm", 65536, [128, 8192], BF16)
        qk_t = [self.carve("aaq0", 27136, [128, 512], BF16),
                self.carve("aaq1", 28160, [128, 512], BF16),
                self.carve("aak0", 29184, [128, 512], BF16),
                self.carve("aak1", 30208, [128, 512], BF16)]

        for sub in range(2):
            S = sub * 512
            for vs in range(4):
                sl_ap, sl_r = slabs[vs % 2]
                src = t["wqkvuT"][:, 4096 + vs * 512: 4096 + (vs + 1) * 512] \
                    .rearrange("(fc p) n -> p fc n", p=128)
                self.load(sl_ap.rearrange("p (fc n) -> p fc n", fc=16), src,
                          f"aslab{vs % 2}", f"aa.m{sub}.wv.{vs}",
                          writes_extra=sl_r)
                for tc in range(4):
                    pi = (vs * 4 + tc) % 4
                    pairs = [(y[:, fc * 1024 + S + tc * 128:
                                fc * 1024 + S + tc * 128 + 128],
                              sl_ap[:, fc * 512:(fc + 1) * 512])
                             for fc in range(16)]
                    self.mmg(ps[pi], pairs, sl_r + ry, self.psr[pi],
                             f"aa.m{sub}.vmm.{vs}.{tc}")
                    dsl = vtm[:, tc * 2048 + vs * 512: tc * 2048 + (vs + 1) * 512]
                    self.dve((lambda a, b: lambda v: v.tensor_copy(a, b))(
                        dsl, ps[pi]), [self.psr[pi]], r_vtm,
                        f"aa.m{sub}.vcp.{vs}.{tc}")
            for g in range(4):
                group_prep(g, f"aa.m{sub}.g{g}", with_main=True, bank=6)
                sl_ap, sl_r = slabs[0]
                src_ = t["wqkvuT"][:, g * 512:(g + 1) * 512] \
                    .rearrange("(fc p) n -> p fc n", p=128)
                self.load(sl_ap.rearrange("p (fc n) -> p fc n", fc=16), src_,
                          "aslab0", f"aa.m{sub}.wq.{g}", writes_extra=sl_r)
                for hh in range(4):
                    h = 4 * g + hh
                    qa, r_qa = qk_t[h % 2]
                    ka, r_ka = qk_t[2 + h % 2]
                    pairs = [(sl_ap[:, kc * 512 + hh * 128:
                                    kc * 512 + (hh + 1) * 128],
                              y[:, kc * 1024 + S: kc * 1024 + S + 512])
                             for kc in range(16)]
                    self.mmg(ps[4], pairs, sl_r + ry, self.psr[4],
                             f"aa.m{sub}.qmm.{h}")
                    self.act((lambda a, b: lambda sc: sc.activation(
                        a, b, AF.Silu))(qa, ps[4]), [self.psr[4]], r_qa,
                        f"aa.m{sub}.qsl.{h}")
                    bi = h % 3
                    srck = t["wqkvuT"][:, 2048 + h * 128: 2048 + (h + 1) * 128] \
                        .rearrange("(kc q) n -> q kc n", q=128)
                    self.load(wst[bi].rearrange("q (kc n) -> q kc n", kc=16),
                              srck, f"wst{bi}", f"aa.m{sub}.wk.{h}")
                    kpairs = [(wst[bi][:, kc * 128:(kc + 1) * 128],
                               y[:, kc * 1024 + S: kc * 1024 + S + 512])
                              for kc in range(16)]
                    self.mmg(ps[5], kpairs, [r_w[bi]] + ry, self.psr[5],
                             f"aa.m{sub}.kmm.{h}")
                    self.act((lambda a, b: lambda sc: sc.activation(
                        a, b, AF.Silu))(ka, ps[5]), [self.psr[5]], r_ka,
                        f"aa.m{sub}.ksl.{h}")
                    for blk in range(2):
                        B = sub * 2 + blk
                        stp = ps[blk % 2]
                        qb = qa[:, blk * 256: blk * 256 + 256]
                        def st_fn(pp=stp, k0=ka[:, blk * 256: blk * 256 + 128],
                                  k1=ka[:, blk * 256 + 128: blk * 256 + 256],
                                  qq=qb):
                            def f(te):
                                te.matmul(pp[:, 0:256], k0, qq, start=True,
                                          stop=True, skip_group_check=True)
                                return te.matmul(pp[:, 256:512], k1, qq,
                                                 start=False, stop=True,
                                                 skip_group_check=True)
                            return f
                        self.P.op("pe", st_fn(), reads=r_ka + r_qa,
                                  writes=[self.psr[blk % 2]],
                                  tag=f"aa.st.{sub}.{h}.{blk}")
                        self.dve((lambda a, b, c2: lambda v: v.tensor_mul(a, b, c2))(
                            shat2[:, blk * 512:(blk + 1) * 512], stp[:, 0:512],
                            ddt4[:, hh * 512:(hh + 1) * 512]),
                            [self.psr[blk % 2]] + self.r_ddt4, r_shat2,
                            f"aa.shat.{sub}.{h}.{blk}")
                        self.dve((lambda a, b, c2: lambda v: v.tensor_mul(a, b, c2))(
                            qqd2[:, blk * 256:(blk + 1) * 256], qb,
                            qd4[:, hh * 256:(hh + 1) * 256]),
                            r_qa + self.r_qd4, r_qqd2,
                            f"aa.qqd.{sub}.{h}.{blk}")
                        self.dve((lambda a, b, c2: lambda v: v.tensor_scalar_mul(
                            a, b, c2))(qgd2[:, blk * 256:(blk + 1) * 256],
                                       qqd2[:, blk * 256:(blk + 1) * 256],
                                       s["scol"][:, 4 + 4 * hh + B:5 + 4 * hh + B]),
                            r_qqd2 + [r_scol], r_qgd2, f"aa.qgd.{sub}.{h}.{blk}")
                    for blk in range(2):
                        B = sub * 2 + blk
                        opi = 2 + blk % 2
                        pairs = [(vtm[:, (2 * blk + j) * 2048 + h * 128:
                                      (2 * blk + j) * 2048 + h * 128 + 128],
                                  shat2[:, blk * 512 + j * 256:
                                        blk * 512 + (j + 1) * 256])
                                 for j in range(2)]
                        reads = r_vtm + r_shat2 + r_qgd2 + r_qqd2 \
                            + r_kvinb + r_kvpre
                        if B > 0:
                            pairs.append((kvpre[:, h * 384 + (B - 1) * 128:
                                               h * 384 + B * 128],
                                          qqd2[:, blk * 256:(blk + 1) * 256]))
                        pairs.append((kvin_b[:, h * 128:(h + 1) * 128],
                                      qgd2[:, blk * 256:(blk + 1) * 256]))
                        self.mmg(ps[opi][:, 0:256], pairs, reads, self.psr[opi],
                                 f"aa.omm.{sub}.{h}.{blk}")
                    self._aa_evac(sub, h, Oar, r_O, sq_t, r_sqt)
            rows = s["rows"]
            r_rows = self.R("rows")
            self.act((lambda a, b, bb_=s["cbias"][0:1, 1:2]: lambda sc: sc.activation(
                a, b, AF.Ln, scale=1.0 / 2048.0, bias=bb_))(rows[:, 0:512],
                                                            ps[7][0:1, :]),
                [self.psr[7], self.R("cbias")], [r_rows], f"aa.rln.{sub}")
            self.act((lambda a, b: lambda sc: sc.activation(
                a, b, AF.Exp, scale=-0.5))(rows[:, 0:512], rows[:, 0:512]),
                [r_rows], [r_rows], f"aa.rex.{sub}")
            self.dve((lambda a, b: lambda v: v.tensor_copy(a, b))(
                self.s["aarb"], rows[0:1, 0:512]),
                [r_rows], self.r_aarb_pages, f"aa.rc.{sub}")
            def rbc_fn(te):
                te.matmul(ps[6][:, 0:256], s["ones_rb"],
                          self.s["aarb"][0:1, 0:256],
                          start=True, stop=True, skip_group_check=True)
                return te.matmul(ps[6][:, 256:512], s["ones_rb"],
                                 self.s["aarb"][0:1, 256:512],
                                 start=False, stop=True, skip_group_check=True)
            self.P.op("pe", rbc_fn,
                      reads=self.r_aarb_pages + [self.R("ones")],
                      writes=[self.psr[6]], tag=f"aa.rbc.{sub}")
            for h in range(16):
                bi = h % 3
                src = t["wqkvuT"][:, 6144 + h * 128: 6144 + (h + 1) * 128] \
                    .rearrange("(kc q) n -> q kc n", q=128)
                self.load(wst[bi].rearrange("q (kc n) -> q kc n", kc=16), src,
                          f"wst{bi}", f"aa.m{sub}.wu.{h}")
                pi = 4 + h % 2
                pairs = [(wst[bi][:, kc * 128:(kc + 1) * 128],
                          y[:, kc * 1024 + S: kc * 1024 + S + 512])
                         for kc in range(16)]
                self.mmg(ps[pi], pairs, [r_w[bi]] + ry, self.psr[pi],
                         f"aa.umm.{sub}.{h}")
                osl = Oar[:, h * 512:(h + 1) * 512]
                self.dve((lambda a, b: lambda v: v.tensor_mul(a, a, b))(
                    osl, ps[pi]), [self.psr[pi]] + r_O, r_O,
                    f"aa.ou.{sub}.{h}")
                self.dve((lambda a, b: lambda v: v.tensor_mul(a, a, b))(
                    osl, ps[6]), [self.psr[6]] + r_O, r_O,
                    f"aa.our.{sub}.{h}")
            for oc in range(16):
                bi = oc % 3
                src = t["woutT"][:, oc * 128:(oc + 1) * 128].rearrange(
                    "(kc q) n -> q kc n", q=128)
                self.load(wst[bi].rearrange("q (kc n) -> q kc n", kc=16), src,
                          f"wst{bi}", f"aa.m{sub}.wo.{oc}")
                pi = 4 + oc % 2
                pairs = [(wst[bi][:, hc * 128:(hc + 1) * 128],
                          Oar[:, hc * 512:(hc + 1) * 512]) for hc in range(16)]
                self.mmg(ps[pi], pairs, [r_w[bi]] + r_O, self.psr[pi],
                         f"aa.wout.{sub}.{oc}")
                xsl = x[:, oc * 1024 + S: oc * 1024 + S + 512]
                self.dve((lambda a, b: lambda v: v.tensor_add(a, a, b))(
                    xsl, ps[pi]), [self.psr[pi], rx[oc]], [rx[oc]],
                    f"aa.res.{sub}.{oc}")

    def _exchange_a(self, r_kvf, r_kvinf):
        s, t = self.s, self.t
        kvf, _ = self.carve("kvf2", 16384, [128, 2048], F32)
        kvin_f, _ = self.carve("kvin_f2", 24576, [128, 2048], F32)
        sems = self.xsems["a"]

        def blob(g):
            sd, sp_ = sems
            src3 = kvf.rearrange("d (h e) -> d h e", h=16)
            dst3 = kvin_f.rearrange("d (h e) -> d h e", h=16)
            with g.register("rparA") as rpar, g.register("rflagA") as rflag, \
                    g.register("rnonA") as rnon, g.register("rcntA") as rcnt:
                g.reg_load(rpar, s["pars"][0:1, 0:1])
                g.reg_load(rnon, s["pars"][0:1, 1:2])
                with g.If_eq(rpar, 0):
                    g.dma_start(out=t["sha"][0].rearrange("(h d) e -> d h e", d=128),
                                in_=src3).then_inc(sd, 16)
                    g.wait_ge(sd, 16)
                    g.dma_start(out=t["shf"][0, 0:1, :], in_=s["pars"]).then_inc(sd, 16)
                    g.wait_ge(sd, 32)
                with g.Else():
                    g.dma_start(out=t["sha"][1].rearrange("(h d) e -> d h e", d=128),
                                in_=src3).then_inc(sd, 16)
                    g.wait_ge(sd, 16)
                    g.dma_start(out=t["shf"][0, 1:2, :], in_=s["pars"]).then_inc(sd, 16)
                    g.wait_ge(sd, 32)
                g.reg_mov(rcnt, 0)
                g.reg_mov(rflag, 0)
                with g.If_eq(rpar, 0):
                    with g.While(lambda: g.snap(rflag) != g.snap(rnon)):
                        g.reg_add(rcnt, rcnt, 16)
                        g.dma_start(out=s["pars"][0:1, 4:8],
                                    in_=t["shf"][0, 1:2, 0:4]).then_inc(sp_, 16)
                        g.wait_ge(sp_, rcnt)
                        g.reg_load(rflag, s["pars"][0:1, 5:6])
                    g.dma_start(out=dst3,
                                in_=t["sha"][1].rearrange("(h d) e -> d h e", d=128)
                                ).then_inc(sd, 16)
                    g.wait_ge(sd, 48)
                with g.Else():
                    with g.While(lambda: g.snap(rflag) != g.snap(rnon)):
                        g.reg_add(rcnt, rcnt, 16)
                        g.dma_start(out=s["pars"][0:1, 4:8],
                                    in_=t["shf"][0, 0:1, 0:4]).then_inc(sp_, 16)
                        g.wait_ge(sp_, rcnt)
                        g.reg_load(rflag, s["pars"][0:1, 5:6])
                    g.dma_start(out=dst3,
                                in_=t["sha"][0].rearrange("(h d) e -> d h e", d=128)
                                ).then_inc(sd, 16)
                    g.wait_ge(sd, 48)
            return g.nop()

        self.P.op("sp", blob, reads=r_kvf + [self.R("pars")],
                  writes=r_kvinf + [self.R("pars_poll")], tag="xchgA")

    def emit_attn_b(self):
        P, s, ps, t = self.P, self.s, self.ps, self.t
        y, x = s["y"], s["x"]
        ry = [self.R(f"y{c}") for c in range(16)]
        rx = [self.R(f"x{c}") for c in range(16)]
        wst = [s[f"wst{i}"] for i in range(3)]
        r_w = [self.R(f"wst{i}") for i in range(3)]
        Kf, r_Kf = self.carve("bKf", 0, [128, 8192], BF16)
        Vf, r_Vf = self.carve("bVf", 16384, [128, 8192], BF16)
        Ob, r_Ob = self.carve("bO", 32768, [128, 8192], BF16)
        cosb, r_cos = self.carve("bcos", 49152, [128, 1024], BF16)
        sinb, r_sin = self.carve("bsin", 51200, [128, 1024], BF16)
        ctri_s, r_ctri = self.carve("bctri", 53248, [128, 2560], BF16)
        lmask_s, r_lm = self.carve("blm", 58368, [128, 2560], BF16)
        Ep = [self.carve("bEp0", 63488, [128, 1024], BF16),
              self.carve("bEp1", 65536, [128, 1024], BF16)]
        Ecp, r_Ecp = self.carve("bEcp", 67584, [128, 1024], BF16)
        qh_t, r_qh = self.carve("bqh", 69632, [128, 512], BF16)
        qro = [self.carve("bqro0", 70656, [128, 512], BF16),
               self.carve("bqro1", 71680, [128, 512], BF16)]
        t1_t, r_t1 = self.carve("bt1", 72704, [128, 512], BF16)
        sqq_t, r_sqq = self.carve("bsqq", 72704, [128, 512], BF16)
        Kown, r_Kown = self.carve("bKown", 0, [128, 4096], BF16)
        Vown, r_Vown = self.carve("bVown", 16384, [128, 4096], BF16)
        vslab, r_vsl = self.carve("bvslab", 32768, [128, 8192], BF16)
        kg_t, r_kg = self.carve("bkg", 53248, [128, 1024], BF16)
        bcols = self.sb("bcols", [128, 32], F32) if "bcols" not in s else s["bcols"]
        r_bcols = [self.R("bcols0"), self.R("bcols1")]
        brow = self.sb("brow", [1, 16], F32) if "brow" not in s else s["brow"]
        r_brow = self.R("brow")
        rows, r_rows = s["rows"], self.R("rows")
        r_scol = [self.R("scolB0"), self.R("scolB1")]

        self.load(cosb, t["cosf"], "bcos", "bb.cos", writes_extra=r_cos)
        self.load(sinb, t["sinf"], "bsin", "bb.sin", writes_extra=r_sin)

        for g in range(4):
            bi = g % 3
            src = t["wk"][:, g * 128:(g + 1) * 128].rearrange(
                "(kc q) n -> q kc n", q=128)
            self.load(wst[bi].rearrange("q (kc n) -> q kc n", kc=16), src,
                      f"wst{bi}", f"bb.wk.{g}")
            for hhalf in range(2):
                pi = hhalf
                pairs = [(wst[bi][:, kc * 128:(kc + 1) * 128],
                          y[:, kc * 1024 + hhalf * 512: kc * 1024 + hhalf * 512 + 512])
                         for kc in range(16)]
                self.mmg(ps[pi], pairs, [r_w[bi]] + ry, self.psr[pi],
                         f"bb.kmm.{g}.{hhalf}")
                ksl = kg_t[:, hhalf * 512:(hhalf + 1) * 512]
                self.dve((lambda a, b: lambda v: v.tensor_copy(a, b))(
                    ksl, ps[pi]), [self.psr[pi]], r_kg, f"bb.kcp.{g}.{hhalf}")
                self.mmg(ps[2 + hhalf], [(s["rt"], ksl)],
                         [self.R("rt")] + r_kg, self.psr[2 + hhalf],
                         f"bb.krot.{g}.{hhalf}")
                self.dve((lambda a, b, c2: lambda v: v.tensor_mul(a, b, c2))(
                    t1_t, ksl, cosb[:, hhalf * 512:(hhalf + 1) * 512]),
                    r_kg + r_cos, r_t1, f"bb.kc1.{g}.{hhalf}")
                dsl = Kown[:, g * 1024 + hhalf * 512: g * 1024 + (hhalf + 1) * 512]
                self.dve((lambda a, b, c2: lambda v: v.tensor_mul(a, b, c2))(
                    dsl, ps[2 + hhalf], sinb[:, hhalf * 512:(hhalf + 1) * 512]),
                    [self.psr[2 + hhalf]] + r_sin, r_Kown, f"bb.ks1.{g}.{hhalf}")
                self.dve((lambda a, b, c2: lambda v: v.tensor_add(a, a, c2))(
                    dsl, dsl, t1_t), r_Kown + r_t1, r_Kown,
                    f"bb.kro.{g}.{hhalf}")
        src = t["wv"].rearrange("(fc p) n -> p fc n", p=128)
        self.load(vslab.rearrange("p (fc n) -> p fc n", fc=16), src,
                  "bvslab", "bb.wv", writes_extra=r_vsl)
        for tc in range(8):
            pi = 4 + tc % 2
            pairs = [(y[:, fc * 1024 + tc * 128: fc * 1024 + tc * 128 + 128],
                      vslab[:, fc * 512:(fc + 1) * 512]) for fc in range(16)]
            self.mmg(ps[pi], pairs, r_vsl + ry, self.psr[pi], f"bb.vmm.{tc}")
            dsl = Vown[:, tc * 512:(tc + 1) * 512]
            self.dve((lambda a, b: lambda v: v.tensor_copy(a, b))(
                dsl, ps[pi]), [self.psr[pi]], r_Vown, f"bb.vcp.{tc}")

        self._exchange_b(r_Kown, r_Vown, r_Kf, r_Vf, Kf, Vf)

        for g in range(4):
            for q4 in range(4):
                ksl = Kf[:, g * 2048 + q4 * 512: g * 2048 + (q4 + 1) * 512]
                self.dve((lambda a, b: lambda v: v.tensor_mul(a, b, b))(
                    sqq_t, ksl), r_Kf, r_sqq, f"bb.ksq.{g}.{q4}")
                self.mmg(ps[6][0:1, :], [(s["ones_cb"], sqq_t)],
                         r_sqq + [self.R("ones")], self.psr[6],
                         f"bb.knm.{g}.{q4}")
                self.dve((lambda b, i=g * 4 + q4: lambda v: v.reduce_max(
                    rows[:, 1024 + i:1025 + i], b, axis=AX.X))(ps[6][0:1, :]),
                    [self.psr[6]], [r_rows], f"bb.kmx.{g}.{q4}")
        self.dve((lambda a: lambda v: v.reduce_max(
            brow[:, 0:1], a, axis=AX.X))(rows[0:1, 1024:1040]),
            [r_rows], [r_brow], "bb.kmx2")

        for st_ in range(2):
            S = st_ * 512
            self.load(ctri_s, t["ctri"][:, st_ * 2560:(st_ + 1) * 2560],
                      "bctri", f"bb.ctri.{st_}", eng="sp", writes_extra=r_ctri)
            self.load(lmask_s, t["lmask"][:, st_ * 2560:(st_ + 1) * 2560],
                      "blm", f"bb.lm.{st_}", eng="sp", writes_extra=r_lm)
            n_main = 7 if st_ == 0 else 11

            def prep(h):
                par = h % 2
                A, B = ps[2 + 2 * par], ps[3 + 2 * par]
                rA, rB = self.psr[2 + 2 * par], self.psr[3 + 2 * par]
                qro_t, r_qro = qro[par]
                bi = h % 3
                src = t["wq"][:, h * 128:(h + 1) * 128].rearrange(
                    "(kc q) n -> q kc n", q=128)
                self.load(wst[bi].rearrange("q (kc n) -> q kc n", kc=16), src,
                          f"wst{bi}", f"bb.wq.{st_}.{h}")
                pairs = [(wst[bi][:, kc * 128:(kc + 1) * 128],
                          y[:, kc * 1024 + S: kc * 1024 + S + 512])
                         for kc in range(16)]
                self.mmg(A, pairs, [r_w[bi]] + ry, rA, f"bb.qmm.{st_}.{h}")
                self.dve((lambda a, b: lambda v: v.tensor_copy(a, b))(
                    qh_t, A), [rA], r_qh, f"bb.qcp.{st_}.{h}")
                self.mmg(B, [(s["rt"], qh_t)], [self.R("rt")] + r_qh, rB,
                         f"bb.qrot.{st_}.{h}")
                self.dve((lambda a, b, c2: lambda v: v.tensor_mul(a, b, c2))(
                    t1_t, qh_t, cosb[:, S:S + 512]), r_qh + r_cos, r_t1,
                    f"bb.qc1.{st_}.{h}")
                self.dve((lambda a, b, c2: lambda v: v.tensor_mul(a, b, c2))(
                    qro_t, B, sinb[:, S:S + 512]),
                    [rB] + r_sin, r_qro, f"bb.qs1.{st_}.{h}")
                self.dve((lambda a, b, c2: lambda v: v.tensor_add(a, a, c2))(
                    qro_t, qro_t, t1_t), r_qro + r_t1, r_qro,
                    f"bb.qro.{st_}.{h}")
                self.dve((lambda a, b: lambda v: v.tensor_mul(a, b, b))(
                    sqq_t, qro_t), r_qro, r_sqq, f"bb.qsq.{st_}.{h}")
                self.mmg(A[0:1, :], [(s["ones_cb"], sqq_t)],
                         r_sqq + [self.R("ones")], rA, f"bb.qnm.{st_}.{h}")
                c0 = 2 + par * 4
                self.dve((lambda b, c1=c0: lambda v: v.reduce_max(
                    brow[:, c1:c1 + 1], b, axis=AX.X))(A[0:1, :]),
                    [rA], [r_brow], f"bb.qmx.{st_}.{h}")
                self.dve((lambda a, b, c2: lambda v: v.tensor_mul(a, b, c2))(
                    brow[:, c0 + 1:c0 + 2], brow[:, c0:c0 + 1], brow[:, 0:1]),
                    [r_brow], [r_brow], f"bb.negm0.{st_}.{h}")
                self.act((lambda a, b: lambda sc: sc.activation(a, b, AF.Ln))(
                    brow[:, c0 + 2:c0 + 3], brow[:, c0 + 1:c0 + 2]),
                    [r_brow], [r_brow], f"bb.negmln.{st_}.{h}")
                self.act((lambda a, b: lambda sc: sc.activation(
                    a, b, AF.Exp, scale=0.5))(brow[:, c0 + 2:c0 + 3],
                                              brow[:, c0 + 2:c0 + 3]),
                    [r_brow], [r_brow], f"bb.negmex.{st_}.{h}")
                self.dve((lambda a, b: lambda v: v.tensor_scalar_mul(
                    a, b, -1.05 * SCALE_B))(brow[:, c0 + 3:c0 + 4],
                                            brow[:, c0 + 2:c0 + 3]),
                    [r_brow], [r_brow], f"bb.negm2.{st_}.{h}")
                self.mmg(B[:, 0:1], [(s["ones_rf"], brow[0:1, c0 + 3:c0 + 4])],
                         [r_brow, self.R("ones")], rB, f"bb.nmb.{st_}.{h}")
                self.dve((lambda a, b: lambda v: v.tensor_copy(a, b))(
                    s["scol"][:, 0 + 3 * par:1 + 3 * par], B[:, 0:1]), [rB],
                    [r_scol[par]], f"bb.nmc.{st_}.{h}")
                self.dve((lambda a, b, c2: lambda v: v.tensor_scalar_add(
                    a, b, c2))(bcols[:, par * 16: par * 16 + 16],
                               s["gpen"][:, st_ * 16: st_ * 16 + 16],
                               s["scol"][:, 0 + 3 * par:1 + 3 * par]),
                    [self.R("gpen"), r_scol[par]], [r_bcols[par]],
                    f"bb.bcols.{st_}.{h}")

            def kcloop(h):
                par = h % 2
                g = h // 4
                A, B = ps[2 + 2 * par], ps[3 + 2 * par]
                rA, rB = self.psr[2 + 2 * par], self.psr[3 + 2 * par]
                qro_t, r_qro = qro[par]
                pend = []
                state = {"n": 0, "nl": 0}

                def flush():
                    for f in pend:
                        f()
                    pend.clear()

                def spair_main(kp, npair):
                    k0 = 2 * kp
                    def f(te):
                        r = te.matmul(ps[0], Kf[:, g * 2048 + k0 * 128:
                                              g * 2048 + (k0 + 1) * 128],
                                      qro_t, start=True, stop=True,
                                      skip_group_check=True)
                        if npair == 2:
                            r = te.matmul(ps[1], Kf[:, g * 2048 + (k0 + 1) * 128:
                                                    g * 2048 + (k0 + 2) * 128],
                                          qro_t, start=True, stop=True,
                                          skip_group_check=True)
                        return r
                    self.P.op("pe", f, reads=r_Kf + r_qro,
                              writes=[self.psr[0], self.psr[1]],
                              tag=f"bb.smm.{st_}.{h}.{kp}")

                for kp in range((n_main + 1) // 2):
                    kc0 = 2 * kp
                    npair = 2 if kc0 + 1 < n_main else 1
                    wid = 512 * npair
                    spair_main(kp, npair)
                    Ea, r_Ea = Ep[kp % 2]
                    self.act((lambda a, b, bc, w=wid: lambda sc: sc.activation(
                        a[:, 0:w], b[:, 0:w], AF.Exp, bias=bc, scale=SCALE_B))(
                        Ea, self.ps_pair,
                        bcols[:, par * 16 + kc0: par * 16 + kc0 + 1]),
                        [self.psr[0], self.psr[1], r_bcols[par]], r_Ea,
                        f"bb.exp.{st_}.{h}.{kp}")
                    flush()
                    def zpv(kc0=kc0, Ea=Ea, r_Ea=r_Ea, npair=npair):
                        for kk in range(npair):
                            kc = kc0 + kk
                            Esl = Ea[:, kk * 512:(kk + 1) * 512]
                            self.mmg(ps[6][0:1, :], [(s["ones_cb"], Esl)],
                                     r_Ea + [self.R("ones")], self.psr[6],
                                     f"bb.zg.{st_}.{h}.{kc}",
                                     accumulate=(state["n"] > 0))
                            self.mmg(A, [(Vf[:, kc * 512 + g * 128:
                                             kc * 512 + (g + 1) * 128], Esl)],
                                     r_Vf + r_Ea, rA,
                                     f"bb.pv.{st_}.{h}.{kc}",
                                     accumulate=(state["n"] > 0))
                            state["n"] += 1
                    pend.append(zpv)
                loc_pairs = [(0, 1), (2, 3), (4,)]
                for pi_, js in enumerate(loc_pairs):
                    j0 = js[0]
                    wid = 512 * len(js)
                    def sl_fn(j0=j0, njs=len(js), st=st_):
                        def f(te):
                            r = None
                            for u in range(njs):
                                jj = 4 * st + j0 + u
                                r = te.matmul(ps[u], Kf[:, g * 2048 + (7 + jj) * 128:
                                                        g * 2048 + (8 + jj) * 128],
                                              qro_t, start=True, stop=True,
                                              skip_group_check=True)
                            return r
                        return f
                    self.P.op("pe", sl_fn(), reads=r_Kf + r_qro,
                              writes=[self.psr[0], self.psr[1]],
                              tag=f"bb.lsmm.{st_}.{h}.{pi_}")
                    Ea, r_Ea = Ep[pi_ % 2]
                    self.act((lambda a, b, bc, w=wid: lambda sc: sc.activation(
                        a[:, 0:w], b[:, 0:w], AF.Exp, bias=bc, scale=SCALE_B))(
                        Ea, self.ps_pair, s["scol"][:, 0 + 3 * par:1 + 3 * par]),
                        [self.psr[0], self.psr[1], r_scol[par]], r_Ea,
                        f"bb.lexp.{st_}.{h}.{pi_}")
                    flush()
                    self.dve((lambda a, b, c2, w=wid: lambda v: v.tensor_mul(
                        a[:, 0:w], b[:, 0:w], c2))(
                        Ecp, Ea, ctri_s[:, j0 * 512: j0 * 512 + wid]),
                        r_Ea + r_ctri, r_Ecp, f"bb.ec.{st_}.{h}.{pi_}")
                    self.dve((lambda a, c2, w=wid: lambda v: v.tensor_mul(
                        a[:, 0:w], a[:, 0:w], c2))(
                        Ea, lmask_s[:, j0 * 512: j0 * 512 + wid]),
                        r_Ea + r_lm, r_Ea, f"bb.el.{st_}.{h}.{pi_}")
                    def zpvl(js=js, Ea=Ea, r_Ea=r_Ea):
                        for u, j in enumerate(js):
                            jj = 4 * st_ + j
                            Ecs = Ecp[:, u * 512:(u + 1) * 512]
                            Els = Ea[:, u * 512:(u + 1) * 512]
                            vsl = Vf[:, (7 + jj) * 512 + g * 128:
                                     (7 + jj) * 512 + (g + 1) * 128]
                            self.mmg(ps[6][0:1, :], [(s["ones_cb"], Ecs)],
                                     r_Ecp + [self.R("ones")], self.psr[6],
                                     f"bb.lzg.{st_}.{h}.{j}",
                                     accumulate=(state["n"] > 0))
                            state["n"] += 1
                            self.mmg(ps[7][0:1, :], [(s["ones_cb"], Els)],
                                     r_Ea + [self.R("ones")], self.psr[7],
                                     f"bb.lzl.{st_}.{h}.{j}",
                                     accumulate=(state["nl"] > 0))
                            self.mmg(A, [(vsl, Ecs)], r_Vf + r_Ecp, rA,
                                     f"bb.lpv.{st_}.{h}.{j}", accumulate=True)
                            self.mmg(B, [(vsl, Els)], r_Vf + r_Ea, rB,
                                     f"bb.lpvl.{st_}.{h}.{j}",
                                     accumulate=(state["nl"] > 0))
                            state["nl"] += 1
                    pend.append(zpvl)
                flush()
                par4 = par * 512
                self.dve((lambda a, b: lambda v: v.tensor_copy(a, b))(
                    rows[:, par4:par4 + 512], ps[6][0:1, :]),
                    [self.psr[6]], [r_rows], f"bb.zge.{st_}.{h}")
                self.dve((lambda a, b: lambda v: v.tensor_copy(a, b))(
                    rows[:, 1024 + par4:1024 + par4 + 512], ps[7][0:1, :]),
                    [self.psr[7]], [r_rows], f"bb.zle.{st_}.{h}")

            def tail(h):
                par = h % 2
                A, B = ps[2 + 2 * par], ps[3 + 2 * par]
                rA, rB = self.psr[2 + 2 * par], self.psr[3 + 2 * par]
                par4 = par * 512
                zg = rows[:, par4:par4 + 512]
                zl = rows[:, 1024 + par4:1024 + par4 + 512]
                self.dve((lambda a, b: lambda v: v.tensor_copy(a, b))(
                    t1_t, A), [rA], r_t1, f"bb.ouc.{st_}.{h}")
                self.act((lambda a, b: lambda sc: sc.activation(
                    a, b, AF.Ln))(zg, zg), [r_rows], [r_rows],
                    f"bb.cgl.{st_}.{h}")
                self.act((lambda a, b, bb_=s["cbias"][0:1, 2:3]: lambda sc:
                          sc.activation(a, b, AF.Exp, scale=-1.0, bias=bb_))(
                    zg, zg), [r_rows, self.R("cbias")], [r_rows],
                    f"bb.cge.{st_}.{h}")
                self.act((lambda a, b: lambda sc: sc.activation(
                    a, b, AF.Ln))(zl, zl), [r_rows], [r_rows],
                    f"bb.cll.{st_}.{h}")
                self.act((lambda a, b, bb_=s["cbias"][0:1, 2:3]: lambda sc:
                          sc.activation(a, b, AF.Exp, scale=-1.0, bias=bb_))(
                    zl, zl), [r_rows, self.R("cbias")], [r_rows],
                    f"bb.cle.{st_}.{h}")
                self.mmg(A, [(s["ones_rf"], zg[0:1, :])],
                         [r_rows, self.R("ones")] + r_t1, rA,
                         f"bb.cgb.{st_}.{h}")
                self.dve((lambda a, b, c2: lambda v: v.tensor_mul(a, b, c2))(
                    qh_t, t1_t, A), r_t1 + [rA], r_qh,
                    f"bb.on1.{st_}.{h}")
                self.dve((lambda a, b: lambda v: v.tensor_copy(a, b))(
                    t1_t, B), [rB], r_t1, f"bb.olc.{st_}.{h}")
                self.mmg(B, [(s["ones_rf"], zl[0:1, :])],
                         [r_rows, self.R("ones")] + r_t1, rB,
                         f"bb.clb.{st_}.{h}")
                self.dve((lambda a, b, c2: lambda v: v.tensor_mul(a, a, c2))(
                    t1_t, t1_t, B), r_t1 + [rB], r_t1,
                    f"bb.on2.{st_}.{h}")
                osl = Ob[:, h * 512:(h + 1) * 512]
                self.dve((lambda a, b, c2: lambda v: v.tensor_add(a, b, c2))(
                    osl, qh_t, t1_t), r_qh + r_t1, r_Ob,
                    f"bb.ow.{st_}.{h}")

            for h in range(16):
                prep(h)
                kcloop(h)
                if h > 0:
                    tail(h - 1)
            tail(15)
            if st_ == 0:
                self.tap("bO0", Ob, [128, 8192], BF16,
                         [self.R(f"pg{i}") for i in range(16, 24)])
            for oc in range(16):
                bi = oc % 3
                src = t["wo"][:, oc * 128:(oc + 1) * 128].rearrange(
                    "(kc q) n -> q kc n", q=128)
                self.load(wst[bi].rearrange("q (kc n) -> q kc n", kc=16), src,
                          f"wst{bi}", f"bb.wo.{st_}.{oc}")
                pi = 2 + oc % 2
                pairs = [(wst[bi][:, hc * 128:(hc + 1) * 128],
                          Ob[:, hc * 512:(hc + 1) * 512]) for hc in range(16)]
                self.mmg(ps[pi], pairs, [r_w[bi]] + r_Ob, self.psr[pi],
                         f"bb.womm.{st_}.{oc}")
                xsl = x[:, oc * 1024 + S: oc * 1024 + S + 512]
                self.dve((lambda a, b: lambda v: v.tensor_add(a, a, b))(
                    xsl, ps[pi]), [self.psr[pi], rx[oc]], [rx[oc]],
                    f"bb.res.{st_}.{oc}")

    def _exchange_b(self, r_Kown, r_Vown, r_Kf, r_Vf, Kf, Vf):
        s, t = self.s, self.t
        Kown, _ = self.carve("bKown2", 0, [128, 4096], BF16)
        Vown, _ = self.carve("bVown2", 16384, [128, 4096], BF16)
        sems = self.xsems["b"]

        def blob(g):
            sd, sp_ = sems
            ko3 = Kown.rearrange("d (g t) -> d g t", g=4)
            vo3 = Vown.rearrange("p (tc v) -> p tc v", tc=8)
            kf3 = Kf.rearrange("d (g t) -> d g t", g=4)
            vf3 = Vf.rearrange("p (tc v) -> p tc v", tc=16)
            shk = [t["shb_k"][i].rearrange("(g d) t -> d g t", d=128)
                   for i in range(2)]
            shv = [t["shb_v"][i].rearrange("(tc p) v -> p tc v", p=128)
                   for i in range(2)]
            with g.register("rparB") as rpar, g.register("rflagB") as rflag, \
                    g.register("rnonB") as rnon, g.register("rcntB") as rcnt:
                g.reg_load(rpar, s["pars"][0:1, 0:1])
                g.reg_load(rnon, s["pars"][0:1, 2:3])
                for p in range(2):
                    ctx = g.If_eq(rpar, 0) if p == 0 else g.Else()
                    with ctx:
                        g.dma_start(out=shk[p], in_=ko3).then_inc(sd, 16)
                        g.dma_start(out=shv[p], in_=vo3).then_inc(sd, 16)
                        g.wait_ge(sd, 32)
                        g.dma_start(out=t["shf"][1, p:p + 1, :],
                                    in_=s["pars"]).then_inc(sd, 16)
                        g.wait_ge(sd, 48)
                g.reg_mov(rcnt, 0)
                g.reg_mov(rflag, 0)
                for p in range(2):
                    o = 1 - p
                    ctx = g.If_eq(rpar, 0) if p == 0 else g.Else()
                    with ctx:
                        with g.While(lambda: g.snap(rflag) != g.snap(rnon)):
                            g.reg_add(rcnt, rcnt, 16)
                            g.dma_start(out=s["pars"][0:1, 4:8],
                                        in_=t["shf"][1, o:o + 1, 0:4]).then_inc(sp_, 16)
                            g.wait_ge(sp_, rcnt)
                            g.reg_load(rflag, s["pars"][0:1, 6:7])
                        g.dma_start(out=kf3[:, :, 0:1024],
                                    in_=shk[o]).then_inc(sd, 16)
                        g.dma_start(out=kf3[:, :, 1024:2048],
                                    in_=shk[p]).then_inc(sd, 16)
                        g.dma_start(out=vf3[:, 0:8, :],
                                    in_=shv[o]).then_inc(sd, 16)
                        g.dma_start(out=vf3[:, 8:16, :],
                                    in_=shv[p]).then_inc(sd, 16)
                        g.wait_ge(sd, 48 + 4 * 16)
            return g.nop()

        self.P.op("sp", blob, reads=r_Kown + r_Vown + [self.R("pars")],
                  writes=r_Kf + r_Vf + [self.R("pars_poll")],
                  tag="xchgB")

    def emit_store(self):
        s, t = self.s, self.t
        rx = [self.R(f"x{c}") for c in range(16)]
        o3 = t["out"].rearrange("(c p) t -> p c t", p=128)
        so = self.P.op("sp", (lambda d, src: lambda g: g.dma_start(out=d, in_=src))(
            o3, s["x"].rearrange("p (c t) -> p c t", c=16)),
            reads=rx, dma=True, tag="store", slot="store")
        self.P.op("sp", lambda g: g.nop(),
                  deps=[so] + self.tap_stores, tag="final")

    def build(self):
        self.declare()
        with contextlib.ExitStack() as stack:
            self.xsems = {
                "a": (stack.enter_context(self.nc.semaphore("xad")),
                      stack.enter_context(self.nc.semaphore("xap"))),
                "b": (stack.enter_context(self.nc.semaphore("xbd")),
                      stack.enter_context(self.nc.semaphore("xbp"))),
            }
            self.emit_setup()
            self.emit_ln(1)
            self.emit_attn_a()
            self.tap("x1", self.s["x"], [128, 16 * 1024], F32,
                     [self.R(f"x{c}") for c in range(16)])
            if self.n_sub >= 2:
                self.emit_ln(2)
                self.emit_ffn(1)
                self.tap("x2", self.s["x"], [128, 16 * 1024], F32,
                         [self.R(f"x{c}") for c in range(16)])
            if self.n_sub >= 3:
                self.emit_ln(3)
                self.emit_attn_b()
                self.tap("x3", self.s["x"], [128, 16 * 1024], F32,
                         [self.R(f"x{c}") for c in range(16)])
                self.tap("bO", self.s["bO"], [128, 8192], BF16,
                         [self.R(f"pg{i}") for i in range(16, 24)])
            if self.n_sub >= 4:
                self.emit_ln(4)
                self.emit_ffn(2)
            self.emit_store()
            self.P.finalize(paranoid=bool(os.environ.get("KFW_PARANOID")))
            sems = {nm: stack.enter_context(self.nc.semaphore(nm.replace(".", "_")))
                    for nm in self.P.sem_names()}
            block = stack.enter_context(self.nc.Block())

            @block.tensor
            def _(te):
                self.P.emit_engine("pe", te, sems)

            @block.scalar
            def _(sc):
                self.P.emit_engine("act", sc, sems)

            @block.vector
            def _(v):
                self.P.emit_engine("dve", v, sems)

            @block.gpsimd
            def _(g):
                self.P.emit_engine("pool", g, sems)

            @block.sync
            def _(sp):
                self.P.emit_engine("sp", sp, sems)
        return self.nc


def build_nc(n_sub=None, taps=None, detect=True, private_shared=False):
    nc = bass.Bass("TRN2", target_bir_lowering=False, debug=False,
                   detect_race_conditions=detect)
    kb = KB(nc, n_sub=N_SUB if n_sub is None else n_sub,
            taps=TAPS if taps is None else taps, private_shared=private_shared)
    kb.build()
    return nc


_NC_CACHE = {}


def make_inputs(inputs_full, nonce):
    x_full = np.ascontiguousarray(
        np.asarray(inputs_full["L0_attn_a_x"], np.float32).transpose(0, 2, 1))
    wqkvuT = np.ascontiguousarray(np.asarray(inputs_full["L0_attn_a_W_qkvu"], np.float32).T)
    woutT = np.ascontiguousarray(np.asarray(inputs_full["L0_attn_a_W_out"], np.float32).T)
    w0T = np.ascontiguousarray(np.asarray(inputs_full["L0_ffn_W0"], np.float32).T)
    w1T = np.ascontiguousarray(np.asarray(inputs_full["L0_ffn_W1"], np.float32).T)
    b0 = np.asarray(inputs_full["L0_ffn_bias0"], np.float32)
    b1 = np.asarray(inputs_full["L0_ffn_bias1"], np.float32)
    shared = {
        "wqkvuT": wqkvuT, "woutT": woutT,
        "wq": np.asarray(inputs_full["L0_attn_b_wq"], np.float32),
        "wk": np.asarray(inputs_full["L0_attn_b_wk"], np.float32),
        "wv": np.asarray(inputs_full["L0_attn_b_wv"], np.float32),
        "wo": np.asarray(inputs_full["L0_attn_b_wo"], np.float32),
        "w0T": w0T, "w1T": w1T,
        "b0c": np.ascontiguousarray(b0.reshape(64, 128).T),
        "b1c": np.ascontiguousarray(b1.reshape(16, 128).T),
        "slope_row": np.asarray(inputs_full["L0_attn_a_slope_rate"],
                                np.float32).reshape(1, 16),
        "cbias": np.array([[1e-5, 1e-6, LN05, 0.0]], np.float32),
    }
    iota_row, kd_col, bd_cols, idxt, rt = _host_consts()
    cos_f, sin_f = _rope_tables()
    consts = {"iota_row": iota_row, "kd_col": kd_col, "bd_cols": bd_cols,
              "idxt": idxt, "rt": rt}
    in_maps = []
    for c in range(8):
        b, par = c // 2, c % 2
        t0 = par * 1024
        gpen, ctri, lmask = _attnb_masks(par)
        m = dict(shared)
        m.update(consts)
        m["x"] = np.ascontiguousarray(x_full[b][:, t0:t0 + 1024])
        m["cosf"] = np.ascontiguousarray(cos_f[:, t0:t0 + 1024])
        m["sinf"] = np.ascontiguousarray(sin_f[:, t0:t0 + 1024])
        m["isodd"] = np.full((128, 1), float(par), np.float32)
        m["pars"] = np.array([[par, nonce + 1, nonce + 2, 0, 0, 0, 0, 0]],
                             np.int32)
        m["gpen"] = gpen
        m["ctri"] = ctri
        m["lmask"] = lmask
        in_maps.append(m)
    return in_maps


def kernel(**inputs):
    import time
    key = (N_SUB, tuple(sorted(TAPS)))
    if key not in _NC_CACHE:
        _NC_CACHE[key] = build_nc()
    nc = _NC_CACHE[key]
    nonce = (int(time.time() * 1000) & 0x3FFFFFF) * 16 + 3
    in_maps = make_inputs(inputs, nonce)
    res = run_bass_kernel_spmd(nc, in_maps, core_ids=list(range(8)))
    out = np.zeros((4, 2048, 2048), np.float32)
    for c in range(8):
        b, par = c // 2, c % 2
        shard = np.asarray(res.results[c]["out"])
        out[b, par * 1024:(par + 1) * 1024, :] = shard.T
    return out


# version 07
# speedup vs baseline: 1.3347x; 1.0414x over previous
import contextlib
import os
import numpy as np
import ml_dtypes
import concourse.bass as bass
import concourse.mybir as mybir
from concourse.bass_utils import run_bass_kernel_spmd

F32 = mybir.dt.float32
BF16 = mybir.dt.bfloat16
I32 = mybir.dt.int32
AF = mybir.ActivationFunctionType
ALU = mybir.AluOpType
AX = mybir.AxisListType

SCALE_B = float(1.0 / np.sqrt(128.0))
LN05 = float(np.log(0.5))

N_SUB = 4
TAPS = ()

ENGINES = ("pe", "act", "dve", "pool", "sp")


class Op:
    __slots__ = ("eng", "fn", "deps", "dma", "idx", "cum", "sem", "needed",
                 "tag")

    def __init__(self, eng, fn, deps, dma, idx, tag, slot=None):
        self.eng = eng
        self.fn = fn
        self.deps = deps
        self.dma = dma
        self.idx = idx
        self.cum = None
        if dma:
            assert slot is not None, f"dma op {tag} needs slot="
            self.sem = f"d.{slot}"
        else:
            self.sem = eng
        self.needed = False
        self.tag = tag


class Res:
    __slots__ = ("writer", "readers", "name")

    def __init__(self, name=""):
        self.writer = None
        self.readers = []
        self.name = name


class Prog:
    def __init__(self):
        self.ops = {e: [] for e in ENGINES}
        self.n = 0

    def op(self, eng, fn, reads=(), writes=(), deps=(), dma=False, tag="",
           slot=None):
        dd = [d for d in deps if d is not None]
        for r in reads:
            if r.writer is not None:
                dd.append(r.writer)
        for w in writes:
            if w.writer is not None:
                dd.append(w.writer)
            dd.extend(w.readers)
        if dma and slot is None and writes:
            slot = writes[0].name
        o = Op(eng, fn, dd, dma, len(self.ops[eng]), tag, slot=slot)
        self.ops[eng].append(o)
        self.n += 1
        for r in reads:
            r.readers.append(o)
        for w in writes:
            w.writer = o
            w.readers = []
        return o

    def finalize(self, paranoid=False):
        self.paranoid = paranoid
        for e in ENGINES:
            for o in self.ops[e]:
                if o.dma:
                    o.needed = True
                for d in o.deps:
                    if d.eng != o.eng or d.dma or paranoid:
                        d.needed = True
        counts = {}
        for e in ENGINES:
            for o in self.ops[e]:
                if o.needed:
                    amt = 16 if o.dma else 1
                    counts[o.sem] = counts.get(o.sem, 0) + amt
                    o.cum = counts[o.sem]
        self.sem_totals = counts
        self._simulate()

    def _simulate(self):
        pos = {e: 0 for e in ENGINES}
        vals = {s: 0 for s in self.sem_totals}
        progress = True
        while progress:
            progress = False
            for e in ENGINES:
                while pos[e] < len(self.ops[e]):
                    o = self.ops[e][pos[e]]
                    ok = True
                    for d in o.deps:
                        if d.eng == e and not d.dma:
                            continue
                        if vals.get(d.sem, 0) < d.cum:
                            ok = False
                            break
                    if not ok:
                        break
                    if o.needed:
                        vals[o.sem] = vals.get(o.sem, 0) + (16 if o.dma else 1)
                    pos[e] += 1
                    progress = True
        stuck = {e: len(self.ops[e]) - pos[e] for e in ENGINES
                 if pos[e] < len(self.ops[e])}
        if stuck:
            for e in ENGINES:
                if pos[e] < len(self.ops[e]):
                    o = self.ops[e][pos[e]]
                    info = [(d.eng, d.tag, d.cum, vals.get(d.sem, 0))
                            for d in o.deps if not (d.eng == e and not d.dma)]
                    raise RuntimeError(
                        f"DEADLOCK: engine {e} stuck at op {pos[e]} "
                        f"tag={o.tag} deps={info}; remaining={stuck}")

    def emit_engine(self, e, eng, sem_handles):
        hiw = {}
        paranoid = getattr(self, "paranoid", False)
        for o in self.ops[e]:
            for d in o.deps:
                if d.eng == e and not d.dma and not paranoid:
                    continue
                if hiw.get(d.sem, 0) >= d.cum:
                    continue
                eng.wait_ge(sem_handles[d.sem], d.cum)
                hiw[d.sem] = d.cum
            inst = o.fn(eng)
            if o.needed:
                if inst is None:
                    raise RuntimeError(
                        f"op {o.tag} on {e} needs inc but returned None")
                inst.then_inc(sem_handles[o.sem], 16 if o.dma else 1)

    def sem_names(self):
        return sorted(self.sem_totals)



def _host_consts():
    arr = np.arange(1, 257, dtype=np.float32)
    iota_row = arr[None, :]
    kd_col = np.zeros((128, 2), np.float32)
    for sub in range(2):
        kd_col[:, sub] = 255.0 - (sub * 128 + np.arange(128))
    bd_cols = np.zeros((128, 4), np.float32)
    for i in range(4):
        bd_cols[:, i] = 256.0 * i
    k_ = np.arange(256)[:, None]
    q_ = np.arange(256)[None, :]
    idxt_full = np.where(q_ >= k_, (q_ - k_).astype(np.float32), 1.0e6)
    idxt = np.zeros((128, 512), np.float32)
    idxt[:, 0:256] = idxt_full[0:128]
    idxt[:, 256:512] = idxt_full[128:256]
    R = np.zeros((128, 128), np.float32)
    for d in range(64):
        R[d, 64 + d] = -1.0
        R[64 + d, d] = 1.0
    rt = np.ascontiguousarray(R.T)
    return iota_row, kd_col, bd_cols, idxt, rt


def _rope_tables(T_total=2048, D=128, base=10000.0):
    inv = 1.0 / base ** (np.arange(0, D, 2, dtype=np.float32) / D)
    t = np.arange(T_total, dtype=np.float32)
    freqs = t[:, None] * inv[None, :]
    emb = np.concatenate([freqs, freqs], 1)
    return np.cos(emb).T.astype(np.float32), np.sin(emb).T.astype(np.float32)


def _attnb_masks(parity, W=256):
    t0 = parity * 1024
    gpen = np.zeros((128, 32), np.float32)
    gpen[:] = -1.0e30
    ctri = np.zeros((2, 5, 128, 512), np.float32)
    lmask = np.zeros((2, 5, 128, 512), np.float32)
    for s in range(2):
        G0 = t0 + 512 * s
        n_main = 7 if s == 0 else 11
        for r in range(n_main):
            absk = (1 - parity) * 8 + r if r < 8 else parity * 8 + (r - 8)
            if (absk + 1) * 128 <= G0:
                gpen[:, s * 16 + r] = 0.0
        for j in range(5):
            jj = 4 * s + j
            kk = np.arange(128)
            gkey = t0 + (jj - 1) * 128 + kk
            qq = np.arange(512)
            gq = G0 + qq
            valid_k = gkey >= 0
            c = (gkey[:, None] <= gq[None, :]) & valid_k[:, None]
            ctri[s, j] = c.astype(np.float32)
            blk = gq // W
            k_start = np.maximum(0, blk * W - W // 2)
            upper = gq - (blk * W - k_start)
            l = ((gkey[:, None] >= k_start[None, :])
                 & (gkey[:, None] <= upper[None, :]) & valid_k[:, None])
            lmask[s, j] = l.astype(np.float32)
    def pack(m):
        out = np.zeros((128, 2 * 5 * 512), np.float32)
        for s in range(2):
            for j in range(5):
                out[:, (s * 5 + j) * 512:(s * 5 + j + 1) * 512] = m[s, j]
        return out.astype(ml_dtypes.bfloat16)
    return gpen, pack(ctri), pack(lmask)



class KB:
    def __init__(self, nc, n_sub=4, taps=(), private_shared=False):
        self.nc = nc
        self.P = Prog()
        self.n_sub = n_sub
        self.private_shared = private_shared
        self.taps = set(taps)
        self.t = {}
        self.s = {}
        self.ps = []
        self.r = {}
        self.tap_stores = []

    def R(self, name):
        if name not in self.r:
            self.r[name] = Res(name)
        return self.r[name]

    def sb(self, name, shape, dtype):
        ap = self.nc.alloc_sbuf_tensor("sb_" + name, shape, dtype).ap()
        self.s[name] = ap
        return ap

    def load(self, dst, src, slot, tag, eng="pool", reads=(), deps=(),
             writes_extra=()):
        return self.P.op(eng, (lambda d, s: lambda g: g.dma_start(out=d, in_=s))(dst, src),
                         reads=list(reads),
                         writes=[self.R(slot)] + list(writes_extra), deps=deps,
                         dma=True, tag=tag, slot=slot)

    def mmg(self, ps_ap, pairs, reads, wres, tag, accumulate=False):
        def fn(t):
            ret = None
            n = len(pairs)
            for i, (l, rhs) in enumerate(pairs):
                ret = t.matmul(ps_ap, l, rhs,
                               start=(i == 0 and not accumulate),
                               stop=(i == n - 1), skip_group_check=True)
            return ret
        return self.P.op("pe", fn, reads=list(reads), writes=[wres], tag=tag)

    def dve(self, fn, reads, writes, tag):
        return self.P.op("dve", fn, reads=list(reads), writes=list(writes), tag=tag)

    def act(self, fn, reads, writes, tag):
        return self.P.op("act", fn, reads=list(reads), writes=list(writes), tag=tag)

    def tap(self, name, src_ap, shape, dtype, reads):
        if name not in self.taps:
            return
        d = self.nc.dram_tensor("tap_" + name, shape, dtype,
                                kind="ExternalOutput").ap()
        o = self.P.op("sp", (lambda dd, ss: lambda g: g.dma_start(out=dd, in_=ss))(d, src_ap),
                      reads=list(reads), dma=True, tag="tap_" + name,
                      slot="tap_" + name)
        self.tap_stores.append(o)

    def declare(self):
        nc = self.nc
        T = self.t
        def din(name, shape, dtype=F32):
            T[name] = nc.dram_tensor(name, shape, dtype, kind="ExternalInput").ap()
        din("x", [2048, 1024])
        din("wqkvuT", [2048, 8192])
        din("woutT", [2048, 2048])
        din("wq", [2048, 2048])
        din("wk", [2048, 512])
        din("wv", [2048, 512])
        din("wo", [2048, 2048])
        din("w0T", [2048, 8192])
        din("w1T", [8192, 2048])
        din("b0c", [128, 64])
        din("b1c", [128, 16])
        din("slope_row", [1, 16])
        din("iota_row", [1, 256])
        din("kd_col", [128, 2])
        din("bd_cols", [128, 4])
        din("idxt", [128, 512])
        din("rt", [128, 128])
        din("cosf", [128, 1024])
        din("sinf", [128, 1024])
        din("isodd", [128, 1])
        din("pars", [1, 8], I32)
        din("cbias", [1, 4])
        din("gpen", [128, 32])
        din("ctri", [128, 5120], BF16)
        din("lmask", [128, 5120], BF16)
        T["out"] = nc.dram_tensor("out", [2048, 1024], F32,
                                  kind="ExternalOutput").ap()
        sp_ = "Local" if self.private_shared else "Shared"
        T["sha"] = nc.dram_tensor("sha", [2, 2048, 128], F32, kind="Internal",
                                  addr_space=sp_).ap()
        T["shb_k"] = nc.dram_tensor("shb_k", [2, 512, 1024], BF16,
                                    kind="Internal", addr_space=sp_).ap()
        T["shb_v"] = nc.dram_tensor("shb_v", [2, 1024, 512], BF16,
                                    kind="Internal", addr_space=sp_).ap()
        T["shf"] = nc.dram_tensor("shf", [2, 2, 8], I32, kind="Internal",
                                  addr_space=sp_).ap()

        self.sb("x", [128, 16 * 1024], F32)
        self.sb("y", [128, 16 * 1024], BF16)
        self.arena = self.sb("arena", [128, 86 * 1024], mybir.dt.uint8)
        self.sb("rows", [1, 2048], F32)
        for i in range(3):
            self.sb(f"wst{i}", [128, 2048], BF16)
        self.sb("iota_row", [1, 256], F32)
        self.sb("kd_col", [128, 2], F32)
        self.sb("bd_cols", [128, 4], F32)
        self.sb("idxt", [128, 512], F32)
        self.sb("rt", [128, 128], BF16)
        self.sb("isodd", [128, 1], F32)
        self.sb("pars", [1, 8], I32)
        self.sb("cbias", [1, 4], F32)
        self.sb("gpen", [128, 32], F32)
        self.sb("slope_row", [1, 16], F32)
        self.sb("negs_row", [1, 16], F32)
        self.sb("ones_cb", [128, 1], BF16)
        self.sb("ones_rb", [1, 128], BF16)
        self.sb("ones_rf", [1, 128], F32)
        self.sb("b0c", [128, 64], F32)
        self.sb("b1c", [128, 16], F32)
        self.sb("srow_bf", [1, 256], BF16)
        self.sb("scol", [128, 28], F32)
        ps_pair = nc.alloc_psum_tensor("psP", [128, 1024], F32).ap()
        self.ps_pair = ps_pair
        self.ps.append(ps_pair[:, 0:512])
        self.ps.append(ps_pair[:, 512:1024])
        for i in range(2, 8):
            self.ps.append(nc.alloc_psum_tensor(f"ps{i}", [128, 512], F32).ap())
        self.psr = [self.R(f"ps{i}") for i in range(8)]

    def carve(self, name, off_bytes, shape, dtype):
        nbytes = {F32: 4, BF16: 2, I32: 4}[dtype]
        assert off_bytes % 4 == 0
        assert off_bytes + shape[1] * nbytes <= 86 * 1024, name
        base = self.s["arena"]
        ap = base.tensor.ap().bitcast(dtype)[0:shape[0],
                                             off_bytes // nbytes:
                                             off_bytes // nbytes + shape[1]]
        self.s[name] = ap
        return ap

    def emit_ln(self, idx):
        P, s, ps = self.P, self.s, self.ps
        x, y = s["x"], s["y"]
        rows = s["rows"]
        tmpf, r_tmp = self.carve("ln_tmp", 0, [128, 1024], F32)
        sqt, r_sq = self.carve("ln_sq", 4096, [128, 1024], BF16)
        sq_mean, r_sqm = self.carve("ln_sqm", 6144, [1, 1024], F32)
        ln_rb, r_lnrb = self.carve("ln_rb", 10240, [1, 2048], BF16)
        rx = [self.R(f"x{c}") for c in range(16)]
        ry = [self.R(f"y{c}") for c in range(16)]
        r_rows = self.R("rows")
        for c in range(16):
            xc = x[:, c * 1024:(c + 1) * 1024]
            yc = y[:, c * 1024:(c + 1) * 1024]
            self.dve((lambda a, b: lambda v: v.tensor_copy(a, b))(yc, xc),
                     [rx[c]], [ry[c]], f"ln{idx}.cp{c}")
            self.dve((lambda a, b: lambda v: v.tensor_mul(a, b, b))(sqt, yc),
                     [ry[c]], r_sq, f"ln{idx}.sq{c}")
            for h in range(2):
                ysl = yc[:, h * 512:(h + 1) * 512]
                ssl = sqt[:, h * 512:(h + 1) * 512]
                self.mmg(ps[4 + h][0:1, :], [(s["ones_cb"], ysl)],
                         [ry[c], self.R("ones")], self.psr[4 + h],
                         f"ln{idx}.s1.{c}.{h}", accumulate=(c != 0))
                self.mmg(ps[6 + h][0:1, :], [(s["ones_cb"], ssl)],
                         r_sq + [self.R("ones")], self.psr[6 + h],
                         f"ln{idx}.s2.{c}.{h}", accumulate=(c != 0))
        mean = rows[:, 0:1024]
        var = rows[:, 1024:2048]
        for h in range(2):
            self.dve((lambda a, b: lambda v: v.tensor_scalar_mul(a, b, 1.0 / 2048.0))(
                mean[:, h * 512:(h + 1) * 512], ps[4 + h][0:1, :]),
                [self.psr[4 + h]], [r_rows], f"ln{idx}.mean{h}")
            self.dve((lambda a, b: lambda v: v.tensor_scalar_mul(a, b, 1.0 / 2048.0))(
                var[:, h * 512:(h + 1) * 512], ps[6 + h][0:1, :]),
                [self.psr[6 + h]], [r_rows], f"ln{idx}.msq{h}")
        self.dve((lambda a, b: lambda v: v.tensor_mul(a, b, b))(sq_mean, mean),
                 [r_rows], r_sqm, f"ln{idx}.m2")
        self.dve((lambda a, b, c2: lambda v: v.tensor_sub(a, b, c2))(var, var, sq_mean),
                 [r_rows] + r_sqm, [r_rows], f"ln{idx}.var")
        self.act((lambda a, b, bb_=s["cbias"][0:1, 0:1]: lambda sc: sc.activation(
            a, b, AF.Ln, bias=bb_))(var, var),
                 [r_rows, self.R("cbias")], [r_rows], f"ln{idx}.ln")
        self.act((lambda a, b: lambda sc: sc.activation(a, b, AF.Exp, scale=-0.5))(var, var),
                 [r_rows], [r_rows], f"ln{idx}.exp")
        self.dve((lambda a, b, c2: lambda v: v.tensor_mul(a, b, c2))(mean, mean, var),
                 [r_rows], [r_rows], f"ln{idx}.mr")
        self.dve((lambda a, b: lambda v: v.tensor_copy(a, b))(
            ln_rb[0:1, 0:1024], var), [r_rows], r_lnrb, f"ln{idx}.rcv")
        self.dve((lambda a, b: lambda v: v.tensor_copy(a, b))(
            ln_rb[0:1, 1024:2048], mean), [r_rows], r_lnrb, f"ln{idx}.rcm")
        for h in range(2):
            self.mmg(ps[0 + h], [(s["ones_rb"], ln_rb[0:1, h * 512:(h + 1) * 512])],
                     r_lnrb + [self.R("ones")], self.psr[0 + h], f"ln{idx}.bR{h}")
            self.mmg(ps[2 + h], [(s["ones_rb"], ln_rb[0:1, 1024 + h * 512:
                                                      1024 + (h + 1) * 512])],
                     r_lnrb + [self.R("ones")], self.psr[2 + h], f"ln{idx}.bM{h}")
        for c in range(16):
            for h in range(2):
                xsl = x[:, c * 1024 + h * 512: c * 1024 + h * 512 + 512]
                ysl = y[:, c * 1024 + h * 512: c * 1024 + h * 512 + 512]
                tsl = tmpf[:, h * 512:(h + 1) * 512]
                self.dve((lambda a, b, c2: lambda v: v.tensor_mul(a, b, c2))(
                    tsl, xsl, ps[0 + h]),
                    [rx[c], self.psr[0 + h]], r_tmp, f"ln{idx}.t{c}.{h}")
                self.dve((lambda a, b, c2: lambda v: v.tensor_sub(a, b, c2))(
                    ysl, tsl, ps[2 + h]),
                    r_tmp + [self.psr[2 + h]], [ry[c]], f"ln{idx}.y{c}.{h}")

    def emit_ffn(self, idx):
        P, s, ps, t = self.P, self.s, self.ps, self.t
        y, x = s["y"], s["x"]
        dbf, _dpg = self.carve("ffn_d", 0, [128, 16 * 1024], BF16)
        ebf, _epg = self.carve("ffn_e", 32 * 1024, [128, 16 * 1024], BF16)

        def dpg(n, h):
            return [self.R(f"pg{n}")]

        def epg(hc):
            return [self.R(f"pg{16 + hc}")]
        ry = [self.R(f"y{c}") for c in range(16)]
        rx = [self.R(f"x{c}") for c in range(16)]
        wst = [s[f"wst{i}"] for i in range(3)]
        r_w = [self.R(f"wst{i}") for i in range(3)]
        for p in range(4):
            for ncnk in range(16):
                gn = p * 16 + ncnk
                bi = gn % 3
                src = t["w0T"][:, gn * 128:(gn + 1) * 128].rearrange(
                    "(kc q) n -> q kc n", q=128)
                self.load(wst[bi].rearrange("q (kc n) -> q kc n", kc=16), src,
                          f"wst{bi}", f"f{idx}.w0.{gn}")
                for h in range(2):
                    pi = (ncnk * 2 + h) % 4
                    pairs = [(wst[bi][:, kc * 128:(kc + 1) * 128],
                              y[:, kc * 1024 + h * 512: kc * 1024 + h * 512 + 512])
                             for kc in range(16)]
                    self.mmg(ps[pi], pairs, [r_w[bi]] + ry, self.psr[pi],
                             f"f{idx}.dmm.{gn}.{h}")
                    dst = dbf[:, ncnk * 1024 + h * 512: ncnk * 1024 + h * 512 + 512]
                    self.act((lambda a, b, bc: lambda sc: sc.activation(
                        a, b, AF.Gelu, bias=bc))(dst, ps[pi], s["b0c"][:, gn:gn + 1]),
                        [self.psr[pi], self.R("b0c")], dpg(ncnk, h),
                        f"f{idx}.gelu.{gn}.{h}")
            for hc in range(16):
                bi = hc % 3
                src = t["w1T"][p * 2048:(p + 1) * 2048,
                               hc * 128:(hc + 1) * 128].rearrange(
                    "(kc q) n -> q kc n", q=128)
                self.load(wst[bi].rearrange("q (kc n) -> q kc n", kc=16), src,
                          f"wst{bi}", f"f{idx}.w1.{p}.{hc}")
                for h in range(2):
                    pi = 4 + (hc * 2 + h) % 2
                    pairs = [(wst[bi][:, n * 128:(n + 1) * 128],
                              dbf[:, n * 1024 + h * 512: n * 1024 + h * 512 + 512])
                             for n in range(16)]
                    self.mmg(ps[pi], pairs,
                             [r_w[bi]] + [self.R(f"pg{n}") for n in range(16)],
                             self.psr[pi], f"f{idx}.emm.{p}.{hc}.{h}")
                    dst = ebf[:, hc * 1024 + h * 512: hc * 1024 + h * 512 + 512]
                    if p == 0 and h == 0:
                        pass
                    if p == 0:
                        fn = (lambda a, b: lambda v: v.tensor_copy(a, b))(dst, ps[pi])
                    else:
                        fn = (lambda a, b: lambda v: v.tensor_add(a, a, b))(dst, ps[pi])
                    self.dve(fn, [self.psr[pi]] + (epg(hc) if p else []),
                             epg(hc), f"f{idx}.eacc.{p}.{hc}.{h}")
        for hc in range(16):
            xc = x[:, hc * 1024:(hc + 1) * 1024]
            ec = ebf[:, hc * 1024:(hc + 1) * 1024]
            self.dve((lambda a, b, bc: lambda v: v.scalar_tensor_tensor(
                a, b, bc, a, ALU.add, ALU.add))(xc, ec, s["b1c"][:, hc:hc + 1]),
                epg(hc) + [self.R("b1c"), rx[hc]], [rx[hc]],
                f"f{idx}.res.{hc}")

    PAGE = 2048

    def carve(self, name, off_bytes, shape, dtype):
        nbytes = {F32: 4, BF16: 2, I32: 4}[dtype]
        end = off_bytes + shape[1] * nbytes
        assert off_bytes % 4 == 0 and end <= 86 * 1024, name
        base = self.s["arena"].tensor.ap().bitcast(dtype)
        ap = base[0:shape[0], off_bytes // nbytes: off_bytes // nbytes + shape[1]]
        pages = [self.R(f"pg{i}") for i in
                 range(off_bytes // self.PAGE, (end + self.PAGE - 1) // self.PAGE)]
        self.s[name] = ap
        return ap, pages

    def emit_setup(self):
        s, t = self.s, self.t
        x3 = t["x"].rearrange("(c p) t -> p c t", p=128)
        self.load(s["x"].rearrange("p (c t) -> p c t", c=16), x3, "x_in",
                  "ld_x", eng="sp")
        xld = self.R("x_in")
        for c in range(16):
            self.R(f"x{c}").writer = xld.writer
        for nm in ("b0c", "b1c", "slope_row", "iota_row", "kd_col", "bd_cols",
                   "idxt", "isodd", "pars", "gpen", "cbias"):
            self.load(s[nm], t[nm], nm, "ld_" + nm, eng="sp")
        self.load(s["rt"], t["rt"], "rt", "ld_rt", eng="pool")
        def ms(g):
            g.memset(s["ones_cb"], 1.0)
            g.memset(s["ones_rb"], 1.0)
            return g.memset(s["ones_rf"], 1.0)
        self.P.op("pool", ms, writes=[self.R("ones")], tag="ones")
        self.dve((lambda a, b: lambda v: v.tensor_scalar_mul(a, b, -1.0))(
            s["negs_row"], s["slope_row"]),
            [self.R("slope_row")], [self.R("negs")], "negs")

    def _negcol(self, h, tag, bank=7):
        s, ps = self.s, self.ps
        self.mmg(ps[bank][:, 0:1], [(s["ones_rf"], s["negs_row"][0:1, h:h + 1])],
                 [self.R("negs"), self.R("ones")], self.psr[bank], tag + ".ncb")
        self.dve((lambda a, b: lambda v: v.tensor_copy(a, b))(
            s["scol"][:, 0:1], ps[bank][:, 0:1]),
            [self.psr[bank]], [self.R("scol")], tag + ".ncc")

    def _aa_evac(self, sub, h, Oar, r_O, sq_t, r_sqt):
        ps, s = self.ps, self.s
        for blk in range(2):
            opi = 2 + blk % 2
            osl = Oar[:, h * 512 + blk * 256: h * 512 + blk * 256 + 256]
            self.dve((lambda a, b: lambda v: v.tensor_copy(a, b))(
                osl, ps[opi][:, 0:256]), [self.psr[opi]], r_O,
                f"aa.ocp.{sub}.{h}.{blk}")
            self.dve((lambda a, b: lambda v: v.tensor_mul(a, b, b))(
                sq_t, osl), r_O, r_sqt, f"aa.osq.{sub}.{h}.{blk}")
            self.mmg(ps[7][0:1, blk * 256: blk * 256 + 256],
                     [(s["ones_cb"], sq_t)], r_sqt + [self.R("ones")],
                     self.psr[7], f"aa.ost.{sub}.{h}.{blk}",
                     accumulate=not (h == 0 and blk == 0))

    def emit_attn_a(self):
        P, s, ps, t = self.P, self.s, self.ps, self.t
        y, x = s["y"], s["x"]
        ry = [self.R(f"y{c}") for c in range(16)]
        rx = [self.R(f"x{c}") for c in range(16)]
        wst = [s[f"wst{i}"] for i in range(3)]
        r_w = [self.R(f"wst{i}") for i in range(3)]
        negsb = self.sb("aanegsb", [1, 16], BF16)
        r_negsb = [self.R("aanegsb")]
        self.r_negsb_pages = r_negsb
        aarb, r_aarb = self.carve("aarb", 31232, [1, 512], BF16)
        self.r_aarb_pages = r_aarb
        self.dve((lambda a, b: lambda v: v.tensor_copy(a, b))(
            negsb, s["negs_row"]), [self.R("negs")], r_negsb, "aa.negsb")
        kvpre, r_kvpre = self.carve("kvpre", 0, [128, 16 * 384], BF16)
        kvin_b, r_kvinb = self.carve("kvin_b", 12288, [128, 2048], BF16)
        kvf, r_kvf = self.carve("kvf", 16384, [128, 2048], F32)
        kvin_f, r_kvinf = self.carve("kvin_f", 24576, [128, 2048], F32)
        slab0, r_sl0 = self.carve("slab0", 32768, [128, 8192], BF16)
        slab1, r_sl1 = self.carve("slab1", 49152, [128, 8192], BF16)
        slabs = [(slab0, r_sl0), (slab1, r_sl1)]
        ktm_g, r_ktm = self.carve("ktm_g", 65536, [128, 4096], BF16)
        vtm_e, r_vtme = self.carve("vtm_e", 73728, [128, 4096], BF16)
        r_scol = self.R("scol")

        def group_prep(g, tag, with_kdc=False, with_main=False, bank=7):
            self.mmg(ps[bank][:, 0:4], [(s["ones_rf"], s["negs_row"][0:1, 4 * g:4 * g + 4])],
                     [self.R("negs"), self.R("ones")],
                     self.psr[bank], tag + ".nc4")
            self.dve((lambda a, b: lambda v: v.tensor_copy(a, b))(
                s["scol"][:, 0:4], ps[bank][:, 0:4]),
                [self.psr[bank]], [r_scol], tag + ".nc4c")
            for hh in range(4):
                sc_ap = s["scol"][:, hh:hh + 1]
                self.act((lambda a, b, sc_: lambda sc: sc.activation(
                    a, b, AF.Exp, scale=sc_))(
                    s["scol"][:, 4 + 4 * hh:8 + 4 * hh], s["bd_cols"], sc_ap),
                    [self.R("bd_cols"), r_scol], [r_scol], tag + f".bdp{hh}")
                if with_kdc:
                    self.act((lambda a, b, sc_: lambda sc: sc.activation(
                        a, b, AF.Exp, scale=sc_))(
                        s["scol"][:, 20 + 2 * hh:22 + 2 * hh], s["kd_col"], sc_ap),
                        [self.R("kd_col"), r_scol], [r_scol], tag + f".kdc{hh}")
                if with_main:
                    h = 4 * g + hh
                    self.act((lambda a, b, sc_: lambda sc: sc.activation(
                        a, b, AF.Exp, scale=sc_))(
                        self.s["aaddt"][:, hh * 512:(hh + 1) * 512],
                        s["idxt"], sc_ap),
                        [self.R("idxt"), r_scol], self.r_ddt4, tag + f".ddt{hh}")
                    self.act((lambda a, b, sc_: lambda sc: sc.activation(
                        a, b, AF.Exp, scale=sc_))(
                        s["srow_bf"][0:1, 0:256], s["iota_row"],
                        s["negs_row"][0:1, h:h + 1]),
                        [self.R("iota_row"), self.R("negs")], [self.R("srow_bf")],
                        tag + f".qdr{hh}")
                    self.mmg(ps[bank][:, 0:256],
                             [(s["ones_rb"], s["srow_bf"][0:1, 0:256])],
                             [self.R("srow_bf"), self.R("ones")], self.psr[bank],
                             tag + f".qdb{hh}")
                    self.dve((lambda a, b: lambda v: v.tensor_copy(a, b))(
                        self.s["aaqd4"][:, hh * 256:(hh + 1) * 256],
                        ps[bank][:, 0:256]), [self.psr[bank]], self.r_qd4,
                        tag + f".qdc{hh}")

        for g in range(4):
            group_prep(g, f"aa.ex.g{g}", with_kdc=True)
            for kind, coff, slot in (("v", 4096, 0), ("k", 2048, 1)):
                sl_ap, sl_r = slabs[slot]
                src = t["wqkvuT"][:, coff + g * 512: coff + (g + 1) * 512] \
                    .rearrange("(fc p) n -> p fc n", p=128)
                self.load(sl_ap.rearrange("p (fc n) -> p fc n", fc=16), src,
                          f"aslab{slot}", f"aa.ex.w{kind}.{g}",
                          writes_extra=sl_r)
            for tc in range(8):
                for kind, slot, dstap, dstres in (
                        ("v", 0, vtm_e, r_vtme), ("k", 1, ktm_g, r_ktm)):
                    sl_ap, sl_r = slabs[slot]
                    pi = (tc * 2 + slot) % 4
                    pairs = [(y[:, fc * 1024 + tc * 128: fc * 1024 + tc * 128 + 128],
                              sl_ap[:, fc * 512:(fc + 1) * 512])
                             for fc in range(16)]
                    self.mmg(ps[pi], pairs, sl_r + ry, self.psr[pi],
                             f"aa.ex.{kind}mm.{g}.{tc}")
                    dsl = dstap[:, tc * 512:(tc + 1) * 512]
                    if kind == "v":
                        self.dve((lambda a, b: lambda v: v.tensor_copy(a, b))(
                            dsl, ps[pi]), [self.psr[pi]], dstres,
                            f"aa.ex.vcp.{g}.{tc}")
                    else:
                        self.act((lambda a, b: lambda sc: sc.activation(
                            a, b, AF.Silu))(dsl, ps[pi]), [self.psr[pi]],
                            dstres, f"aa.ex.ksl.{g}.{tc}")
            for hh in range(4):
                h = g * 4 + hh
                for tc in range(8):
                    sl = ktm_g[:, tc * 512 + hh * 128: tc * 512 + hh * 128 + 128]
                    self.dve((lambda a, c2: lambda v: v.tensor_scalar_mul(
                        a, a, c2))(sl, s["scol"][:, 20 + 2 * hh + tc % 2:
                                                 21 + 2 * hh + tc % 2]),
                        [r_scol] + r_ktm, r_ktm, f"aa.ex.kkd.{h}.{tc}")
                kvf_h = kvf[:, h * 128:(h + 1) * 128]
                for B in range(4):
                    pairs = [(ktm_g[:, (2 * B + j) * 512 + hh * 128:
                                    (2 * B + j) * 512 + hh * 128 + 128],
                              vtm_e[:, (2 * B + j) * 512 + hh * 128:
                                    (2 * B + j) * 512 + hh * 128 + 128])
                             for j in range(2)]
                    self.mmg(ps[6][:, 0:128], pairs, r_ktm + r_vtme,
                             self.psr[6], f"aa.ex.upd.{h}.{B}")
                    if B == 0:
                        fn = (lambda a, b: lambda v: v.tensor_copy(a, b))(
                            kvf_h, ps[6][:, 0:128])
                    else:
                        fn = (lambda a, b, sc_: lambda v: v.scalar_tensor_tensor(
                            a, a, sc_, b, ALU.mult, ALU.add))(
                            kvf_h, ps[6][:, 0:128],
                            s["scol"][:, 5 + 4 * hh:6 + 4 * hh])
                    self.dve(fn, [self.psr[6], r_scol] + r_kvf, r_kvf,
                             f"aa.ex.kv.{h}.{B}")
                    if B < 3:
                        dsl = kvpre[:, h * 384 + B * 128: h * 384 + B * 128 + 128]
                        self.dve((lambda a, b: lambda v: v.tensor_copy(a, b))(
                            dsl, kvf_h), r_kvf, r_kvpre, f"aa.ex.pre.{h}.{B}")

        self._exchange_a(r_kvf, r_kvinf)

        Oar, r_O = self.carve("aaO", 49152, [128, 8192], BF16)
        ddt4, self.r_ddt4 = self.carve("aaddt", 16384, [128, 2048], BF16)
        qd4, self.r_qd4 = self.carve("aaqd4", 20480, [128, 1024], BF16)
        shat2, r_shat2 = self.carve("aash2", 22528, [128, 1024], BF16)
        qqd2, r_qqd2 = self.carve("aaqqd", 81920, [128, 512], BF16)
        qgd2, r_qgd2 = self.carve("aaqgd", 82944, [128, 512], BF16)
        sq_t, r_sqt = self.carve("aasq", 26624, [128, 256], BF16)
        vtm, r_vtm = self.carve("aavtm", 65536, [128, 8192], BF16)
        qk_t = [self.carve("aaq0", 83968, [128, 512], BF16),
                self.carve("aaq1", 84992, [128, 512], BF16),
                self.carve("aak0", 86016, [128, 512], BF16),
                self.carve("aak1", 87040, [128, 512], BF16)]

        for sub in range(2):
            S = sub * 512
            for vs in range(4):
                sl_ap, sl_r = slabs[vs % 2]
                src = t["wqkvuT"][:, 4096 + vs * 512: 4096 + (vs + 1) * 512] \
                    .rearrange("(fc p) n -> p fc n", p=128)
                self.load(sl_ap.rearrange("p (fc n) -> p fc n", fc=16), src,
                          f"aslab{vs % 2}", f"aa.m{sub}.wv.{vs}",
                          writes_extra=sl_r)
                for tc in range(4):
                    pi = (vs * 4 + tc) % 4
                    pairs = [(y[:, fc * 1024 + S + tc * 128:
                                fc * 1024 + S + tc * 128 + 128],
                              sl_ap[:, fc * 512:(fc + 1) * 512])
                             for fc in range(16)]
                    self.mmg(ps[pi], pairs, sl_r + ry, self.psr[pi],
                             f"aa.m{sub}.vmm.{vs}.{tc}")
                    dsl = vtm[:, tc * 2048 + vs * 512: tc * 2048 + (vs + 1) * 512]
                    self.dve((lambda a, b: lambda v: v.tensor_copy(a, b))(
                        dsl, ps[pi]), [self.psr[pi]], r_vtm,
                        f"aa.m{sub}.vcp.{vs}.{tc}")
            if sub == 0:
                self.dve((lambda a, b, c2: lambda v: v.tensor_scalar_mul(a, b, c2))(
                    kvin_b, kvin_f, s["isodd"][:, 0:1]),
                    r_kvinf + [self.R("isodd")], r_kvinb, "aa.kvinbf")
            for g in range(4):
                group_prep(g, f"aa.m{sub}.g{g}", with_main=True, bank=6)
                sl_ap, sl_r = slabs[0]
                src_ = t["wqkvuT"][:, g * 512:(g + 1) * 512] \
                    .rearrange("(fc p) n -> p fc n", p=128)
                self.load(sl_ap.rearrange("p (fc n) -> p fc n", fc=16), src_,
                          "aslab0", f"aa.m{sub}.wq.{g}", writes_extra=sl_r)
                for hh in range(4):
                    h = 4 * g + hh
                    qa, r_qa = qk_t[h % 2]
                    ka, r_ka = qk_t[2 + h % 2]
                    pairs = [(sl_ap[:, kc * 512 + hh * 128:
                                    kc * 512 + (hh + 1) * 128],
                              y[:, kc * 1024 + S: kc * 1024 + S + 512])
                             for kc in range(16)]
                    self.mmg(ps[4], pairs, sl_r + ry, self.psr[4],
                             f"aa.m{sub}.qmm.{h}")
                    self.act((lambda a, b: lambda sc: sc.activation(
                        a, b, AF.Silu))(qa, ps[4]), [self.psr[4]], r_qa,
                        f"aa.m{sub}.qsl.{h}")
                    bi = h % 3
                    srck = t["wqkvuT"][:, 2048 + h * 128: 2048 + (h + 1) * 128] \
                        .rearrange("(kc q) n -> q kc n", q=128)
                    self.load(wst[bi].rearrange("q (kc n) -> q kc n", kc=16),
                              srck, f"wst{bi}", f"aa.m{sub}.wk.{h}")
                    kpairs = [(wst[bi][:, kc * 128:(kc + 1) * 128],
                               y[:, kc * 1024 + S: kc * 1024 + S + 512])
                              for kc in range(16)]
                    self.mmg(ps[5], kpairs, [r_w[bi]] + ry, self.psr[5],
                             f"aa.m{sub}.kmm.{h}")
                    self.act((lambda a, b: lambda sc: sc.activation(
                        a, b, AF.Silu))(ka, ps[5]), [self.psr[5]], r_ka,
                        f"aa.m{sub}.ksl.{h}")
                    for blk in range(2):
                        B = sub * 2 + blk
                        stp = ps[blk % 2]
                        qb = qa[:, blk * 256: blk * 256 + 256]
                        def st_fn(pp=stp, k0=ka[:, blk * 256: blk * 256 + 128],
                                  k1=ka[:, blk * 256 + 128: blk * 256 + 256],
                                  qq=qb):
                            def f(te):
                                te.matmul(pp[:, 0:256], k0, qq, start=True,
                                          stop=True, skip_group_check=True)
                                return te.matmul(pp[:, 256:512], k1, qq,
                                                 start=False, stop=True,
                                                 skip_group_check=True)
                            return f
                        self.P.op("pe", st_fn(), reads=r_ka + r_qa,
                                  writes=[self.psr[blk % 2]],
                                  tag=f"aa.st.{sub}.{h}.{blk}")
                        self.dve((lambda a, b, c2: lambda v: v.tensor_mul(a, b, c2))(
                            shat2[:, blk * 512:(blk + 1) * 512], stp[:, 0:512],
                            ddt4[:, hh * 512:(hh + 1) * 512]),
                            [self.psr[blk % 2]] + self.r_ddt4, r_shat2,
                            f"aa.shat.{sub}.{h}.{blk}")
                        self.dve((lambda a, b, c2: lambda v: v.tensor_mul(a, b, c2))(
                            qqd2[:, blk * 256:(blk + 1) * 256], qb,
                            qd4[:, hh * 256:(hh + 1) * 256]),
                            r_qa + self.r_qd4, r_qqd2,
                            f"aa.qqd.{sub}.{h}.{blk}")
                        self.dve((lambda a, b, c2: lambda v: v.tensor_scalar_mul(
                            a, b, c2))(qgd2[:, blk * 256:(blk + 1) * 256],
                                       qqd2[:, blk * 256:(blk + 1) * 256],
                                       s["scol"][:, 4 + 4 * hh + B:5 + 4 * hh + B]),
                            r_qqd2 + [r_scol], r_qgd2, f"aa.qgd.{sub}.{h}.{blk}")
                    for blk in range(2):
                        B = sub * 2 + blk
                        opi = 2 + blk % 2
                        pairs = [(vtm[:, (2 * blk + j) * 2048 + h * 128:
                                      (2 * blk + j) * 2048 + h * 128 + 128],
                                  shat2[:, blk * 512 + j * 256:
                                        blk * 512 + (j + 1) * 256])
                                 for j in range(2)]
                        reads = r_vtm + r_shat2 + r_qgd2 + r_qqd2 \
                            + r_kvinb + r_kvpre
                        if B > 0:
                            pairs.append((kvpre[:, h * 384 + (B - 1) * 128:
                                               h * 384 + B * 128],
                                          qqd2[:, blk * 256:(blk + 1) * 256]))
                        pairs.append((kvin_b[:, h * 128:(h + 1) * 128],
                                      qgd2[:, blk * 256:(blk + 1) * 256]))
                        self.mmg(ps[opi][:, 0:256], pairs, reads, self.psr[opi],
                                 f"aa.omm.{sub}.{h}.{blk}")
                    self._aa_evac(sub, h, Oar, r_O, sq_t, r_sqt)
            rows = s["rows"]
            r_rows = self.R("rows")
            self.act((lambda a, b, bb_=s["cbias"][0:1, 1:2]: lambda sc: sc.activation(
                a, b, AF.Ln, scale=1.0 / 2048.0, bias=bb_))(rows[:, 0:512],
                                                            ps[7][0:1, :]),
                [self.psr[7], self.R("cbias")], [r_rows], f"aa.rln.{sub}")
            self.act((lambda a, b: lambda sc: sc.activation(
                a, b, AF.Exp, scale=-0.5))(rows[:, 0:512], rows[:, 0:512]),
                [r_rows], [r_rows], f"aa.rex.{sub}")
            self.dve((lambda a, b: lambda v: v.tensor_copy(a, b))(
                self.s["aarb"], rows[0:1, 0:512]),
                [r_rows], self.r_aarb_pages, f"aa.rc.{sub}")
            def rbc_fn(te):
                te.matmul(ps[6][:, 0:256], s["ones_rb"],
                          self.s["aarb"][0:1, 0:256],
                          start=True, stop=True, skip_group_check=True)
                return te.matmul(ps[6][:, 256:512], s["ones_rb"],
                                 self.s["aarb"][0:1, 256:512],
                                 start=False, stop=True, skip_group_check=True)
            self.P.op("pe", rbc_fn,
                      reads=self.r_aarb_pages + [self.R("ones")],
                      writes=[self.psr[6]], tag=f"aa.rbc.{sub}")
            for h in range(16):
                bi = h % 3
                src = t["wqkvuT"][:, 6144 + h * 128: 6144 + (h + 1) * 128] \
                    .rearrange("(kc q) n -> q kc n", q=128)
                self.load(wst[bi].rearrange("q (kc n) -> q kc n", kc=16), src,
                          f"wst{bi}", f"aa.m{sub}.wu.{h}")
                pi = 4 + h % 2
                pairs = [(wst[bi][:, kc * 128:(kc + 1) * 128],
                          y[:, kc * 1024 + S: kc * 1024 + S + 512])
                         for kc in range(16)]
                self.mmg(ps[pi], pairs, [r_w[bi]] + ry, self.psr[pi],
                         f"aa.umm.{sub}.{h}")
                osl = Oar[:, h * 512:(h + 1) * 512]
                self.dve((lambda a, b: lambda v: v.tensor_mul(a, a, b))(
                    osl, ps[pi]), [self.psr[pi]] + r_O, r_O,
                    f"aa.ou.{sub}.{h}")
                self.dve((lambda a, b: lambda v: v.tensor_mul(a, a, b))(
                    osl, ps[6]), [self.psr[6]] + r_O, r_O,
                    f"aa.our.{sub}.{h}")
            for oc in range(16):
                bi = oc % 3
                src = t["woutT"][:, oc * 128:(oc + 1) * 128].rearrange(
                    "(kc q) n -> q kc n", q=128)
                self.load(wst[bi].rearrange("q (kc n) -> q kc n", kc=16), src,
                          f"wst{bi}", f"aa.m{sub}.wo.{oc}")
                pi = 4 + oc % 2
                pairs = [(wst[bi][:, hc * 128:(hc + 1) * 128],
                          Oar[:, hc * 512:(hc + 1) * 512]) for hc in range(16)]
                self.mmg(ps[pi], pairs, [r_w[bi]] + r_O, self.psr[pi],
                         f"aa.wout.{sub}.{oc}")
                xsl = x[:, oc * 1024 + S: oc * 1024 + S + 512]
                self.dve((lambda a, b: lambda v: v.tensor_add(a, a, b))(
                    xsl, ps[pi]), [self.psr[pi], rx[oc]], [rx[oc]],
                    f"aa.res.{sub}.{oc}")

    def _exchange_a(self, r_kvf, r_kvinf):
        s, t = self.s, self.t
        kvf, _ = self.carve("kvf2", 16384, [128, 2048], F32)
        kvin_f, _ = self.carve("kvin_f2", 24576, [128, 2048], F32)
        sems = self.xsems["a"]

        def blob_w(g):
            sd, sp_ = sems
            src3 = kvf.rearrange("d (h e) -> d h e", h=16)
            with g.register("rparA") as rpar:
                g.reg_load(rpar, s["pars"][0:1, 0:1])
                with g.If_eq(rpar, 0):
                    g.dma_start(out=t["sha"][0].rearrange("(h d) e -> d h e", d=128),
                                in_=src3).then_inc(sd, 16)
                    g.wait_ge(sd, 16)
                    g.dma_start(out=t["shf"][0, 0:1, :], in_=s["pars"]).then_inc(sd, 16)
                    g.wait_ge(sd, 32)
                with g.Else():
                    g.dma_start(out=t["sha"][1].rearrange("(h d) e -> d h e", d=128),
                                in_=src3).then_inc(sd, 16)
                    g.wait_ge(sd, 16)
                    g.dma_start(out=t["shf"][0, 1:2, :], in_=s["pars"]).then_inc(sd, 16)
                    g.wait_ge(sd, 32)
            return g.nop()

        self.P.op("sp", blob_w, reads=r_kvf + [self.R("pars")],
                  writes=[self.R("xga"), self.R("pars_poll")], tag="xchgAw")

        def blob_r(g):
            sd, sp_ = sems
            dst3 = kvin_f.rearrange("d (h e) -> d h e", h=16)
            with g.register("rparA2") as rpar, g.register("rflagA") as rflag, \
                    g.register("rnonA") as rnon, g.register("rcntA") as rcnt:
                g.reg_load(rpar, s["pars"][0:1, 0:1])
                g.reg_load(rnon, s["pars"][0:1, 1:2])
                g.reg_mov(rcnt, 0)
                g.reg_mov(rflag, 0)
                with g.If_eq(rpar, 0):
                    with g.While(lambda: g.snap(rflag) != g.snap(rnon)):
                        g.reg_add(rcnt, rcnt, 16)
                        g.dma_start(out=s["pars"][0:1, 4:8],
                                    in_=t["shf"][0, 1:2, 0:4]).then_inc(sp_, 16)
                        g.wait_ge(sp_, rcnt)
                        g.reg_load(rflag, s["pars"][0:1, 5:6])
                    g.dma_start(out=dst3,
                                in_=t["sha"][1].rearrange("(h d) e -> d h e", d=128)
                                ).then_inc(sd, 16)
                    g.wait_ge(sd, 48)
                with g.Else():
                    with g.While(lambda: g.snap(rflag) != g.snap(rnon)):
                        g.reg_add(rcnt, rcnt, 16)
                        g.dma_start(out=s["pars"][0:1, 4:8],
                                    in_=t["shf"][0, 0:1, 0:4]).then_inc(sp_, 16)
                        g.wait_ge(sp_, rcnt)
                        g.reg_load(rflag, s["pars"][0:1, 5:6])
                    g.dma_start(out=dst3,
                                in_=t["sha"][0].rearrange("(h d) e -> d h e", d=128)
                                ).then_inc(sd, 16)
                    g.wait_ge(sd, 48)
            return g.nop()

        self.P.op("sp", blob_r, reads=[self.R("xga"), self.R("pars")],
                  writes=r_kvinf + [self.R("pars_poll")], tag="xchgAr")

    def emit_attn_b(self):
        P, s, ps, t = self.P, self.s, self.ps, self.t
        y, x = s["y"], s["x"]
        ry = [self.R(f"y{c}") for c in range(16)]
        rx = [self.R(f"x{c}") for c in range(16)]
        wst = [s[f"wst{i}"] for i in range(3)]
        r_w = [self.R(f"wst{i}") for i in range(3)]
        Kf, r_Kf = self.carve("bKf", 0, [128, 8192], BF16)
        Vf, r_Vf = self.carve("bVf", 16384, [128, 8192], BF16)
        Ob, r_Ob = self.carve("bO", 32768, [128, 8192], BF16)
        cosb, r_cos = self.carve("bcos", 49152, [128, 1024], BF16)
        sinb, r_sin = self.carve("bsin", 51200, [128, 1024], BF16)
        ctri_s, r_ctri = self.carve("bctri", 53248, [128, 2560], BF16)
        lmask_s, r_lm = self.carve("blm", 58368, [128, 2560], BF16)
        Ep = [self.carve("bEp0", 63488, [128, 1024], BF16),
              self.carve("bEp1", 65536, [128, 1024], BF16)]
        Ecp, r_Ecp = self.carve("bEcp", 67584, [128, 1024], BF16)
        qh_t, r_qh = self.carve("bqh", 69632, [128, 512], BF16)
        qro = [self.carve("bqro0", 70656, [128, 512], BF16),
               self.carve("bqro1", 71680, [128, 512], BF16)]
        t1_t, r_t1 = self.carve("bt1", 72704, [128, 512], BF16)
        sqq_t, r_sqq = self.carve("bsqq", 72704, [128, 512], BF16)
        Kown, r_Kown = self.carve("bKown", 0, [128, 4096], BF16)
        Vown, r_Vown = self.carve("bVown", 16384, [128, 4096], BF16)
        vslab, r_vsl = self.carve("bvslab", 32768, [128, 8192], BF16)
        kg_t, r_kg = self.carve("bkg", 53248, [128, 1024], BF16)
        bcols = self.sb("bcols", [128, 32], F32) if "bcols" not in s else s["bcols"]
        r_bcols = [self.R("bcols0"), self.R("bcols1")]
        brow = self.sb("brow", [1, 16], F32) if "brow" not in s else s["brow"]
        r_brow = self.R("brow")
        rows, r_rows = s["rows"], self.R("rows")
        r_scol = [self.R("scolB0"), self.R("scolB1")]

        self.load(cosb, t["cosf"], "bcos", "bb.cos", writes_extra=r_cos)
        self.load(sinb, t["sinf"], "bsin", "bb.sin", writes_extra=r_sin)

        for g in range(4):
            bi = g % 3
            src = t["wk"][:, g * 128:(g + 1) * 128].rearrange(
                "(kc q) n -> q kc n", q=128)
            self.load(wst[bi].rearrange("q (kc n) -> q kc n", kc=16), src,
                      f"wst{bi}", f"bb.wk.{g}")
            for hhalf in range(2):
                pi = hhalf
                pairs = [(wst[bi][:, kc * 128:(kc + 1) * 128],
                          y[:, kc * 1024 + hhalf * 512: kc * 1024 + hhalf * 512 + 512])
                         for kc in range(16)]
                self.mmg(ps[pi], pairs, [r_w[bi]] + ry, self.psr[pi],
                         f"bb.kmm.{g}.{hhalf}")
                ksl = kg_t[:, hhalf * 512:(hhalf + 1) * 512]
                self.dve((lambda a, b: lambda v: v.tensor_copy(a, b))(
                    ksl, ps[pi]), [self.psr[pi]], r_kg, f"bb.kcp.{g}.{hhalf}")
                self.mmg(ps[2 + hhalf], [(s["rt"], ksl)],
                         [self.R("rt")] + r_kg, self.psr[2 + hhalf],
                         f"bb.krot.{g}.{hhalf}")
                self.dve((lambda a, b, c2: lambda v: v.tensor_mul(a, b, c2))(
                    t1_t, ksl, cosb[:, hhalf * 512:(hhalf + 1) * 512]),
                    r_kg + r_cos, r_t1, f"bb.kc1.{g}.{hhalf}")
                dsl = Kown[:, g * 1024 + hhalf * 512: g * 1024 + (hhalf + 1) * 512]
                self.dve((lambda a, b, c2: lambda v: v.tensor_mul(a, b, c2))(
                    dsl, ps[2 + hhalf], sinb[:, hhalf * 512:(hhalf + 1) * 512]),
                    [self.psr[2 + hhalf]] + r_sin, r_Kown, f"bb.ks1.{g}.{hhalf}")
                self.dve((lambda a, b, c2: lambda v: v.tensor_add(a, a, c2))(
                    dsl, dsl, t1_t), r_Kown + r_t1, r_Kown,
                    f"bb.kro.{g}.{hhalf}")
        src = t["wv"].rearrange("(fc p) n -> p fc n", p=128)
        self.load(vslab.rearrange("p (fc n) -> p fc n", fc=16), src,
                  "bvslab", "bb.wv", writes_extra=r_vsl)
        for tc in range(8):
            pi = 4 + tc % 2
            pairs = [(y[:, fc * 1024 + tc * 128: fc * 1024 + tc * 128 + 128],
                      vslab[:, fc * 512:(fc + 1) * 512]) for fc in range(16)]
            self.mmg(ps[pi], pairs, r_vsl + ry, self.psr[pi], f"bb.vmm.{tc}")
            dsl = Vown[:, tc * 512:(tc + 1) * 512]
            self.dve((lambda a, b: lambda v: v.tensor_copy(a, b))(
                dsl, ps[pi]), [self.psr[pi]], r_Vown, f"bb.vcp.{tc}")

        self._exchange_b(r_Kown, r_Vown, r_Kf, r_Vf, Kf, Vf)

        for g in range(4):
            for q4 in range(4):
                ksl = Kf[:, g * 2048 + q4 * 512: g * 2048 + (q4 + 1) * 512]
                self.dve((lambda a, b: lambda v: v.tensor_mul(a, b, b))(
                    sqq_t, ksl), r_Kf, r_sqq, f"bb.ksq.{g}.{q4}")
                self.mmg(ps[6][0:1, :], [(s["ones_cb"], sqq_t)],
                         r_sqq + [self.R("ones")], self.psr[6],
                         f"bb.knm.{g}.{q4}")
                self.dve((lambda b, i=g * 4 + q4: lambda v: v.reduce_max(
                    rows[:, 1024 + i:1025 + i], b, axis=AX.X))(ps[6][0:1, :]),
                    [self.psr[6]], [r_rows], f"bb.kmx.{g}.{q4}")
        self.dve((lambda a: lambda v: v.reduce_max(
            brow[:, 0:1], a, axis=AX.X))(rows[0:1, 1024:1040]),
            [r_rows], [r_brow], "bb.kmx2")

        for st_ in range(2):
            S = st_ * 512
            self.load(ctri_s, t["ctri"][:, st_ * 2560:(st_ + 1) * 2560],
                      "bctri", f"bb.ctri.{st_}", eng="sp", writes_extra=r_ctri)
            self.load(lmask_s, t["lmask"][:, st_ * 2560:(st_ + 1) * 2560],
                      "blm", f"bb.lm.{st_}", eng="sp", writes_extra=r_lm)
            n_main = 7 if st_ == 0 else 11

            def prep(h):
                par = h % 2
                A, B = ps[2 + 2 * par], ps[3 + 2 * par]
                rA, rB = self.psr[2 + 2 * par], self.psr[3 + 2 * par]
                qro_t, r_qro = qro[par]
                bi = h % 3
                src = t["wq"][:, h * 128:(h + 1) * 128].rearrange(
                    "(kc q) n -> q kc n", q=128)
                self.load(wst[bi].rearrange("q (kc n) -> q kc n", kc=16), src,
                          f"wst{bi}", f"bb.wq.{st_}.{h}")
                pairs = [(wst[bi][:, kc * 128:(kc + 1) * 128],
                          y[:, kc * 1024 + S: kc * 1024 + S + 512])
                         for kc in range(16)]
                self.mmg(A, pairs, [r_w[bi]] + ry, rA, f"bb.qmm.{st_}.{h}")
                self.dve((lambda a, b: lambda v: v.tensor_copy(a, b))(
                    qh_t, A), [rA], r_qh, f"bb.qcp.{st_}.{h}")
                self.mmg(B, [(s["rt"], qh_t)], [self.R("rt")] + r_qh, rB,
                         f"bb.qrot.{st_}.{h}")
                self.dve((lambda a, b, c2: lambda v: v.tensor_mul(a, b, c2))(
                    t1_t, qh_t, cosb[:, S:S + 512]), r_qh + r_cos, r_t1,
                    f"bb.qc1.{st_}.{h}")
                self.dve((lambda a, b, c2: lambda v: v.tensor_mul(a, b, c2))(
                    qro_t, B, sinb[:, S:S + 512]),
                    [rB] + r_sin, r_qro, f"bb.qs1.{st_}.{h}")
                self.dve((lambda a, b, c2: lambda v: v.tensor_add(a, a, c2))(
                    qro_t, qro_t, t1_t), r_qro + r_t1, r_qro,
                    f"bb.qro.{st_}.{h}")
                self.dve((lambda a, b: lambda v: v.tensor_mul(a, b, b))(
                    sqq_t, qro_t), r_qro, r_sqq, f"bb.qsq.{st_}.{h}")
                self.mmg(A[0:1, :], [(s["ones_cb"], sqq_t)],
                         r_sqq + [self.R("ones")], rA, f"bb.qnm.{st_}.{h}")
                c0 = 2 + par * 4
                self.dve((lambda b, c1=c0: lambda v: v.reduce_max(
                    brow[:, c1:c1 + 1], b, axis=AX.X))(A[0:1, :]),
                    [rA], [r_brow], f"bb.qmx.{st_}.{h}")
                self.dve((lambda a, b, c2: lambda v: v.tensor_mul(a, b, c2))(
                    brow[:, c0 + 1:c0 + 2], brow[:, c0:c0 + 1], brow[:, 0:1]),
                    [r_brow], [r_brow], f"bb.negm0.{st_}.{h}")
                self.act((lambda a, b: lambda sc: sc.activation(a, b, AF.Ln))(
                    brow[:, c0 + 2:c0 + 3], brow[:, c0 + 1:c0 + 2]),
                    [r_brow], [r_brow], f"bb.negmln.{st_}.{h}")
                self.act((lambda a, b: lambda sc: sc.activation(
                    a, b, AF.Exp, scale=0.5))(brow[:, c0 + 2:c0 + 3],
                                              brow[:, c0 + 2:c0 + 3]),
                    [r_brow], [r_brow], f"bb.negmex.{st_}.{h}")
                self.dve((lambda a, b: lambda v: v.tensor_scalar_mul(
                    a, b, -1.05 * SCALE_B))(brow[:, c0 + 3:c0 + 4],
                                            brow[:, c0 + 2:c0 + 3]),
                    [r_brow], [r_brow], f"bb.negm2.{st_}.{h}")
                self.mmg(B[:, 0:1], [(s["ones_rf"], brow[0:1, c0 + 3:c0 + 4])],
                         [r_brow, self.R("ones")], rB, f"bb.nmb.{st_}.{h}")
                self.dve((lambda a, b: lambda v: v.tensor_copy(a, b))(
                    s["scol"][:, 0 + 3 * par:1 + 3 * par], B[:, 0:1]), [rB],
                    [r_scol[par]], f"bb.nmc.{st_}.{h}")
                self.dve((lambda a, b, c2: lambda v: v.tensor_scalar_add(
                    a, b, c2))(bcols[:, par * 16: par * 16 + 16],
                               s["gpen"][:, st_ * 16: st_ * 16 + 16],
                               s["scol"][:, 0 + 3 * par:1 + 3 * par]),
                    [self.R("gpen"), r_scol[par]], [r_bcols[par]],
                    f"bb.bcols.{st_}.{h}")

            def kcloop(h):
                par = h % 2
                g = h // 4
                A, B = ps[2 + 2 * par], ps[3 + 2 * par]
                rA, rB = self.psr[2 + 2 * par], self.psr[3 + 2 * par]
                qro_t, r_qro = qro[par]
                pend = []
                state = {"n": 0, "nl": 0}

                def flush():
                    for f in pend:
                        f()
                    pend.clear()

                def spair_main(kp, npair):
                    k0 = 2 * kp
                    def f(te):
                        r = te.matmul(ps[0], Kf[:, g * 2048 + k0 * 128:
                                              g * 2048 + (k0 + 1) * 128],
                                      qro_t, start=True, stop=True,
                                      skip_group_check=True)
                        if npair == 2:
                            r = te.matmul(ps[1], Kf[:, g * 2048 + (k0 + 1) * 128:
                                                    g * 2048 + (k0 + 2) * 128],
                                          qro_t, start=True, stop=True,
                                          skip_group_check=True)
                        return r
                    self.P.op("pe", f, reads=r_Kf + r_qro,
                              writes=[self.psr[0], self.psr[1]],
                              tag=f"bb.smm.{st_}.{h}.{kp}")

                for kp in range((n_main + 1) // 2):
                    kc0 = 2 * kp
                    npair = 2 if kc0 + 1 < n_main else 1
                    wid = 512 * npair
                    spair_main(kp, npair)
                    Ea, r_Ea = Ep[kp % 2]
                    self.act((lambda a, b, bc, w=wid: lambda sc: sc.activation(
                        a[:, 0:w], b[:, 0:w], AF.Exp, bias=bc, scale=SCALE_B))(
                        Ea, self.ps_pair,
                        bcols[:, par * 16 + kc0: par * 16 + kc0 + 1]),
                        [self.psr[0], self.psr[1], r_bcols[par]], r_Ea,
                        f"bb.exp.{st_}.{h}.{kp}")
                    flush()
                    def zpv(kc0=kc0, Ea=Ea, r_Ea=r_Ea, npair=npair):
                        for kk in range(npair):
                            kc = kc0 + kk
                            Esl = Ea[:, kk * 512:(kk + 1) * 512]
                            self.mmg(ps[6][0:1, :], [(s["ones_cb"], Esl)],
                                     r_Ea + [self.R("ones")], self.psr[6],
                                     f"bb.zg.{st_}.{h}.{kc}",
                                     accumulate=(state["n"] > 0))
                            self.mmg(A, [(Vf[:, kc * 512 + g * 128:
                                             kc * 512 + (g + 1) * 128], Esl)],
                                     r_Vf + r_Ea, rA,
                                     f"bb.pv.{st_}.{h}.{kc}",
                                     accumulate=(state["n"] > 0))
                            state["n"] += 1
                    pend.append(zpv)
                loc_pairs = [(0, 1), (2, 3), (4,)]
                for pi_, js in enumerate(loc_pairs):
                    j0 = js[0]
                    wid = 512 * len(js)
                    def sl_fn(j0=j0, njs=len(js), st=st_):
                        def f(te):
                            r = None
                            for u in range(njs):
                                jj = 4 * st + j0 + u
                                r = te.matmul(ps[u], Kf[:, g * 2048 + (7 + jj) * 128:
                                                        g * 2048 + (8 + jj) * 128],
                                              qro_t, start=True, stop=True,
                                              skip_group_check=True)
                            return r
                        return f
                    self.P.op("pe", sl_fn(), reads=r_Kf + r_qro,
                              writes=[self.psr[0], self.psr[1]],
                              tag=f"bb.lsmm.{st_}.{h}.{pi_}")
                    Ea, r_Ea = Ep[pi_ % 2]
                    self.act((lambda a, b, bc, w=wid: lambda sc: sc.activation(
                        a[:, 0:w], b[:, 0:w], AF.Exp, bias=bc, scale=SCALE_B))(
                        Ea, self.ps_pair, s["scol"][:, 0 + 3 * par:1 + 3 * par]),
                        [self.psr[0], self.psr[1], r_scol[par]], r_Ea,
                        f"bb.lexp.{st_}.{h}.{pi_}")
                    flush()
                    self.dve((lambda a, b, c2, w=wid: lambda v: v.tensor_mul(
                        a[:, 0:w], b[:, 0:w], c2))(
                        Ecp, Ea, ctri_s[:, j0 * 512: j0 * 512 + wid]),
                        r_Ea + r_ctri, r_Ecp, f"bb.ec.{st_}.{h}.{pi_}")
                    self.dve((lambda a, c2, w=wid: lambda v: v.tensor_mul(
                        a[:, 0:w], a[:, 0:w], c2))(
                        Ea, lmask_s[:, j0 * 512: j0 * 512 + wid]),
                        r_Ea + r_lm, r_Ea, f"bb.el.{st_}.{h}.{pi_}")
                    def zpvl(js=js, Ea=Ea, r_Ea=r_Ea):
                        for u, j in enumerate(js):
                            jj = 4 * st_ + j
                            Ecs = Ecp[:, u * 512:(u + 1) * 512]
                            Els = Ea[:, u * 512:(u + 1) * 512]
                            vsl = Vf[:, (7 + jj) * 512 + g * 128:
                                     (7 + jj) * 512 + (g + 1) * 128]
                            self.mmg(ps[6][0:1, :], [(s["ones_cb"], Ecs)],
                                     r_Ecp + [self.R("ones")], self.psr[6],
                                     f"bb.lzg.{st_}.{h}.{j}",
                                     accumulate=(state["n"] > 0))
                            state["n"] += 1
                            self.mmg(ps[7][0:1, :], [(s["ones_cb"], Els)],
                                     r_Ea + [self.R("ones")], self.psr[7],
                                     f"bb.lzl.{st_}.{h}.{j}",
                                     accumulate=(state["nl"] > 0))
                            self.mmg(A, [(vsl, Ecs)], r_Vf + r_Ecp, rA,
                                     f"bb.lpv.{st_}.{h}.{j}", accumulate=True)
                            self.mmg(B, [(vsl, Els)], r_Vf + r_Ea, rB,
                                     f"bb.lpvl.{st_}.{h}.{j}",
                                     accumulate=(state["nl"] > 0))
                            state["nl"] += 1
                    pend.append(zpvl)
                flush()
                par4 = par * 512
                self.dve((lambda a, b: lambda v: v.tensor_copy(a, b))(
                    rows[:, par4:par4 + 512], ps[6][0:1, :]),
                    [self.psr[6]], [r_rows], f"bb.zge.{st_}.{h}")
                self.dve((lambda a, b: lambda v: v.tensor_copy(a, b))(
                    rows[:, 1024 + par4:1024 + par4 + 512], ps[7][0:1, :]),
                    [self.psr[7]], [r_rows], f"bb.zle.{st_}.{h}")

            def tail(h):
                par = h % 2
                A, B = ps[2 + 2 * par], ps[3 + 2 * par]
                rA, rB = self.psr[2 + 2 * par], self.psr[3 + 2 * par]
                par4 = par * 512
                zg = rows[:, par4:par4 + 512]
                zl = rows[:, 1024 + par4:1024 + par4 + 512]
                self.dve((lambda a, b: lambda v: v.tensor_copy(a, b))(
                    t1_t, A), [rA], r_t1, f"bb.ouc.{st_}.{h}")
                self.act((lambda a, b: lambda sc: sc.activation(
                    a, b, AF.Ln))(zg, zg), [r_rows], [r_rows],
                    f"bb.cgl.{st_}.{h}")
                self.act((lambda a, b, bb_=s["cbias"][0:1, 2:3]: lambda sc:
                          sc.activation(a, b, AF.Exp, scale=-1.0, bias=bb_))(
                    zg, zg), [r_rows, self.R("cbias")], [r_rows],
                    f"bb.cge.{st_}.{h}")
                self.act((lambda a, b: lambda sc: sc.activation(
                    a, b, AF.Ln))(zl, zl), [r_rows], [r_rows],
                    f"bb.cll.{st_}.{h}")
                self.act((lambda a, b, bb_=s["cbias"][0:1, 2:3]: lambda sc:
                          sc.activation(a, b, AF.Exp, scale=-1.0, bias=bb_))(
                    zl, zl), [r_rows, self.R("cbias")], [r_rows],
                    f"bb.cle.{st_}.{h}")
                self.mmg(A, [(s["ones_rf"], zg[0:1, :])],
                         [r_rows, self.R("ones")] + r_t1, rA,
                         f"bb.cgb.{st_}.{h}")
                self.dve((lambda a, b, c2: lambda v: v.tensor_mul(a, b, c2))(
                    qh_t, t1_t, A), r_t1 + [rA], r_qh,
                    f"bb.on1.{st_}.{h}")
                self.dve((lambda a, b: lambda v: v.tensor_copy(a, b))(
                    t1_t, B), [rB], r_t1, f"bb.olc.{st_}.{h}")
                self.mmg(B, [(s["ones_rf"], zl[0:1, :])],
                         [r_rows, self.R("ones")] + r_t1, rB,
                         f"bb.clb.{st_}.{h}")
                self.dve((lambda a, b, c2: lambda v: v.tensor_mul(a, a, c2))(
                    t1_t, t1_t, B), r_t1 + [rB], r_t1,
                    f"bb.on2.{st_}.{h}")
                osl = Ob[:, h * 512:(h + 1) * 512]
                self.dve((lambda a, b, c2: lambda v: v.tensor_add(a, b, c2))(
                    osl, qh_t, t1_t), r_qh + r_t1, r_Ob,
                    f"bb.ow.{st_}.{h}")

            for h in range(16):
                prep(h)
                kcloop(h)
                if h > 0:
                    tail(h - 1)
            tail(15)
            if st_ == 0:
                self.tap("bO0", Ob, [128, 8192], BF16,
                         [self.R(f"pg{i}") for i in range(16, 24)])
            for oc in range(16):
                bi = oc % 3
                src = t["wo"][:, oc * 128:(oc + 1) * 128].rearrange(
                    "(kc q) n -> q kc n", q=128)
                self.load(wst[bi].rearrange("q (kc n) -> q kc n", kc=16), src,
                          f"wst{bi}", f"bb.wo.{st_}.{oc}")
                pi = 2 + oc % 2
                pairs = [(wst[bi][:, hc * 128:(hc + 1) * 128],
                          Ob[:, hc * 512:(hc + 1) * 512]) for hc in range(16)]
                self.mmg(ps[pi], pairs, [r_w[bi]] + r_Ob, self.psr[pi],
                         f"bb.womm.{st_}.{oc}")
                xsl = x[:, oc * 1024 + S: oc * 1024 + S + 512]
                self.dve((lambda a, b: lambda v: v.tensor_add(a, a, b))(
                    xsl, ps[pi]), [self.psr[pi], rx[oc]], [rx[oc]],
                    f"bb.res.{st_}.{oc}")

    def _exchange_b(self, r_Kown, r_Vown, r_Kf, r_Vf, Kf, Vf):
        s, t = self.s, self.t
        Kown, _ = self.carve("bKown2", 0, [128, 4096], BF16)
        Vown, _ = self.carve("bVown2", 16384, [128, 4096], BF16)
        sems = self.xsems["b"]

        def blob(g):
            sd, sp_ = sems
            ko3 = Kown.rearrange("d (g t) -> d g t", g=4)
            vo3 = Vown.rearrange("p (tc v) -> p tc v", tc=8)
            kf3 = Kf.rearrange("d (g t) -> d g t", g=4)
            vf3 = Vf.rearrange("p (tc v) -> p tc v", tc=16)
            shk = [t["shb_k"][i].rearrange("(g d) t -> d g t", d=128)
                   for i in range(2)]
            shv = [t["shb_v"][i].rearrange("(tc p) v -> p tc v", p=128)
                   for i in range(2)]
            with g.register("rparB") as rpar, g.register("rflagB") as rflag, \
                    g.register("rnonB") as rnon, g.register("rcntB") as rcnt:
                g.reg_load(rpar, s["pars"][0:1, 0:1])
                g.reg_load(rnon, s["pars"][0:1, 2:3])
                for p in range(2):
                    ctx = g.If_eq(rpar, 0) if p == 0 else g.Else()
                    with ctx:
                        g.dma_start(out=shk[p], in_=ko3).then_inc(sd, 16)
                        g.dma_start(out=shv[p], in_=vo3).then_inc(sd, 16)
                        g.wait_ge(sd, 32)
                        g.dma_start(out=t["shf"][1, p:p + 1, :],
                                    in_=s["pars"]).then_inc(sd, 16)
                        g.wait_ge(sd, 48)
                g.reg_mov(rcnt, 0)
                g.reg_mov(rflag, 0)
                for p in range(2):
                    o = 1 - p
                    ctx = g.If_eq(rpar, 0) if p == 0 else g.Else()
                    with ctx:
                        with g.While(lambda: g.snap(rflag) != g.snap(rnon)):
                            g.reg_add(rcnt, rcnt, 16)
                            g.dma_start(out=s["pars"][0:1, 4:8],
                                        in_=t["shf"][1, o:o + 1, 0:4]).then_inc(sp_, 16)
                            g.wait_ge(sp_, rcnt)
                            g.reg_load(rflag, s["pars"][0:1, 6:7])
                        g.dma_start(out=kf3[:, :, 0:1024],
                                    in_=shk[o]).then_inc(sd, 16)
                        g.dma_start(out=kf3[:, :, 1024:2048],
                                    in_=shk[p]).then_inc(sd, 16)
                        g.dma_start(out=vf3[:, 0:8, :],
                                    in_=shv[o]).then_inc(sd, 16)
                        g.dma_start(out=vf3[:, 8:16, :],
                                    in_=shv[p]).then_inc(sd, 16)
                        g.wait_ge(sd, 48 + 4 * 16)
            return g.nop()

        self.P.op("sp", blob, reads=r_Kown + r_Vown + [self.R("pars")],
                  writes=r_Kf + r_Vf + [self.R("pars_poll")],
                  tag="xchgB")

    def emit_store(self):
        s, t = self.s, self.t
        rx = [self.R(f"x{c}") for c in range(16)]
        o3 = t["out"].rearrange("(c p) t -> p c t", p=128)
        so = self.P.op("sp", (lambda d, src: lambda g: g.dma_start(out=d, in_=src))(
            o3, s["x"].rearrange("p (c t) -> p c t", c=16)),
            reads=rx, dma=True, tag="store", slot="store")
        self.P.op("sp", lambda g: g.nop(),
                  deps=[so] + self.tap_stores, tag="final")

    def build(self):
        self.declare()
        with contextlib.ExitStack() as stack:
            self.xsems = {
                "a": (stack.enter_context(self.nc.semaphore("xad")),
                      stack.enter_context(self.nc.semaphore("xap"))),
                "b": (stack.enter_context(self.nc.semaphore("xbd")),
                      stack.enter_context(self.nc.semaphore("xbp"))),
            }
            self.emit_setup()
            self.emit_ln(1)
            self.emit_attn_a()
            self.tap("x1", self.s["x"], [128, 16 * 1024], F32,
                     [self.R(f"x{c}") for c in range(16)])
            if self.n_sub >= 2:
                self.emit_ln(2)
                self.emit_ffn(1)
                self.tap("x2", self.s["x"], [128, 16 * 1024], F32,
                         [self.R(f"x{c}") for c in range(16)])
            if self.n_sub >= 3:
                self.emit_ln(3)
                self.emit_attn_b()
                self.tap("x3", self.s["x"], [128, 16 * 1024], F32,
                         [self.R(f"x{c}") for c in range(16)])
                self.tap("bO", self.s["bO"], [128, 8192], BF16,
                         [self.R(f"pg{i}") for i in range(16, 24)])
            if self.n_sub >= 4:
                self.emit_ln(4)
                self.emit_ffn(2)
            self.emit_store()
            self.P.finalize(paranoid=bool(os.environ.get("KFW_PARANOID")))
            sems = {nm: stack.enter_context(self.nc.semaphore(nm.replace(".", "_")))
                    for nm in self.P.sem_names()}
            block = stack.enter_context(self.nc.Block())

            @block.tensor
            def _(te):
                self.P.emit_engine("pe", te, sems)

            @block.scalar
            def _(sc):
                self.P.emit_engine("act", sc, sems)

            @block.vector
            def _(v):
                self.P.emit_engine("dve", v, sems)

            @block.gpsimd
            def _(g):
                self.P.emit_engine("pool", g, sems)

            @block.sync
            def _(sp):
                self.P.emit_engine("sp", sp, sems)
        return self.nc


def build_nc(n_sub=None, taps=None, detect=True, private_shared=False):
    nc = bass.Bass("TRN2", target_bir_lowering=False, debug=False,
                   detect_race_conditions=detect)
    kb = KB(nc, n_sub=N_SUB if n_sub is None else n_sub,
            taps=TAPS if taps is None else taps, private_shared=private_shared)
    kb.build()
    return nc


_NC_CACHE = {}


def make_inputs(inputs_full, nonce):
    x_full = np.ascontiguousarray(
        np.asarray(inputs_full["L0_attn_a_x"], np.float32).transpose(0, 2, 1))
    wqkvuT = np.ascontiguousarray(np.asarray(inputs_full["L0_attn_a_W_qkvu"], np.float32).T)
    woutT = np.ascontiguousarray(np.asarray(inputs_full["L0_attn_a_W_out"], np.float32).T)
    w0T = np.ascontiguousarray(np.asarray(inputs_full["L0_ffn_W0"], np.float32).T)
    w1T = np.ascontiguousarray(np.asarray(inputs_full["L0_ffn_W1"], np.float32).T)
    b0 = np.asarray(inputs_full["L0_ffn_bias0"], np.float32)
    b1 = np.asarray(inputs_full["L0_ffn_bias1"], np.float32)
    shared = {
        "wqkvuT": wqkvuT, "woutT": woutT,
        "wq": np.asarray(inputs_full["L0_attn_b_wq"], np.float32),
        "wk": np.asarray(inputs_full["L0_attn_b_wk"], np.float32),
        "wv": np.asarray(inputs_full["L0_attn_b_wv"], np.float32),
        "wo": np.asarray(inputs_full["L0_attn_b_wo"], np.float32),
        "w0T": w0T, "w1T": w1T,
        "b0c": np.ascontiguousarray(b0.reshape(64, 128).T),
        "b1c": np.ascontiguousarray(b1.reshape(16, 128).T),
        "slope_row": np.asarray(inputs_full["L0_attn_a_slope_rate"],
                                np.float32).reshape(1, 16),
        "cbias": np.array([[1e-5, 1e-6, LN05, 0.0]], np.float32),
    }
    iota_row, kd_col, bd_cols, idxt, rt = _host_consts()
    cos_f, sin_f = _rope_tables()
    consts = {"iota_row": iota_row, "kd_col": kd_col, "bd_cols": bd_cols,
              "idxt": idxt, "rt": rt}
    in_maps = []
    for c in range(8):
        b, par = c // 2, c % 2
        t0 = par * 1024
        gpen, ctri, lmask = _attnb_masks(par)
        m = dict(shared)
        m.update(consts)
        m["x"] = np.ascontiguousarray(x_full[b][:, t0:t0 + 1024])
        m["cosf"] = np.ascontiguousarray(cos_f[:, t0:t0 + 1024])
        m["sinf"] = np.ascontiguousarray(sin_f[:, t0:t0 + 1024])
        m["isodd"] = np.full((128, 1), float(par), np.float32)
        m["pars"] = np.array([[par, nonce + 1, nonce + 2, 0, 0, 0, 0, 0]],
                             np.int32)
        m["gpen"] = gpen
        m["ctri"] = ctri
        m["lmask"] = lmask
        in_maps.append(m)
    return in_maps


def kernel(**inputs):
    import time
    key = (N_SUB, tuple(sorted(TAPS)))
    if key not in _NC_CACHE:
        _NC_CACHE[key] = build_nc()
    nc = _NC_CACHE[key]
    nonce = (int(time.time() * 1000) & 0x3FFFFFF) * 16 + 3
    in_maps = make_inputs(inputs, nonce)
    res = run_bass_kernel_spmd(nc, in_maps, core_ids=list(range(8)))
    out = np.zeros((4, 2048, 2048), np.float32)
    for c in range(8):
        b, par = c // 2, c % 2
        shard = np.asarray(res.results[c]["out"])
        out[b, par * 1024:(par + 1) * 1024, :] = shard.T
    return out
